# version 13
# baseline (speedup 1.0000x reference)
import contextlib
import numpy as np
import ml_dtypes
import concourse.bass as bass
import concourse.mybir as mybir
from concourse.bass_utils import run_bass_kernel_spmd

F32 = mybir.dt.float32
BF16 = mybir.dt.bfloat16
I32 = mybir.dt.int32
AF = mybir.ActivationFunctionType
ALU = mybir.AluOpType
AX = mybir.AxisListType
BF = ml_dtypes.bfloat16

T = 2048
D = 1024
DI = 2048
NS = 16
NCORE = 8
CAP = 2560
E0 = 8
E1 = 6
DE1 = 512
NH = 8
HD = 128
TWOPI = 6.283185307179586
ENGS = ("sync", "tensor", "vector", "scalar", "gpsimd")


class B:

    def __init__(self, nc):
        self.nc = nc
        self.ops = {e: [] for e in ENGS}
        self.cnt = {e: 0 for e in ENGS}
        self.wm = {e: {o: 0 for o in ENGS} for e in ENGS}
        self.regions = {}
        self.pending_reads = {e: [] for e in ENGS}

    def _need(self, eng, waits, dep):
        deng, dcnt = dep
        if dcnt <= self.wm[eng][deng]:
            return
        waits.append((deng, dcnt))
        self.wm[eng][deng] = dcnt

    def op(self, eng, emit, reads=(), writes=(), inc=1):
        waits = []
        for r in reads:
            st = self.regions.get(r)
            if st and st["w"]:
                self._need(eng, waits, st["w"])
        for w in writes:
            st = self.regions.get(w)
            if st:
                if st["w"]:
                    self._need(eng, waits, st["w"])
                for reng, rcnt in st["r"].items():
                    self._need(eng, waits, (reng, rcnt))
        self.cnt[eng] += inc
        after = self.cnt[eng]
        self.ops[eng].append((waits, emit, inc))
        if inc:
            for r in list(self.pending_reads[eng]) + list(reads):
                st = self.regions.setdefault(r, {"w": None, "r": {}})
                st["r"][eng] = max(st["r"].get(eng, 0), after)
            self.pending_reads[eng] = []
            for w in writes:
                self.regions[w] = {"w": (eng, after), "r": {}}
        else:
            self.pending_reads[eng].extend(reads)
            assert not writes, "quiet ops must not declare writes"
        return after

    def barrier(self):
        snap = dict(self.cnt)
        for eng in ENGS:
            waits = []
            for o in ENGS:
                if o != eng and snap[o] > self.wm[eng][o]:
                    waits.append((o, snap[o]))
                    self.wm[eng][o] = snap[o]
            self.cnt[eng] += 1
            self.ops[eng].append((waits, lambda E: E.nop(), 1))

    def dma(self, eng, out, in_, reads=(), writes=(), **kw):
        return self.op(eng, lambda E: E.dma_start(out=out, in_=in_, **kw),
                       reads=reads, writes=writes, inc=16)

    def finish(self):
        nc = self.nc
        with contextlib.ExitStack() as st:
            sems = {e: st.enter_context(nc.semaphore(f"s_{e}")) for e in ENGS}
            st.enter_context(nc.Block()).__class__
        raise RuntimeError("use finish_into instead")

    def finish_into(self, stack):
        nc = self.nc
        sems = {e: stack.enter_context(nc.semaphore(f"s_{e}")) for e in ENGS}
        blk = stack.enter_context(nc.Block())
        final = {e: self.cnt[e] for e in ENGS}
        for eng in ENGS:
            ops = self.ops[eng]

            def section(E, eng=eng, ops=ops):
                for waits, emit, inc in ops:
                    for deng, dcnt in waits:
                        E.wait_ge(sems[deng], dcnt)
                    ins = emit(E)
                    if inc:
                        ins.then_inc(sems[eng], inc)
                if eng == "sync":
                    for o in ENGS:
                        if final[o]:
                            E.wait_ge(sems[o], final[o])

            getattr(blk, eng)(section)


def pack_rows(w):
    G = w.shape[0] // 128
    return np.ascontiguousarray(
        w.reshape(G, 128, w.shape[1]).transpose(1, 0, 2).reshape(128, -1))


def host_prep(inputs):
    f32c = lambda a: np.ascontiguousarray(a.astype(np.float32))
    c = {}
    c["eye"] = np.eye(128, dtype=np.float32)
    c["triu"] = np.triu(np.ones((128, 128), np.float32), 1)
    c["ones"] = np.ones((128, 128), np.float32)
    c["w_router_p"] = pack_rows(f32c(inputs["L0_ffn_w_router"]))
    c["wipT"] = f32c(inputs["L0_attn_in_proj_w"].T)
    c["wxpP"] = pack_rows(f32c(inputs["L0_attn_x_proj_w"].T))
    c["wdtT"] = f32c(inputs["L0_attn_dt_proj_w"].T)
    c["wopT"] = f32c(inputs["L0_attn_out_proj_w"].T)
    c["wgluT"] = f32c(inputs["L0_attn_glu_w"].T)
    c["convP"] = pack_rows(f32c(inputs["L0_attn_conv_w"][:, 0, :]))
    c["convbP"] = pack_rows(f32c(inputs["L0_attn_conv_b"][:, None]))
    c["dtbP"] = pack_rows(f32c(inputs["L0_attn_dt_proj_b"][:, None]))
    c["alogP"] = pack_rows(f32c(inputs["L0_attn_A_log"]))
    c["DP"] = pack_rows(f32c(inputs["L0_attn_D"][:, None]))
    c["glubP"] = pack_rows(f32c(inputs["L0_attn_glu_b"][:, None]))
    c["lngP"] = pack_rows(f32c(inputs["L0_attn_ln_g"][:, None]))
    c["lnbP"] = pack_rows(f32c(inputs["L0_attn_ln_b"][:, None]))
    sel16 = np.zeros((16, 16 * 128), np.float32)
    for n in range(16):
        sel16[n, n * 128:(n + 1) * 128] = 1.0
    c["sel16"] = sel16
    c["w1c"] = f32c(inputs["L0_ffn_w1"])
    c["w2c"] = f32c(inputs["L0_ffn_w2"])
    c["b1P"] = np.ascontiguousarray(
        inputs["L0_ffn_b1"].astype(np.float32).reshape(E0, 32, 128).transpose(0, 2, 1))
    c["b2c"] = f32c(inputs["L0_ffn_b2"])
    sel8 = np.zeros((E0, E0 * 128), np.float32)
    for n in range(E0):
        sel8[n, n * 128:(n + 1) * 128] = 1.0
    c["sel8"] = sel8
    c["qk_w"] = f32c(inputs["L1_attn_qk_w"])
    c["qk_b"] = f32c(inputs["L1_attn_qk_b"])
    c["lepe_w"] = f32c(inputs["L1_attn_lepe_w"])
    c["lepebP"] = pack_rows(f32c(inputs["L1_attn_lepe_b"][:, None]))
    c["projP"] = pack_rows(f32c(inputs["L1_ffn_proj_w"].T))
    c["anchT"] = f32c(inputs["L1_ffn_anchors"].T)
    c["rtemp"] = np.asarray(inputs["L1_ffn_routing_temp"], np.float32).reshape(1, 1)
    c["w1f"] = f32c(inputs["L1_ffn_w1"])
    c["w2f"] = f32c(inputs["L1_ffn_w2"])
    c["xTf"] = np.ascontiguousarray(
        inputs["L0_attn_x"].astype(np.float32).transpose(0, 2, 1))
    return c





def emit_A(b, nc, P):
    NT = T // 128
    NT = T // 128
    with contextlib.ExitStack() as st:
        e = st.enter_context
        xt = [e(nc.sbuf_tensor(f"A_xt{i}", [128, D], F32)) for i in range(2)]
        eye = e(nc.sbuf_tensor("A_sb_eye", [128, 128], F32))
        triu = e(nc.sbuf_tensor("A_sb_triu", [128, 128], F32))
        ones = e(nc.sbuf_tensor("A_sb_ones", [128, 128], F32))
        wr = e(nc.sbuf_tensor("A_sb_wr", [128, E0 * 8], F32))
        br_rep = e(nc.sbuf_tensor("A_br_rep", [128, E0], F32))
        stat = e(nc.sbuf_tensor("A_stat", [128, 8], F32))
        xsq = e(nc.sbuf_tensor("A_xsq", [128, D], F32))
        xnf = e(nc.sbuf_tensor("A_xnf", [128, D], F32))
        xnTf = e(nc.sbuf_tensor("A_xnTf", [128, D], F32))
        xnTb = e(nc.sbuf_tensor("A_xnTb", [128, D], BF16))
        gat = e(nc.sbuf_tensor("A_gat", [128, E0], F32))
        oh1 = e(nc.sbuf_tensor("A_oh1", [128, E0], F32))
        oh2 = e(nc.sbuf_tensor("A_oh2", [128, E0], F32))
        oht = e(nc.sbuf_tensor("A_oht", [128, E0], F32))
        posx = e(nc.sbuf_tensor("A_posx", [128, E0], F32))
        scr = e(nc.sbuf_tensor("A_scr", [128, E0], F32))
        scr2 = e(nc.sbuf_tensor("A_scr2", [128, E0], F32))
        rpack = e(nc.sbuf_tensor("A_rpack", [128, 20], F32))
        carry = e(nc.sbuf_tensor("A_carry", [1, E0], F32))
        onecol = e(nc.sbuf_tensor("A_onecol", [128, E0], F32))
        cvals = e(nc.sbuf_tensor("A_cvals", [128, 2], F32))
        ps_t = e(nc.psum_tensor("A_ps_t", [128, 512], F32))
        ps_l = e(nc.psum_tensor("A_ps_l", [128, E0], F32))
        ps_p = e(nc.psum_tensor("A_ps_p", [128, E0], F32))
        ps_c = e(nc.psum_tensor("A_ps_c", [1, E0], F32))
        ps_b = e(nc.psum_tensor("A_ps_b", [128, E0], F32))

        b.dma("sync", eye[:], P["eye_in"], writes=("eye",))
        b.dma("sync", triu[:], P["triu_in"], writes=("triu",))
        b.dma("sync", ones[:], P["ones_in"], writes=("ones",))
        b.dma("sync", wr[:], P["wr_in"], writes=("wr",))
        b.dma("sync", br_rep[0:1, :], P["br_in"].rearrange("(o e) -> o e", o=1), writes=("br",))
        b.op("tensor", lambda E: E.matmul(ps_b[:], ones[0:1, :], br_rep[0:1, :],
                                          start=True, stop=True),
             reads=("ones", "br"), writes=("ps_b",))
        b.op("vector", lambda E: E.tensor_copy(br_rep[:], ps_b[:]),
             reads=("ps_b",), writes=("br_rep",))
        b.op("vector", lambda E: E.memset(carry[:], 0.0), writes=("carry",))
        b.op("vector", lambda E: E.memset(onecol[:], 1.0), writes=("onecol",))
        b.op("vector", lambda E: E.memset(cvals[:, 0:1], 1e-5), writes=("cvals",))

        for i in range(NT):
            cur = xt[i % 2]
            xk = f"xt{i % 2}"
            b.dma("sync", cur[:], P["x_in"][i * 128:(i + 1) * 128, :], writes=(xk,))
            b.op("vector", lambda E, cur=cur: E.tensor_reduce(
                stat[:, 0:1], cur[:], axis=AX.X, op=ALU.add),
                reads=(xk,), writes=("stat",))
            b.op("scalar", lambda E, cur=cur: E.activation(
                xsq[:], cur[:], AF.Square), reads=(xk,), writes=("xsq",))
            b.op("vector", lambda E: E.tensor_reduce(
                stat[:, 1:2], xsq[:], axis=AX.X, op=ALU.add),
                reads=("xsq",), writes=("stat",))
            b.op("vector", lambda E: E.tensor_scalar(
                stat[:, 2:3], stat[:, 0:1], 1.0 / D, None, ALU.mult),
                reads=("stat",), writes=("stat",))
            b.op("vector", lambda E: E.tensor_scalar(
                stat[:, 3:4], stat[:, 1:2], 1.0 / D, None, ALU.mult),
                reads=("stat",), writes=("stat",))
            b.op("vector", lambda E: E.tensor_mul(
                stat[:, 4:5], stat[:, 2:3], stat[:, 2:3]),
                reads=("stat",), writes=("stat",))
            b.op("vector", lambda E: E.tensor_sub(
                stat[:, 4:5], stat[:, 3:4], stat[:, 4:5]),
                reads=("stat",), writes=("stat",))
            b.op("scalar", lambda E: E.activation(
                stat[:, 5:6], stat[:, 4:5], AF.Ln, bias=cvals[:, 0:1]),
                reads=("stat", "cvals"), writes=("stat",))
            b.op("scalar", lambda E: E.activation(
                stat[:, 5:6], stat[:, 5:6], AF.Exp, scale=-0.5),
                reads=("stat",), writes=("stat",))
            b.op("vector", lambda E: E.tensor_mul(
                stat[:, 6:7], stat[:, 2:3], stat[:, 5:6]),
                reads=("stat",), writes=("stat",))
            b.op("vector", lambda E: E.tensor_scalar(
                stat[:, 6:7], stat[:, 6:7], -1.0, None, ALU.mult),
                reads=("stat",), writes=("stat",))
            b.op("vector", lambda E, cur=cur: E.tensor_scalar(
                xnf[:], cur[:], stat[:, 5:6], stat[:, 6:7], ALU.mult, ALU.add),
                reads=(xk, "stat"), writes=("xnf",))
            for jj in (0, 4):
                for j in range(jj, jj + 4):
                    b.op("tensor", lambda E, j=j, jj=jj: E.transpose(
                        ps_t[:, (j - jj) * 128:(j - jj) * 128 + 128],
                        xnf[:, j * 128:(j + 1) * 128], eye[:]),
                        reads=("xnf", "eye"), writes=("ps_t",))
                for j in range(jj, jj + 4):
                    b.op("scalar", lambda E, j=j, jj=jj: E.activation(
                        xnTf[:, j * 128:(j + 1) * 128],
                        ps_t[:, (j - jj) * 128:(j - jj) * 128 + 128], AF.Copy),
                        reads=("ps_t",), writes=("xnTf",))
            b.op("vector", lambda E: E.tensor_copy(xnTb[:], xnTf[:]),
                 reads=("xnTf",), writes=("xnTb",))
            b.dma("sync", P["xnTb_hbm"][:, :, i * 128:(i + 1) * 128],
                  xnTb[:].rearrange("p (g t) -> p g t", g=8), reads=("xnTb",))
            for k in range(8):
                b.op("tensor", lambda E, k=k: E.matmul(
                    ps_l[:], xnTf[:, k * 128:(k + 1) * 128],
                    wr[:, k * 8:(k + 1) * 8],
                    start=(k == 0), stop=(k == 7)),
                    reads=("xnTf", "wr"),
                    writes=("ps_l",) if k in (0, 7) else (),
                    inc=1 if k in (0, 7) else 0)
            b.op("vector", lambda E: E.tensor_add(gat[:], ps_l[:], br_rep[:]),
                 reads=("ps_l", "br_rep"), writes=("gat",))

            def top1(dst, dstk, src, srck, gcol):
                b.op("vector", lambda E: E.reduce_max(
                    stat[:, gcol:gcol + 1], src[:], axis=AX.X),
                    reads=(srck,), writes=("stat",))
                b.op("vector", lambda E: E.tensor_scalar(
                    dst[:], src[:], stat[:, gcol:gcol + 1], None, ALU.is_equal),
                    reads=(srck, "stat"), writes=(dstk,))
                b.op("vector", lambda E: E.tensor_tensor_scan(
                    scr[:], onecol[:], dst[:], 0.0, ALU.mult, ALU.add),
                    reads=(dstk, "onecol"), writes=("scr",))
                b.op("vector", lambda E: E.tensor_scalar(
                    scr[:], scr[:], 1.0, None, ALU.is_equal),
                    reads=("scr",), writes=("scr",))
                b.op("vector", lambda E: E.tensor_mul(dst[:], dst[:], scr[:]),
                     reads=(dstk, "scr"), writes=(dstk,))

            top1(oh1, "oh1", gat, "gat", 0)
            b.op("vector", lambda E: E.scalar_tensor_tensor(
                scr2[:], oh1[:], -1e30, gat[:], ALU.mult, ALU.add),
                reads=("oh1", "gat"), writes=("scr2",))
            top1(oh2, "oh2", scr2, "scr2", 1)
            b.op("vector", lambda E: E.tensor_add(oht[:], oh1[:], oh2[:]),
                 reads=("oh1", "oh2"), writes=("oht",))
            b.op("vector", lambda E: E.reduce_max(stat[:, 7:8], gat[:], axis=AX.X),
                 reads=("gat",), writes=("stat",))
            b.op("vector", lambda E: E.tensor_scalar(
                stat[:, 7:8], stat[:, 7:8], -1.0, None, ALU.mult),
                reads=("stat",), writes=("stat",))
            b.op("scalar", lambda E: E.activation(
                gat[:], gat[:], AF.Exp, bias=stat[:, 7:8]),
                reads=("gat", "stat"), writes=("gat",))
            b.op("vector", lambda E: E.tensor_reduce(
                stat[:, 7:8], gat[:], axis=AX.X, op=ALU.add),
                reads=("gat",), writes=("stat",))
            b.op("vector", lambda E: E.reciprocal(stat[:, 7:8], stat[:, 7:8]),
                 reads=("stat",), writes=("stat",))
            b.op("vector", lambda E: E.tensor_scalar(
                gat[:], gat[:], stat[:, 7:8], None, ALU.mult),
                reads=("gat", "stat"), writes=("gat",))
            b.op("vector", lambda E: E.tensor_mul(scr[:], gat[:], oh1[:]),
                 reads=("gat", "oh1"), writes=("scr",))
            b.op("vector", lambda E: E.tensor_reduce(
                stat[:, 0:1], scr[:], axis=AX.X, op=ALU.add),
                reads=("scr",), writes=("stat",))
            b.op("vector", lambda E: E.tensor_mul(scr[:], gat[:], oh2[:]),
                 reads=("gat", "oh2"), writes=("scr",))
            b.op("vector", lambda E: E.tensor_reduce(
                stat[:, 1:2], scr[:], axis=AX.X, op=ALU.add),
                reads=("scr",), writes=("stat",))
            b.op("tensor", lambda E: E.matmul(ps_p[:], triu[:], oht[:],
                                              start=True, stop=False),
                 reads=("triu", "oht"), writes=("ps_p",))
            b.op("tensor", lambda E: E.matmul(ps_p[:], ones[0:1, :], carry[:],
                                              start=False, stop=True),
                 reads=("ones", "carry"), writes=("ps_p",))
            b.op("vector", lambda E: E.tensor_copy(posx[:], ps_p[:]),
                 reads=("ps_p",), writes=("posx",))
            b.op("tensor", lambda E: E.matmul(ps_c[:], ones[:, 0:1], oht[:],
                                              start=True, stop=True),
                 reads=("ones", "oht"), writes=("ps_c",))
            b.op("vector", lambda E: E.tensor_add(carry[:], carry[:], ps_c[:]),
                 reads=("ps_c", "carry"), writes=("carry",))
            b.op("vector", lambda E: E.tensor_mul(scr[:], posx[:], oh1[:]),
                 reads=("posx", "oh1"), writes=("scr",))
            b.op("vector", lambda E: E.tensor_reduce(
                stat[:, 2:3], scr[:], axis=AX.X, op=ALU.add),
                reads=("scr",), writes=("stat",))
            b.op("vector", lambda E: E.tensor_add(scr[:], posx[:], oh1[:]),
                 reads=("posx", "oh1"), writes=("scr",))
            b.op("vector", lambda E: E.tensor_mul(scr[:], scr[:], oh2[:]),
                 reads=("scr", "oh2"), writes=("scr",))
            b.op("vector", lambda E: E.tensor_reduce(
                stat[:, 3:4], scr[:], axis=AX.X, op=ALU.add),
                reads=("scr",), writes=("stat",))
            b.op("vector", lambda E: E.tensor_copy(rpack[:, 0:8], oh1[:]),
                 reads=("oh1",), writes=("rpack",))
            b.op("vector", lambda E: E.tensor_copy(rpack[:, 8:16], oh2[:]),
                 reads=("oh2",), writes=("rpack",))
            b.op("vector", lambda E: E.tensor_copy(rpack[:, 16:20], stat[:, 0:4]),
                 reads=("stat",), writes=("rpack",))
            b.op("vector", lambda E, i=i: E.tensor_copy(
                P["route_sb"][:, i * 20:(i + 1) * 20], rpack[:]),
                reads=("rpack",), writes=("route_sb",))



def emit_B_mk(b, nc, P, st):
    CH = 512
    NCH = T // CH
    if True:
        e = st.enter_context
        xn2 = e(nc.sbuf_tensor("B_xn2", [128, 8 * CH], BF16))
        wp = [e(nc.sbuf_tensor(f"B_wp{i}", [128, 2048], BF16)) for i in range(8)]
        wxp = e(nc.sbuf_tensor("B_wxp", [128, 16 * 96], BF16))
        wdt = e(nc.sbuf_tensor("B_wdt", [64, DI], BF16))
        cw = e(nc.sbuf_tensor("B_cw", [128, 64], F32))
        cb = e(nc.sbuf_tensor("B_cb", [128, 16], F32))
        dtbc = e(nc.sbuf_tensor("B_dtbc", [128, 16], F32))
        Aneg = e(nc.sbuf_tensor("B_Aneg", [128, 256], F32))
        Dv = e(nc.sbuf_tensor("B_Dv", [128, 16], F32))
        glub = e(nc.sbuf_tensor("B_glub", [128, 16], F32))
        lng = e(nc.sbuf_tensor("B_lng", [128, 8], F32))
        lnb = e(nc.sbuf_tensor("B_lnb", [128, 8], F32))
        sel = e(nc.sbuf_tensor("B_sel", [16, 16 * 128], F32))
        tails = e(nc.sbuf_tensor("B_tails", [128, 48], BF16))
        xpct = e(nc.sbuf_tensor("B_xpct", [128, 515], BF16))
        convacc = e(nc.sbuf_tensor("B_convacc", [128, CH], F32))
        xc = e(nc.sbuf_tensor("B_xc", [128, 16 * CH], BF16))
        siluz = e(nc.sbuf_tensor("B_siluz", [128, 16 * CH], BF16))
        dtraw = e(nc.sbuf_tensor("B_dtraw", [64, CH], BF16))
        Bmt = e(nc.sbuf_tensor("B_Bmt", [16, CH], F32))
        Cmt = e(nc.sbuf_tensor("B_Cmt", [16, CH], F32))
        brep = e(nc.sbuf_tensor("B_brep", [128, 16 * CH], BF16))
        crep = e(nc.sbuf_tensor("B_crep", [128, 16 * CH], BF16))
        qsp = e(nc.sbuf_tensor("B_qsp", [128, CH], F32))
        spa = e(nc.sbuf_tensor("B_spa", [128, CH], F32))
        spb = e(nc.sbuf_tensor("B_spb", [128, CH], F32))
        dtbg = [e(nc.sbuf_tensor(f"B_dtbg{i}", [128, CH], BF16)) for i in range(2)]
        dtu = [e(nc.sbuf_tensor(f"B_dtu{i}", [128, CH], BF16)) for i in range(2)]
        dA_t = [e(nc.sbuf_tensor(f"B_dA_t{i}", [128, CH], BF16)) for i in range(2)]
        dBu_t = [e(nc.sbuf_tensor(f"B_dBu_t{i}", [128, CH], BF16)) for i in range(2)]
        h_t = [e(nc.sbuf_tensor(f"B_h_t{i}", [128, CH], BF16)) for i in range(2)]
        hc_t = [e(nc.sbuf_tensor(f"B_hc_t{i}", [128, CH], BF16)) for i in range(2)]
        yacc = e(nc.sbuf_tensor("B_yacc", [128, CH], BF16))
        hcarry = e(nc.sbuf_tensor("B_hcarry", [128, 256], F32))
        y3 = e(nc.sbuf_tensor("B_y3", [128, 16 * CH], BF16))
        mT = e(nc.sbuf_tensor("B_mT", [128, 8 * CH], BF16))
        g1 = e(nc.sbuf_tensor("B_g1", [128, 8 * CH], BF16))
        sg2 = e(nc.sbuf_tensor("B_sg2", [128, 8 * CH], BF16))
        onescol = e(nc.sbuf_tensor("B_onescol", [128, 1], F32))
        psA2x = [e(nc.psum_tensor(f"B_psA{i}", [128, 512], F32)) for i in range(2)]
        psC = e(nc.psum_tensor("B_psC", [16, 512], F32))
        psD = e(nc.psum_tensor("B_psD", [16, 512], F32))
        psE = e(nc.psum_tensor("B_psE", [128, 512], F32))
        psBG = e(nc.psum_tensor("B_psBG", [128, 512], F32))

        b.dma("gpsimd", wxp[:], P["wxp_in"], writes=("wxp",))
        b.dma("gpsimd", wdt[:], P["wdt_in"], writes=("wdt",))
        for t_, a_, k_ in [(cw, P["conv_in"], "cw"), (cb, P["convb_in"], "cb"),
                           (dtbc, P["dtb_in"], "dtbc"), (Aneg, P["alog_in"], "Aneg"),
                           (Dv, P["dv_in"], "Dv"), (glub, P["glub_in"], "glub"),
                           (lng, P["lng_in"], "lng"), (lnb, P["lnb_in"], "lnb"),
                           (sel, P["sel_in"], "sel")]:
            b.dma("sync", t_[:], a_, writes=(k_,))
        b.op("scalar", lambda E: E.activation(Aneg[:], Aneg[:], AF.Exp),
             reads=("Aneg",), writes=("Aneg",))
        b.op("vector", lambda E: E.tensor_scalar(
            Aneg[:], Aneg[:], -1.0, None, ALU.mult),
            reads=("Aneg",), writes=("Aneg",))
        b.op("vector", lambda E: E.memset(tails[:], 0.0), writes=("tails",))
        b.op("vector", lambda E: E.memset(hcarry[:], 0.0), writes=tuple(f"hcarry{z}" for z in range(32)))
        b.op("vector", lambda E: E.memset(onescol[:], 1.0), writes=("onescol",))

        def do_chunk(ci):
            c0 = ci * CH
            b.dma("sync", xn2[:].rearrange("p (g t) -> p g t", g=8),
                  P["xnTb_hbm"][:, :, c0:c0 + CH], writes=("xn2",))
            for g in range(8):
                b.op("vector", lambda E, g=g: E.tensor_scalar(
                    xn2[:, g * CH:(g + 1) * CH], xn2[:, g * CH:(g + 1) * CH],
                    lng[:, g:g + 1], lnb[:, g:g + 1], ALU.mult, ALU.add),
                    reads=("xn2", "lng", "lnb"), writes=("xn2",))
            for half in range(2):
                for k in range(8):
                    b.dma("gpsimd", wp[k][:],
                          P["wip_in"][k * 128:(k + 1) * 128,
                                 half * 2048:half * 2048 + 2048],
                          writes=(f"wp{k}",))
                for ogl in range(16):
                    og = half * 16 + ogl
                    pA = psA2x[ogl % 2]
                    rA = f"psA{ogl % 2}"
                    for k in range(8):
                        b.op("tensor", lambda E, k=k, ogl=ogl, pA=pA: E.matmul(
                            pA[:], wp[k][:, ogl * 128:(ogl + 1) * 128],
                            xn2[:, k * CH:(k + 1) * CH],
                            start=(k == 0), stop=(k == 7)),
                            reads=(f"wp{k}", "xn2"),
                            writes=(rA,) if k in (0, 7) else (),
                            inc=1 if k in (0, 7) else 0)
                    if og < 16:
                        dg = og
                        b.op("scalar", lambda E, pA=pA: E.activation(
                            xpct[:, 3:515], pA[:], AF.Copy),
                            reads=(rA,), writes=("xpct",))
                        b.op("vector", lambda E, dg=dg: E.tensor_copy(
                            xpct[:, 0:3], tails[:, dg * 3:dg * 3 + 3]),
                            reads=("tails",), writes=("xpct",))
                        b.op("vector", lambda E, dg=dg: E.tensor_copy(
                            tails[:, dg * 3:dg * 3 + 3], xpct[:, 512:515]),
                            reads=("xpct",), writes=("tails",))
                        b.op("vector", lambda E, dg=dg: E.tensor_scalar(
                            convacc[:], xpct[:, 0:CH], cw[:, dg * 4:dg * 4 + 1],
                            None, ALU.mult),
                            reads=("xpct", "cw"), writes=("convacc",))
                        for kk in range(1, 4):
                            b.op("vector", lambda E, dg=dg, kk=kk: E.scalar_tensor_tensor(
                                convacc[:], xpct[:, kk:kk + CH],
                                cw[:, dg * 4 + kk:dg * 4 + kk + 1], convacc[:],
                                ALU.mult, ALU.add),
                                reads=("xpct", "cw", "convacc"), writes=("convacc",))
                        b.op("scalar", lambda E, dg=dg: E.activation(
                            xc[:, dg * CH:(dg + 1) * CH], convacc[:], AF.Silu,
                            bias=cb[:, dg:dg + 1]),
                            reads=("convacc", "cb"), writes=("xc",))
                    else:
                        zg = og - 16
                        b.op("scalar", lambda E, zg=zg, pA=pA: E.activation(
                            siluz[:, zg * CH:(zg + 1) * CH], pA[:], AF.Silu),
                            reads=(rA,), writes=("siluz",))
            for k in range(16):
                b.op("tensor", lambda E, k=k: E.matmul(
                    psBG[0:64, :], wxp[:, k * 96:k * 96 + 64],
                    xc[:, k * CH:(k + 1) * CH],
                    start=(k == 0), stop=(k == 15)),
                    reads=("wxp", "xc"),
                    writes=("psBG",) if k in (0, 15) else (),
                    inc=1 if k in (0, 15) else 0)
            for k in range(16):
                b.op("tensor", lambda E, k=k: E.matmul(
                    psC[:], wxp[:, k * 96 + 64:k * 96 + 80],
                    xc[:, k * CH:(k + 1) * CH],
                    start=(k == 0), stop=(k == 15)),
                    reads=("wxp", "xc"),
                    writes=("psC",) if k in (0, 15) else (),
                    inc=1 if k in (0, 15) else 0)
            for k in range(16):
                b.op("tensor", lambda E, k=k: E.matmul(
                    psD[:], wxp[:, k * 96 + 80:k * 96 + 96],
                    xc[:, k * CH:(k + 1) * CH],
                    start=(k == 0), stop=(k == 15)),
                    reads=("wxp", "xc"),
                    writes=("psD",) if k in (0, 15) else (),
                    inc=1 if k in (0, 15) else 0)
            b.op("scalar", lambda E: E.activation(dtraw[:], psBG[0:64, :], AF.Copy),
                 reads=("psBG",), writes=("dtraw",))
            b.op("scalar", lambda E: E.activation(Bmt[:], psC[:], AF.Copy),
                 reads=("psC",), writes=("Bmt",))
            b.op("scalar", lambda E: E.activation(Cmt[:], psD[:], AF.Copy),
                 reads=("psD",), writes=("Cmt",))
            for n in range(NS):
                b.op("tensor", lambda E, n=n: E.matmul(
                    psE[:], sel[:, n * 128:(n + 1) * 128], Bmt[:],
                    start=True, stop=True),
                    reads=("sel", "Bmt"), writes=("psE",))
                b.op("scalar", lambda E, n=n: E.activation(
                    brep[:, n * CH:(n + 1) * CH], psE[:], AF.Copy),
                    reads=("psE",), writes=("brep",))
                b.op("tensor", lambda E, n=n: E.matmul(
                    psE[:], sel[:, n * 128:(n + 1) * 128], Cmt[:],
                    start=True, stop=True),
                    reads=("sel", "Cmt"), writes=("psE",))
                b.op("scalar", lambda E, n=n: E.activation(
                    crep[:, n * CH:(n + 1) * CH], psE[:], AF.Copy),
                    reads=("psE",), writes=("crep",))
            for dg in range(16):
                b.op("tensor", lambda E, dg=dg: E.matmul(
                    psBG[:], wdt[:, dg * 128:(dg + 1) * 128], dtraw[:],
                    start=True, stop=True),
                    reads=("wdt", "dtraw"), writes=("psBG",))
                b.op("scalar", lambda E: E.activation(qsp[:], psBG[:], AF.Copy),
                     reads=("psBG",), writes=("qsp",))
                b.op("scalar", lambda E, dg=dg: E.activation(
                    spa[:], qsp[:], AF.Abs, bias=dtbc[:, dg:dg + 1]),
                    reads=("qsp", "dtbc"), writes=("spa",))
                b.op("scalar", lambda E: E.activation(
                    spa[:], spa[:], AF.Exp, scale=-1.0),
                    reads=("spa",), writes=("spa",))
                b.op("scalar", lambda E: E.activation(
                    spa[:], spa[:], AF.Ln, bias=onescol[:, 0:1]),
                    reads=("spa", "onescol"), writes=("spa",))
                b.op("vector", lambda E, dg=dg: E.tensor_scalar(
                    spb[:], qsp[:], dtbc[:, dg:dg + 1], 0.0, ALU.add, ALU.max),
                    reads=("qsp", "dtbc"), writes=("spb",))
                q2 = dg % 2
                b.op("vector", lambda E, q2=q2: E.tensor_add(dtbg[q2][:], spa[:], spb[:]),
                     reads=("spa", "spb"), writes=(f"dtbg{q2}",))
                b.op("vector", lambda E, dg=dg, q2=q2: E.tensor_mul(
                    dtu[q2][:], dtbg[q2][:], xc[:, dg * CH:(dg + 1) * CH]),
                    reads=(f"dtbg{q2}", "xc"), writes=(f"dtu{q2}",))
                b.op("vector", lambda E: E.memset(yacc[:], 0.0), writes=("yacc",))
                for n in range(NS):
                    p = n % 2
                    q2 = dg % 2
                    b.op("scalar", lambda E, dg=dg, n=n, p=p, q2=q2: E.activation(
                        dA_t[p][:], dtbg[q2][:], AF.Exp,
                        scale=Aneg[:, dg * 16 + n:dg * 16 + n + 1]),
                        reads=(f"dtbg{q2}", "Aneg"), writes=(f"dA_t{p}",))
                    b.op("gpsimd", lambda E, n=n, p=p, q2=q2: E.tensor_mul(
                        dBu_t[p][:], dtu[q2][:], brep[:, n * CH:(n + 1) * CH]),
                        reads=(f"dtu{q2}", "brep"), writes=(f"dBu_t{p}",))
                    b.op("vector", lambda E, dg=dg, n=n, p=p: E.tensor_tensor_scan(
                        h_t[p][:], dA_t[p][:], dBu_t[p][:],
                        hcarry[:, dg * 16 + n:dg * 16 + n + 1],
                        ALU.mult, ALU.add),
                        reads=(f"dA_t{p}", f"dBu_t{p}", f"hcarry{(dg * 16 + n) % 32}"), writes=(f"h_t{p}",))
                    b.op("scalar", lambda E, dg=dg, n=n, p=p: E.activation(
                        hcarry[:, dg * 16 + n:dg * 16 + n + 1],
                        h_t[p][:, CH - 1:CH], AF.Copy),
                        reads=(f"h_t{p}",), writes=(f"hcarry{(dg * 16 + n) % 32}",))
                    b.op("vector", lambda E, n=n, p=p: E.tensor_mul(
                        hc_t[p][:], h_t[p][:], crep[:, n * CH:(n + 1) * CH]),
                        reads=(f"h_t{p}", "crep"), writes=(f"hc_t{p}",))
                    b.op("vector", lambda E, p=p: E.tensor_add(
                        yacc[:], yacc[:], hc_t[p][:]),
                        reads=(f"hc_t{p}", "yacc"), writes=("yacc",))
                b.op("vector", lambda E, dg=dg: E.scalar_tensor_tensor(
                    yacc[:], xc[:, dg * CH:(dg + 1) * CH], Dv[:, dg:dg + 1],
                    yacc[:], ALU.mult, ALU.add),
                    reads=("xc", "Dv", "yacc"), writes=("yacc",))
                b.op("vector", lambda E, dg=dg: E.tensor_mul(
                    y3[:, dg * CH:(dg + 1) * CH], yacc[:],
                    siluz[:, dg * CH:(dg + 1) * CH]),
                    reads=("yacc", "siluz"), writes=("y3",))
            for j in range(8):
                for half in range(2):
                    k = j * 2 + half
                    b.dma("gpsimd", wp[j][:, half * 1024:half * 1024 + 1024],
                          P["wop_in"][k * 128:(k + 1) * 128, :], writes=(f"wp{j}",))
            for og in range(8):
                pA = psA2x[og % 2]
                rA = f"psA{og % 2}"
                for k in range(16):
                    b.op("tensor", lambda E, og=og, k=k, pA=pA: E.matmul(
                        pA[:], wp[k // 2][:, (k % 2) * 1024 + og * 128:
                                          (k % 2) * 1024 + (og + 1) * 128],
                        y3[:, k * CH:(k + 1) * CH],
                        start=(k == 0), stop=(k == 15)),
                        reads=(f"wp{k // 2}", "y3"),
                        writes=(rA,) if k in (0, 15) else (),
                        inc=1 if k in (0, 15) else 0)
                b.op("scalar", lambda E, og=og, pA=pA: E.activation(
                    mT[:, og * CH:(og + 1) * CH], pA[:], AF.Gelu),
                    reads=(rA,), writes=("mT",))
            for k in range(8):
                b.dma("gpsimd", wp[k][:], P["wglu_in"][k * 128:(k + 1) * 128, :],
                      writes=(f"wp{k}",))
            for og in range(16):
                pA = psA2x[og % 2]
                rA = f"psA{og % 2}"
                for k in range(8):
                    b.op("tensor", lambda E, og=og, k=k, pA=pA: E.matmul(
                        pA[:], wp[k][:, og * 128:(og + 1) * 128],
                        mT[:, k * CH:(k + 1) * CH],
                        start=(k == 0), stop=(k == 7)),
                        reads=(f"wp{k}", "mT"),
                        writes=(rA,) if k in (0, 7) else (),
                        inc=1 if k in (0, 7) else 0)
                if og < 8:
                    b.op("scalar", lambda E, og=og, pA=pA: E.activation(
                        g1[:, og * CH:(og + 1) * CH], pA[:], AF.Copy),
                        reads=(rA,), writes=("g1",))
                    b.op("vector", lambda E, og=og: E.tensor_scalar(
                        g1[:, og * CH:(og + 1) * CH],
                        g1[:, og * CH:(og + 1) * CH],
                        glub[:, og:og + 1], None, ALU.add),
                        reads=("g1", "glub"), writes=("g1",))
                else:
                    b.op("scalar", lambda E, og=og, pA=pA: E.activation(
                        sg2[:, (og - 8) * CH:(og - 7) * CH], pA[:], AF.Sigmoid,
                        bias=glub[:, og:og + 1]),
                        reads=(rA, "glub"), writes=("sg2",))
            b.dma("sync", xn2[:].rearrange("p (g t) -> p g t", g=8),
                  P["xnTb_hbm"][:, :, c0:c0 + CH], writes=("xn2",))
            for g in range(8):
                b.op("vector", lambda E, g=g: E.tensor_mul(
                    g1[:, g * CH:(g + 1) * CH], g1[:, g * CH:(g + 1) * CH],
                    sg2[:, g * CH:(g + 1) * CH]),
                    reads=("g1", "sg2"), writes=("g1",))
                b.op("vector", lambda E, g=g: E.tensor_add(
                    g1[:, g * CH:(g + 1) * CH], g1[:, g * CH:(g + 1) * CH],
                    xn2[:, g * CH:(g + 1) * CH]),
                    reads=("g1", "xn2"), writes=("g1",))
            b.dma("sync", P["sT_hbm"][:, :, c0:c0 + CH],
                  g1[:].rearrange("p (g t) -> p g t", g=8), reads=("g1",))
        return do_chunk

def emit_C(b, nc, P):
    NT = T // 128
    NT = T // 128
    with contextlib.ExitStack() as st:
        e = st.enter_context
        wp = [e(nc.sbuf_tensor(f"C_wp{i}", [128, 2048], BF16)) for i in range(8)]
        xnb = e(nc.sbuf_tensor("C_xnb", [128, 8 * T], BF16))
        hfb = e(nc.sbuf_tensor("C_hfb", [128, 16 * T], BF16))
        yacc = e(nc.sbuf_tensor("C_yacc", [128, 8 * T], BF16))
        eye = e(nc.sbuf_tensor("C_sb_eye", [128, 128], F32))
        ones = e(nc.sbuf_tensor("C_sb_ones", [128, 128], F32))
        sel8 = e(nc.sbuf_tensor("C_sb_sel8", [E0, E0 * 128], F32))
        b1t = e(nc.sbuf_tensor("C_b1t", [128, 32], F32))
        b2t = e(nc.sbuf_tensor("C_b2t", [E0, D], F32))
        rt = e(nc.sbuf_tensor("C_rt", [128, 20], F32))
        mtok = e(nc.sbuf_tensor("C_mtok", [128, E0], F32))
        scr = e(nc.sbuf_tensor("C_scr", [128, E0], F32))
        wcol = e(nc.sbuf_tensor("C_wcol", [128, 4], F32))
        mTt = e(nc.sbuf_tensor("C_mTt", [E0, T], F32))
        wnT = e(nc.sbuf_tensor("C_wnT", [1, T], F32))
        mrep = e(nc.sbuf_tensor("C_mrep", [128, T], BF16))
        wnrep = e(nc.sbuf_tensor("C_wnrep", [128, 512], BF16))
        gtmp = e(nc.sbuf_tensor("C_gtmp", [128, 8 * 512], BF16))
        ps8 = [e(nc.psum_tensor(f"C_ps8_{i}", [128, 512], F32)) for i in range(8)]

        b.dma("sync", eye[:], P["eye_in"], writes=("eye",))
        b.dma("sync", ones[:], P["ones_in"], writes=("ones",))
        b.dma("sync", sel8[:], P["sel8_in"], writes=("sel8",))
        b.dma("sync", b2t[:], P["b2_in"], writes=("b2t",))

        for i in range(NT):
            b.op("vector", lambda E, i=i: E.tensor_copy(
                rt[:], P["route_sb"][:, i * 20:(i + 1) * 20]),
                reads=("route_sb",), writes=("rt",))
            b.op("vector", lambda E: E.tensor_mul(scr[:], P["offs_rep"][:], rt[:, 0:8]),
                 reads=("offs_rep", "rt"), writes=("scr",))
            b.op("vector", lambda E: E.tensor_reduce(
                wcol[:, 0:1], scr[:], axis=AX.X, op=ALU.add),
                reads=("scr",), writes=("wcol",))
            b.op("vector", lambda E: E.tensor_add(wcol[:, 0:1], wcol[:, 0:1],
                                                  rt[:, 18:19]),
                 reads=("wcol", "rt"), writes=("wcol",))
            b.op("vector", lambda E: E.tensor_scalar(
                wcol[:, 0:1], wcol[:, 0:1], float(CAP), None, ALU.is_lt),
                reads=("wcol",), writes=("wcol",))
            b.op("vector", lambda E: E.tensor_mul(wcol[:, 0:1], wcol[:, 0:1],
                                                  rt[:, 16:17]),
                 reads=("wcol", "rt"), writes=("wcol",))
            b.op("vector", lambda E: E.tensor_mul(scr[:], P["offs_rep"][:], rt[:, 8:16]),
                 reads=("offs_rep", "rt"), writes=("scr",))
            b.op("vector", lambda E: E.tensor_reduce(
                wcol[:, 1:2], scr[:], axis=AX.X, op=ALU.add),
                reads=("scr",), writes=("wcol",))
            b.op("vector", lambda E: E.tensor_add(wcol[:, 1:2], wcol[:, 1:2],
                                                  rt[:, 19:20]),
                 reads=("wcol", "rt"), writes=("wcol",))
            b.op("vector", lambda E: E.tensor_scalar(
                wcol[:, 1:2], wcol[:, 1:2], float(CAP), None, ALU.is_lt),
                reads=("wcol",), writes=("wcol",))
            b.op("vector", lambda E: E.tensor_mul(wcol[:, 1:2], wcol[:, 1:2],
                                                  rt[:, 17:18]),
                 reads=("wcol", "rt"), writes=("wcol",))
            b.op("vector", lambda E: E.tensor_scalar(
                mtok[:], rt[:, 0:8], wcol[:, 0:1], None, ALU.mult),
                reads=("rt", "wcol"), writes=("mtok",))
            b.op("vector", lambda E: E.tensor_scalar(
                scr[:], rt[:, 8:16], wcol[:, 1:2], None, ALU.mult),
                reads=("rt", "wcol"), writes=("scr",))
            b.op("vector", lambda E: E.tensor_add(mtok[:], mtok[:], scr[:]),
                 reads=("mtok", "scr"), writes=("mtok",))
            b.op("vector", lambda E: E.tensor_add(wcol[:, 2:3], wcol[:, 0:1],
                                                  wcol[:, 1:2]),
                 reads=("wcol",), writes=("wcol",))
            b.op("vector", lambda E: E.tensor_scalar(
                wcol[:, 2:3], wcol[:, 2:3], 1e-6, None, ALU.max),
                reads=("wcol",), writes=("wcol",))
            b.op("vector", lambda E: E.reciprocal(wcol[:, 3:4], wcol[:, 2:3]),
                 reads=("wcol",), writes=("wcol",))
            b.op("tensor", lambda E: E.matmul(ps8[2][0:8, 0:128], mtok[:], eye[:],
                                              start=True, stop=True),
                 reads=("mtok", "eye"), writes=("ps2",))
            b.op("scalar", lambda E, i=i: E.activation(
                mTt[:, i * 128:(i + 1) * 128], ps8[2][0:8, 0:128], AF.Copy),
                reads=("ps2",), writes=("mTt",))
            b.op("tensor", lambda E: E.matmul(ps8[3][0:1, 0:128], wcol[:, 3:4],
                                              eye[:], start=True, stop=True),
                 reads=("wcol", "eye"), writes=("ps3",))
            b.op("scalar", lambda E, i=i: E.activation(
                wnT[:, i * 128:(i + 1) * 128], ps8[3][0:1, 0:128], AF.Copy),
                reads=("ps3",), writes=("wnT",))
        b.dma("sync", xnb[:].rearrange("p (g t) -> p g t", g=8),
              P["xnTb_hbm"][:, :, :], writes=("xnb",))
        b.op("vector", lambda E: E.memset(yacc[:], 0.0), writes=("yacc",))
        for ex in range(E0):
            b.dma("sync", b1t[:], P["b1_in"][ex], writes=("b1t",))
            for nt in range(4):
                b.op("tensor", lambda E, ex=ex, nt=nt: E.matmul(
                    ps8[4][:], sel8[:, ex * 128:(ex + 1) * 128],
                    mTt[:, nt * 512:(nt + 1) * 512], start=True, stop=True),
                    reads=("sel8", "mTt"), writes=("ps4",))
                b.op("scalar", lambda E, nt=nt: E.activation(
                    mrep[:, nt * 512:(nt + 1) * 512], ps8[4][:], AF.Copy),
                    reads=("ps4",), writes=("mrep",))
            for fb in range(2):
                for k in range(8):
                    b.dma("gpsimd", wp[k][:],
                          P["w1_in"][ex, k * 128:(k + 1) * 128,
                                fb * 2048:fb * 2048 + 2048],
                          writes=(f"wp{k}",))
                for ogl in range(16):
                    og = fb * 16 + ogl
                    for nt in range(4):
                        pb = (ogl * 4 + nt) % 2
                        for k in range(8):
                            b.op("tensor", lambda E, ogl=ogl, nt=nt, k=k, pb=pb: E.matmul(
                                ps8[pb][:], wp[k][:, ogl * 128:(ogl + 1) * 128],
                                xnb[:, k * T + nt * 512:k * T + (nt + 1) * 512],
                                start=(k == 0), stop=(k == 7)),
                                reads=(f"wp{k}", "xnb"),
                                writes=(f"ps{pb}",) if k in (0, 7) else (),
                                inc=1 if k in (0, 7) else 0)
                        b.op("scalar", lambda E, og=og, ogl=ogl, nt=nt, pb=pb: E.activation(
                            hfb[:, ogl * T + nt * 512:ogl * T + (nt + 1) * 512],
                            ps8[pb][:], AF.Gelu, bias=b1t[:, og:og + 1]),
                            reads=(f"ps{pb}", "b1t"), writes=("hfb",))
                    b.op("vector", lambda E, ogl=ogl: E.tensor_mul(
                        hfb[:, ogl * T:(ogl + 1) * T],
                        hfb[:, ogl * T:(ogl + 1) * T], mrep[:]),
                        reads=("hfb", "mrep"), writes=("hfb",))
                for k2 in range(16):
                    b.dma("gpsimd", wp[k2 // 2][:, (k2 % 2) * 1024:(k2 % 2 + 1) * 1024],
                          P["w2_in"][ex, fb * 2048 + k2 * 128:fb * 2048 + (k2 + 1) * 128, :],
                          writes=(f"wp{k2 // 2}",))
                for nt in range(4):
                    for k2 in range(16):
                        for og in range(8):
                            b.op("tensor", lambda E, k2=k2, og=og, nt=nt: E.matmul(
                                ps8[og][:],
                                wp[k2 // 2][:, (k2 % 2) * 1024 + og * 128:
                                            (k2 % 2) * 1024 + (og + 1) * 128],
                                hfb[:, k2 * T + nt * 512:k2 * T + (nt + 1) * 512],
                                start=(k2 == 0), stop=(k2 == 15)),
                                reads=(f"wp{k2 // 2}", "hfb"),
                                writes=(f"ps{og}",) if k2 in (0, 15) else (),
                                inc=1 if k2 in (0, 15) else 0)
                    for og in range(8):
                        b.op("vector", lambda E, og=og, nt=nt: E.scalar_tensor_tensor(
                            yacc[:, og * T + nt * 512:og * T + (nt + 1) * 512],
                            ps8[og][:], 1.0,
                            yacc[:, og * T + nt * 512:og * T + (nt + 1) * 512],
                            ALU.mult, ALU.add),
                            reads=(f"ps{og}", "yacc"), writes=("yacc",))
        for og in range(8):
            for nt in range(4):
                b.op("tensor", lambda E, og=og, nt=nt: E.matmul(
                    ps8[og][:], b2t[:, og * 128:(og + 1) * 128],
                    mTt[:, nt * 512:(nt + 1) * 512], start=True, stop=True),
                    reads=("b2t", "mTt"), writes=(f"ps{og}",))
                b.op("vector", lambda E, og=og, nt=nt: E.scalar_tensor_tensor(
                    yacc[:, og * T + nt * 512:og * T + (nt + 1) * 512],
                    ps8[og][:], 1.0,
                    yacc[:, og * T + nt * 512:og * T + (nt + 1) * 512],
                    ALU.mult, ALU.add),
                    reads=(f"ps{og}", "yacc"), writes=("yacc",))
        for nt in range(4):
            b.op("tensor", lambda E, nt=nt: E.matmul(
                ps8[4][:], ones[0:1, :], wnT[:, nt * 512:(nt + 1) * 512],
                start=True, stop=True),
                reads=("ones", "wnT"), writes=("ps4",))
            b.op("scalar", lambda E: E.activation(wnrep[:], ps8[4][:], AF.Copy),
                 reads=("ps4",), writes=("wnrep",))
            for og in range(8):
                b.op("vector", lambda E, og=og, nt=nt: E.tensor_mul(
                    gtmp[:, og * 512:(og + 1) * 512],
                    yacc[:, og * T + nt * 512:og * T + (nt + 1) * 512], wnrep[:]),
                    reads=("yacc", "wnrep"), writes=("gtmp",))
                b.op("scalar", lambda E, og=og: E.activation(
                    gtmp[:, og * 512:(og + 1) * 512],
                    gtmp[:, og * 512:(og + 1) * 512], AF.Silu),
                    reads=("gtmp",), writes=("gtmp",))
            b.dma("sync", P["gT_ap"].rearrange("(g p) t -> p g t", p=128)[:, :, nt * 512:(nt + 1) * 512],
                  gtmp[:, 0:8 * 512].rearrange("p (g t) -> p g t", g=8),
                  reads=("gtmp",))


def emit_D(b, nc, P):
    CH = 512
    CH = 512
    with contextlib.ExitStack() as st:
        e = st.enter_context
        eye = e(nc.sbuf_tensor("D_sb_eye", [128, 128], F32))
        ones = e(nc.sbuf_tensor("D_sb_ones", [128, 128], F32))
        xt_ = [e(nc.sbuf_tensor(f"D_xt{i}", [128, D], F32)) for i in range(2)]
        xT_ = [e(nc.sbuf_tensor(f"D_xT{i}", [128, 8 * CH], F32)) for i in range(2)]
        s_c_ = [e(nc.sbuf_tensor(f"D_s_c{i}", [128, 8 * CH], BF16)) for i in range(2)]
        g_c_ = [e(nc.sbuf_tensor(f"D_g_c{i}", [128, 8 * CH], BF16)) for i in range(2)]
        x1c_ = [e(nc.sbuf_tensor(f"D_x1c{i}", [128, 8 * CH], F32)) for i in range(2)]
        xsq_ = [e(nc.sbuf_tensor(f"D_xsq{i}", [128, 8 * CH], F32)) for i in range(2)]
        strow_ = [e(nc.sbuf_tensor(f"D_strow{i}", [1, 6 * CH], F32)) for i in range(2)]
        rrep_ = [e(nc.sbuf_tensor(f"D_rrep{i}", [128, CH], F32)) for i in range(2)]
        mrep2_ = [e(nc.sbuf_tensor(f"D_mrep2{i}", [128, CH], F32)) for i in range(2)]
        cvals = e(nc.sbuf_tensor("D_cvals", [128, 1], F32))
        ps_t_ = [e(nc.psum_tensor(f"D_ps_t{i}", [128, 512], F32)) for i in range(2)]
        ps_sq_ = [e(nc.psum_tensor(f"D_ps_sq{i}", [64, 512], F32)) for i in range(2)]
        ps_s_ = [t[0:1, :] for t in ps_sq_]
        ps_q_ = [t[32:33, :] for t in ps_sq_]
        ps_b12_ = [e(nc.psum_tensor(f"D_ps_b12{i}", [128, 512], F32)) for i in range(2)]
        ps_b1_ = ps_b12_
        ps_b2_ = ps_b12_

        b.dma("sync", eye[:], P["eye_in"], writes=("eye",))
        b.dma("sync", ones[:], P["ones_in"], writes=("ones",))
        b.op("vector", lambda E: E.memset(cvals[:], 1e-5), writes=("cvals",))
        def _d_s0(ci, c0, xt, xT, s_c, g_c, x1c, xsq, strow, rrep, mrep2,
                  ps_t, ps_s, ps_q, ps_b1, ps_b2, Z):
            b.dma("sync", s_c[:].rearrange("p (g t) -> p g t", g=8),
                  P["sT_hbm"][:, :, c0:c0 + CH], writes=("Ds_c" + Z,))
            b.dma("sync", g_c[:].rearrange("p (g t) -> p g t", g=8),
                  P["gT_hbm"][:, :, c0:c0 + CH], writes=("Dg_c" + Z,))
            for tt in range(4):
                b.dma("sync", xt[:], P["x_in"][c0 + tt * 128:c0 + (tt + 1) * 128, :],
                      writes=("Dxt" + Z,))
                for jj in (0, 4):
                    for j in range(jj, jj + 4):
                        b.op("tensor", lambda E, j=j, jj=jj: E.transpose(
                            ps_t[:, (j - jj) * 128:(j - jj) * 128 + 128],
                            xt[:, j * 128:(j + 1) * 128], eye[:]),
                            reads=("Dxt" + Z, "eye"), writes=("Dps_t" + Z,))
                    for j in range(jj, jj + 4):
                        b.op("scalar", lambda E, j=j, jj=jj, tt=tt: E.activation(
                            xT[:, j * CH + tt * 128:j * CH + (tt + 1) * 128],
                            ps_t[:, (j - jj) * 128:(j - jj) * 128 + 128], AF.Copy),
                            reads=("Dps_t" + Z,), writes=("DxT" + Z,))

        def _d_s1(ci, c0, xt, xT, s_c, g_c, x1c, xsq, strow, rrep, mrep2,
                  ps_t, ps_s, ps_q, ps_b1, ps_b2, Z):
            for g in range(8):
                b.op("vector", lambda E, g=g: E.tensor_mul(
                    x1c[:, g * CH:(g + 1) * CH], s_c[:, g * CH:(g + 1) * CH],
                    g_c[:, g * CH:(g + 1) * CH]),
                    reads=("Ds_c" + Z, "Dg_c" + Z), writes=("Dx1c" + Z,))
                b.op("vector", lambda E, g=g: E.tensor_add(
                    x1c[:, g * CH:(g + 1) * CH], x1c[:, g * CH:(g + 1) * CH],
                    xT[:, g * CH:(g + 1) * CH]),
                    reads=("Dx1c" + Z, "DxT" + Z), writes=("Dx1c" + Z,))
            b.dma("sync", P["x1_hbm"][:, :, c0:c0 + CH],
                  x1c[:].rearrange("p (g t) -> p g t", g=8), reads=("Dx1c" + Z,))

        def _d_s2(ci, c0, xt, xT, s_c, g_c, x1c, xsq, strow, rrep, mrep2,
                  ps_t, ps_s, ps_q, ps_b1, ps_b2, Z):
            b.op("scalar", lambda E: E.activation(xsq[:], x1c[:], AF.Square),
                 reads=("Dx1c" + Z,), writes=("Dxsq" + Z,))
            for g in range(8):
                b.op("tensor", lambda E, g=g: E.matmul(
                    ps_s[:], ones[:, 0:1], x1c[:, g * CH:(g + 1) * CH],
                    start=(g == 0), stop=(g == 7)),
                    reads=("ones", "Dx1c" + Z),
                    writes=("Dps_sq" + Z,) if g in (0, 7) else (),
                    inc=1 if g in (0, 7) else 0)
            for g in range(8):
                b.op("tensor", lambda E, g=g: E.matmul(
                    ps_q[:], ones[:, 0:1], xsq[:, g * CH:(g + 1) * CH],
                    start=(g == 0), stop=(g == 7)),
                    reads=("ones", "Dxsq" + Z),
                    writes=("Dps_sq" + Z,) if g in (0, 7) else (),
                    inc=1 if g in (0, 7) else 0)

        def _d_s3(ci, c0, xt, xT, s_c, g_c, x1c, xsq, strow, rrep, mrep2,
                  ps_t, ps_s, ps_q, ps_b1, ps_b2, Z):
            b.op("scalar", lambda E: E.activation(
                strow[:, 0 * CH:1 * CH], ps_s[:], AF.Copy, scale=1.0 / D),
                reads=("Dps_sq" + Z,), writes=("Dstrow" + Z,))
            b.op("scalar", lambda E: E.activation(
                strow[:, 1 * CH:2 * CH], ps_q[:], AF.Copy, scale=1.0 / D),
                reads=("Dps_sq" + Z,), writes=("Dstrow" + Z,))
            b.op("vector", lambda E: E.tensor_mul(
                strow[:, 2 * CH:3 * CH], strow[:, 0:CH], strow[:, 0:CH]),
                reads=("Dstrow" + Z,), writes=("Dstrow" + Z,))
            b.op("vector", lambda E: E.tensor_sub(
                strow[:, 2 * CH:3 * CH], strow[:, CH:2 * CH], strow[:, 2 * CH:3 * CH]),
                reads=("Dstrow" + Z,), writes=("Dstrow" + Z,))
            b.op("scalar", lambda E: E.activation(
                strow[:, 3 * CH:4 * CH], strow[:, 2 * CH:3 * CH], AF.Ln,
                bias=cvals[0:1, 0:1]),
                reads=("Dstrow" + Z, "cvals"), writes=("Dstrow" + Z,))
            b.op("scalar", lambda E: E.activation(
                strow[:, 3 * CH:4 * CH], strow[:, 3 * CH:4 * CH], AF.Exp, scale=-0.5),
                reads=("Dstrow" + Z,), writes=("Dstrow" + Z,))
            b.op("vector", lambda E: E.tensor_mul(
                strow[:, 4 * CH:5 * CH], strow[:, 0:CH], strow[:, 3 * CH:4 * CH]),
                reads=("Dstrow" + Z,), writes=("Dstrow" + Z,))
            b.op("vector", lambda E: E.tensor_scalar(
                strow[:, 4 * CH:5 * CH], strow[:, 4 * CH:5 * CH], -1.0, None,
                ALU.mult),
                reads=("Dstrow" + Z,), writes=("Dstrow" + Z,))
            b.op("tensor", lambda E: E.matmul(
                ps_b1[:], ones[0:1, :], strow[:, 3 * CH:4 * CH],
                start=True, stop=True),
                reads=("ones", "Dstrow" + Z), writes=("Dps_b" + Z,))
            b.op("scalar", lambda E: E.activation(rrep[:], ps_b1[:], AF.Copy),
                 reads=("Dps_b" + Z,), writes=("Drrep" + Z,))
            b.op("tensor", lambda E: E.matmul(
                ps_b2[:], ones[0:1, :], strow[:, 4 * CH:5 * CH],
                start=True, stop=True),
                reads=("ones", "Dstrow" + Z), writes=("Dps_b" + Z,))
            b.op("scalar", lambda E: E.activation(mrep2[:], ps_b2[:], AF.Copy),
                 reads=("Dps_b" + Z,), writes=("Dmrep2" + Z,))

        def _d_s4(ci, c0, xt, xT, s_c, g_c, x1c, xsq, strow, rrep, mrep2,
                  ps_t, ps_s, ps_q, ps_b1, ps_b2, Z):
            for g in range(8):
                b.op("vector", lambda E, g=g: E.tensor_mul(
                    xsq[:, g * CH:(g + 1) * CH], x1c[:, g * CH:(g + 1) * CH],
                    rrep[:]),
                    reads=("Dx1c" + Z, "Drrep" + Z), writes=("Dxsq" + Z,))
                b.op("vector", lambda E, g=g: E.tensor_add(
                    g_c[:, g * CH:(g + 1) * CH], xsq[:, g * CH:(g + 1) * CH],
                    mrep2[:]),
                    reads=("Dxsq" + Z, "Dmrep2" + Z), writes=("Dg_c" + Z,))
            b.dma("sync", P["x1n_hbm"][:, :, c0:c0 + CH],
                  g_c[:].rearrange("p (g t) -> p g t", g=8), reads=("Dg_c" + Z,))

        def dargs(ci):
            p = ci % 2
            return (ci, ci * CH, xt_[p], xT_[p], s_c_[p], g_c_[p], x1c_[p],
                    xsq_[p], strow_[p], rrep_[p], mrep2_[p], ps_t_[p], ps_s_[p],
                    ps_q_[p], ps_b1_[p], ps_b2_[p], f"{p}")

        for base in (0, 2):
            pair = [dargs(base), dargs(base + 1)]
            for st_fn in (_d_s0, _d_s1, _d_s2, _d_s3, _d_s4):
                for a in pair:
                    st_fn(*a)

def emit_E(b, nc, P):
    NT = T // 128
    SC = 1.0 / np.sqrt(2048.0)
    NT = T // 128
    SC = 1.0 / np.sqrt(2048.0)
    with contextlib.ExitStack() as st:
        e = st.enter_context
        x1n = e(nc.sbuf_tensor("E_x1n", [128, 8 * T], BF16))
        s1 = e(nc.sbuf_tensor("E_s1", [128, 8 * T], BF16))
        wpool = e(nc.sbuf_tensor("E_wpool", [128, 24 * 1024], BF16))
        eye = e(nc.sbuf_tensor("E_sb_eye", [128, 128], F32))
        eyeb = e(nc.sbuf_tensor("E_sb_eyeb", [128, 128], BF16))
        ones = e(nc.sbuf_tensor("E_sb_ones", [128, 128], F32))
        onesb = e(nc.sbuf_tensor("E_sb_onesb", [128, 128], BF16))
        sel8 = e(nc.sbuf_tensor("E_sb_sel8", [E0, E0 * 128], F32))
        lepeb = e(nc.sbuf_tensor("E_lepeb", [128, 8], F32))
        qkbr = e(nc.sbuf_tensor("E_qkbr", [1, 2048], F32))
        qkbrep = e(nc.sbuf_tensor("E_qkbrep", [128, 2048], F32))
        threp = e(nc.sbuf_tensor("E_threp", [128, 512], F32))
        tcol = e(nc.sbuf_tensor("E_tcol", [128, 1], F32))
        ang = e(nc.sbuf_tensor("E_ang", [128, 512], F32))
        ang2 = e(nc.sbuf_tensor("E_ang2", [128, 512], F32))
        angi = e(nc.sbuf_tensor("E_angi", [128, 512], I32))
        sint = e(nc.sbuf_tensor("E_sint", [128, 512], BF16))
        cost = e(nc.sbuf_tensor("E_cost", [128, 512], BF16))
        kqf = e(nc.sbuf_tensor("E_kqf", [128, 1024], F32))
        sca = e(nc.sbuf_tensor("E_sca", [128, 1024], F32))
        kq = e(nc.sbuf_tensor("E_kq", [128, 1024], BF16))
        kqr = e(nc.sbuf_tensor("E_kqr", [128, 1024], BF16))
        v_t = e(nc.sbuf_tensor("E_v_t", [128, 1024], BF16))
        qrT = e(nc.sbuf_tensor("E_qrT", [128, 1024], BF16))
        kv_sb = e(nc.sbuf_tensor("E_kv_sb", [128, 1024], BF16))
        kmean = e(nc.sbuf_tensor("E_kmean", [1, 1024], F32))
        kmrep = e(nc.sbuf_tensor("E_kmrep", [128, 1024], BF16))
        zden = e(nc.sbuf_tensor("E_zden", [128, 8], F32))
        zT = e(nc.sbuf_tensor("E_zT", [8, T], F32))
        zrep = e(nc.sbuf_tensor("E_zrep", [128, 512], BF16))
        otmp = e(nc.sbuf_tensor("E_otmp", [128, 1024], BF16))
        ps_a = e(nc.psum_tensor("E_ps_a", [128, 512], F32))
        ps_trb = e(nc.psum_tensor("E_ps_trb", [128, 512], BF16))
        ps_kv = e(nc.psum_tensor("E_ps_kv", [128, 1024], F32))
        ps_km = e(nc.psum_tensor("E_ps_km", [1, 1024], F32))
        ps_o = e(nc.psum_tensor("E_ps_o", [128, 1024], F32))

        b.dma("sync", eye[:], P["eye_in"], writes=("eye",))
        b.dma("sync", ones[:], P["ones_in"], writes=("ones",))
        b.dma("gpsimd", eyeb[:], P["eye_in"], writes=("eyeb",))
        b.dma("gpsimd", onesb[:], P["ones_in"], writes=("onesb",))
        b.dma("sync", sel8[:], P["sel8_in"], writes=("sel8",))
        b.dma("sync", lepeb[:], P["lepeb_in"], writes=("lepeb",))
        b.dma("sync", qkbr[:], P["qkb_in"].rearrange("(o t) -> o t", o=1), writes=("qkbr",))
        b.dma("sync", x1n[:].rearrange("p (g t) -> p g t", g=8),
              P["x1n_hbm"][:, :, :], writes=("x1n",))
        for nt in range(4):
            b.op("tensor", lambda E, nt=nt: E.matmul(
                ps_a[:], ones[0:1, :], qkbr[:, nt * 512:(nt + 1) * 512],
                start=True, stop=True),
                reads=("ones", "qkbr"), writes=("ps_a",))
            b.op("scalar", lambda E, nt=nt: E.activation(
                qkbrep[:, nt * 512:(nt + 1) * 512], ps_a[:], AF.Copy),
                reads=("ps_a",), writes=("qkbrep",))
        b.op("gpsimd", lambda E: E.iota(ang[0:1, :], [[1, 512]],
                                        channel_multiplier=0,
                                        allow_small_or_imprecise_dtypes=True),
             writes=("ang",))
        b.op("scalar", lambda E: E.activation(
            ang[0:1, :], ang[0:1, :], AF.Exp, scale=float(-np.log(10000.0) / 512)),
            reads=("ang",), writes=("ang",))
        b.op("tensor", lambda E: E.matmul(ps_a[:], ones[0:1, :], ang[0:1, :],
                                          start=True, stop=True),
             reads=("ones", "ang"), writes=("ps_a",))
        b.op("scalar", lambda E: E.activation(threp[:], ps_a[:], AF.Copy),
             reads=("ps_a",), writes=("threp",))

        for w in range(3):
            for k in range(8):
                b.dma("gpsimd", wpool[:, (w * 8 + k) * 1024:(w * 8 + k + 1) * 1024],
                      P["lepe_in"][w, k * 128:(k + 1) * 128, :], writes=("wpool",))
        for og in range(8):
            for nt in range(4):
                first = True
                for w in range(3):
                    lo = nt * 512 + (w - 1)
                    hi = lo + 512
                    plo, phi = 0, 512
                    if lo < 0:
                        plo = 1; lo = 0
                    if hi > T:
                        phi = 511; hi = T
                    for k in range(8):
                        last = (w == 2 and k == 7)
                        b.op("tensor", lambda E, og=og, w=w, k=k, lo=lo, hi=hi,
                             plo=plo, phi=phi, first=first: E.matmul(
                            ps_a[:, plo:phi],
                            wpool[:, (w * 8 + k) * 1024 + og * 128:
                                  (w * 8 + k) * 1024 + (og + 1) * 128],
                            x1n[:, k * T + lo:k * T + hi],
                            start=first, stop=(w == 2 and k == 7)),
                            reads=("wpool", "x1n"),
                            writes=("ps_a",) if (first or last) else (),
                            inc=1 if (first or last) else 0)
                        first = False
                b.op("scalar", lambda E, og=og, nt=nt: E.activation(
                    s1[:, og * T + nt * 512:og * T + (nt + 1) * 512],
                    ps_a[:], AF.Copy),
                    reads=("ps_a",), writes=("s1",))
                b.op("vector", lambda E, og=og, nt=nt: E.tensor_scalar(
                    s1[:, og * T + nt * 512:og * T + (nt + 1) * 512],
                    s1[:, og * T + nt * 512:og * T + (nt + 1) * 512],
                    lepeb[:, og:og + 1], None, ALU.add),
                    reads=("s1", "lepeb"), writes=("s1",))

        for k in range(8):
            for half in range(2):
                b.dma("gpsimd", wpool[:, (k * 2 + half) * 1024:(k * 2 + half + 1) * 1024],
                      P["qkw_in"][k * 128:(k + 1) * 128, half * 1024:(half + 1) * 1024],
                      writes=("wpool",))

        def rope_tables(i):
            b.op("gpsimd", lambda E, i=i: E.iota(tcol[:], [[1, 1]], base=i * 128,
                                                 channel_multiplier=1,
                                                 allow_small_or_imprecise_dtypes=True),
                 writes=("tcol",))
            b.op("vector", lambda E: E.tensor_scalar(
                ang[:], threp[:], tcol[:, 0:1], None, ALU.mult),
                reads=("threp", "tcol"), writes=("ang",))
            for src, dst, dstk in ((ang, sint, "sint"), (ang, cost, "cost")):
                if dst is cost:
                    b.op("vector", lambda E: E.tensor_scalar(
                        ang[:], ang[:], float(np.pi / 2), None, ALU.add),
                        reads=("ang",), writes=("ang",))
                b.op("vector", lambda E: E.tensor_scalar(
                    ang2[:], ang[:], float(1.0 / TWOPI), None, ALU.mult),
                    reads=("ang",), writes=("ang2",))
                b.op("vector", lambda E: E.tensor_copy(angi[:], ang2[:]),
                     reads=("ang2",), writes=("angi",))
                b.op("vector", lambda E: E.tensor_copy(ang2[:], angi[:]),
                     reads=("angi",), writes=("ang2",))
                b.op("vector", lambda E: E.scalar_tensor_tensor(
                    ang2[:], ang2[:], float(-TWOPI), ang[:], ALU.mult, ALU.add),
                    reads=("ang2", "ang"), writes=("ang2",))
                b.op("vector", lambda E: E.tensor_scalar(
                    sca[:, 0:512], ang2[:], float(np.pi), None, ALU.is_gt),
                    reads=("ang2",), writes=("sca",))
                b.op("vector", lambda E: E.scalar_tensor_tensor(
                    ang2[:], sca[:, 0:512], float(-TWOPI), ang2[:], ALU.mult, ALU.add),
                    reads=("sca", "ang2"), writes=("ang2",))
                b.op("vector", lambda E: E.tensor_scalar(
                    sca[:, 0:512], ang2[:], float(-np.pi), None, ALU.is_lt),
                    reads=("ang2",), writes=("sca",))
                b.op("vector", lambda E: E.scalar_tensor_tensor(
                    ang2[:], sca[:, 0:512], float(TWOPI), ang2[:], ALU.mult, ALU.add),
                    reads=("sca", "ang2"), writes=("ang2",))
                b.op("scalar", lambda E, dst=dst: E.activation(
                    dst[:], ang2[:], AF.Sin),
                    reads=("ang2",), writes=(dstk,))

        def qk_stage(i, half):
            for nsub in range(2):
                for k in range(8):
                    b.op("tensor", lambda E, i=i, half=half, nsub=nsub, k=k: E.matmul(
                        ps_a[:], x1n[:, k * T + i * 128:k * T + (i + 1) * 128],
                        wpool[:, (k * 2 + half) * 1024 + nsub * 512:
                              (k * 2 + half) * 1024 + (nsub + 1) * 512],
                        start=(k == 0), stop=(k == 7)),
                        reads=("x1n", "wpool"),
                        writes=("ps_a",) if k in (0, 7) else (),
                        inc=1 if k in (0, 7) else 0)
                b.op("scalar", lambda E, nsub=nsub: E.activation(
                    kqf[:, nsub * 512:(nsub + 1) * 512], ps_a[:], AF.Copy),
                    reads=("ps_a",), writes=("kqf",))
            b.op("vector", lambda E, half=half: E.tensor_add(
                kqf[:], kqf[:], qkbrep[:, half * 1024:(half + 1) * 1024]),
                reads=("kqf", "qkbrep"), writes=("kqf",))
            b.op("vector", lambda E: E.tensor_scalar(
                sca[:], kqf[:], 0.0, None, ALU.min),
                reads=("kqf",), writes=("sca",))
            b.op("scalar", lambda E: E.activation(sca[:], sca[:], AF.Exp),
                 reads=("sca",), writes=("sca",))
            b.op("vector", lambda E: E.tensor_scalar(
                kqf[:], kqf[:], 0.0, None, ALU.max),
                reads=("kqf",), writes=("kqf",))
            b.op("vector", lambda E: E.tensor_add(kq[:], kqf[:], sca[:]),
                 reads=("kqf", "sca"), writes=("kq",))
            b.op("vector", lambda E: E.tensor_mul(
                kqr[:, 0:512], cost[:], kq[:, 0:1024:2]),
                reads=("cost", "kq"), writes=("kqr",))
            b.op("vector", lambda E: E.tensor_mul(
                sca[:, 0:512], sint[:], kq[:, 1:1024:2]),
                reads=("sint", "kq"), writes=("sca",))
            b.op("vector", lambda E: E.tensor_sub(
                kqr[:, 0:512], kqr[:, 0:512], sca[:, 0:512]),
                reads=("kqr", "sca"), writes=("kqr",))
            b.op("vector", lambda E: E.tensor_mul(
                kqr[:, 512:1024], cost[:], kq[:, 1:1024:2]),
                reads=("cost", "kq"), writes=("kqr",))
            b.op("vector", lambda E: E.tensor_mul(
                sca[:, 0:512], sint[:], kq[:, 0:1024:2]),
                reads=("sint", "kq"), writes=("sca",))
            b.op("vector", lambda E: E.tensor_add(
                kqr[:, 512:1024], kqr[:, 512:1024], sca[:, 0:512]),
                reads=("kqr", "sca"), writes=("kqr",))

        for i in range(NT):
            rope_tables(i)
            qk_stage(i, 1)
            for jj in (0, 4):
                for j in range(jj, jj + 4):
                    b.op("tensor", lambda E, i=i, j=j, jj=jj: E.transpose(
                        ps_trb[:, (j - jj) * 128:(j - jj) * 128 + 128],
                        x1n[:, j * T + i * 128:j * T + (i + 1) * 128], eyeb[:]),
                        reads=("x1n", "eyeb"), writes=("ps_trb",))
                for j in range(jj, jj + 4):
                    b.op("scalar", lambda E, j=j, jj=jj: E.activation(
                        v_t[:, j * 128:(j + 1) * 128],
                        ps_trb[:, (j - jj) * 128:(j - jj) * 128 + 128], AF.Copy),
                        reads=("ps_trb",), writes=("v_t",))
            for h in range(NH):
                b.op("tensor", lambda E, i=i, h=h: E.matmul(
                    ps_kv[:, h * 128:(h + 1) * 128], kqr[:, h * 128:(h + 1) * 128],
                    v_t[:, h * 128:(h + 1) * 128],
                    start=(i == 0 and h % 4 == 0),
                    stop=(i == NT - 1 and h % 4 == 3)),
                    reads=("kqr", "v_t"),
                    writes=("ps_kv",) if i in (0, NT - 1) else (),
                    inc=1 if i in (0, NT - 1) else 0)
            for half in range(2):
                b.op("tensor", lambda E, i=i, half=half: E.matmul(
                    ps_km[:, half * 512:(half + 1) * 512], onesb[:, 0:1],
                    kq[:, half * 512:(half + 1) * 512],
                    start=(i == 0), stop=(i == NT - 1)),
                    reads=("onesb", "kq"),
                    writes=("ps_km",) if i in (0, NT - 1) else (),
                    inc=1 if i in (0, NT - 1) else 0)
        b.op("scalar", lambda E: E.activation(kv_sb[:], ps_kv[:], AF.Copy,
                                              scale=float(SC)),
             reads=("ps_kv",), writes=("kv_sb",))
        b.op("scalar", lambda E: E.activation(kmean[:], ps_km[:], AF.Copy,
                                              scale=float(1.0 / T)),
             reads=("ps_km",), writes=("kmean",))
        for half in range(2):
            b.op("tensor", lambda E, half=half: E.matmul(
                ps_o[:, half * 512:(half + 1) * 512], ones[0:1, :],
                kmean[:, half * 512:(half + 1) * 512], start=True, stop=True),
                reads=("ones", "kmean"), writes=("ps_o",))
        b.op("scalar", lambda E: E.activation(kmrep[:], ps_o[:], AF.Copy),
             reads=("ps_o",), writes=("kmrep",))

        for i in range(NT):
            rope_tables(i)
            qk_stage(i, 0)
            b.op("vector", lambda E: E.tensor_mul(sca[:], kq[:], kmrep[:]),
                 reads=("kq", "kmrep"), writes=("sca",))
            b.op("vector", lambda E: E.tensor_reduce(
                zden[:], sca[:].rearrange("p (h d) -> p h d", h=8),
                axis=AX.X, op=ALU.add),
                reads=("sca",), writes=("zden",))
            b.op("vector", lambda E: E.tensor_scalar(
                zden[:], zden[:], 1e-6, None, ALU.add),
                reads=("zden",), writes=("zden",))
            b.op("vector", lambda E: E.reciprocal(zden[:], zden[:]),
                 reads=("zden",), writes=("zden",))
            b.op("tensor", lambda E: E.matmul(ps_a[0:8, 0:128], zden[:], eye[:],
                                              start=True, stop=True),
                 reads=("zden", "eye"), writes=("ps_a",))
            b.op("scalar", lambda E, i=i: E.activation(
                zT[:, i * 128:(i + 1) * 128], ps_a[0:8, 0:128], AF.Copy),
                reads=("ps_a",), writes=("zT",))
            for jj in (0, 4):
                for j in range(jj, jj + 4):
                    b.op("tensor", lambda E, j=j, jj=jj: E.transpose(
                        ps_trb[:, (j - jj) * 128:(j - jj) * 128 + 128],
                        kqr[:, j * 128:(j + 1) * 128], eyeb[:]),
                        reads=("kqr", "eyeb"), writes=("ps_trb",))
                for j in range(jj, jj + 4):
                    b.op("scalar", lambda E, j=j, jj=jj: E.activation(
                        qrT[:, j * 128:(j + 1) * 128],
                        ps_trb[:, (j - jj) * 128:(j - jj) * 128 + 128], AF.Copy),
                        reads=("ps_trb",), writes=("qrT",))
            for h in range(NH):
                b.op("tensor", lambda E, h=h: E.matmul(
                    ps_o[:, h * 128:(h + 1) * 128], kv_sb[:, h * 128:(h + 1) * 128],
                    qrT[:, h * 128:(h + 1) * 128],
                    start=(h % 4 == 0), stop=(h % 4 == 3)),
                    reads=("kv_sb", "qrT"), writes=("ps_o",))
            for hq in range(2):
                for h4 in range(4):
                    h = hq * 4 + h4
                    b.op("tensor", lambda E, h=h, h4=h4, i=i: E.matmul(
                        ps_a[:, h4 * 128:(h4 + 1) * 128],
                        sel8[:, h * 128:(h + 1) * 128],
                        zT[:, i * 128:(i + 1) * 128],
                        start=(h4 == 0), stop=(h4 == 3)),
                        reads=("sel8", "zT"), writes=("ps_a",))
                b.op("scalar", lambda E, hq=hq: E.activation(
                    zrep[:], ps_a[:], AF.Copy),
                    reads=("ps_a",), writes=("zrep",))
                for h4 in range(4):
                    h = hq * 4 + h4
                    b.op("vector", lambda E, h=h, h4=h4: E.tensor_mul(
                        otmp[:, h * 128:(h + 1) * 128],
                        ps_o[:, h * 128:(h + 1) * 128],
                        zrep[:, h4 * 128:(h4 + 1) * 128]),
                        reads=("ps_o", "zrep"), writes=("otmp",))
            for h in range(NH):
                b.op("vector", lambda E, h=h, i=i: E.tensor_add(
                    s1[:, h * T + i * 128:h * T + (i + 1) * 128],
                    s1[:, h * T + i * 128:h * T + (i + 1) * 128],
                    otmp[:, h * 128:(h + 1) * 128]),
                    reads=("s1", "otmp"), writes=("s1",))
        for nt in range(4):
            b.dma("sync", P["s1_hbm"][:, :, nt * 512:(nt + 1) * 512],
                  s1[:, :].rearrange("p (g t) -> p g t", g=8)[:, :, nt * 512:(nt + 1) * 512],
                  reads=("s1",))


def emit_F(b, nc, P):
    NT = T // 128
    NT = T // 128
    with contextlib.ExitStack() as st:
        e = st.enter_context
        x1n = e(nc.sbuf_tensor("F_x1n", [128, 8 * T], BF16))
        wp = [e(nc.sbuf_tensor(f"F_wp{i}", [128, 2048], BF16)) for i in range(8)]
        eye = e(nc.sbuf_tensor("F_sb_eye", [128, 128], F32))
        ones = e(nc.sbuf_tensor("F_sb_ones", [128, 128], F32))
        sel8 = e(nc.sbuf_tensor("F_sb_sel8", [E0, E0 * 128], F32))
        projP = e(nc.sbuf_tensor("F_sb_projP", [128, 48], BF16))
        anchT = e(nc.sbuf_tensor("F_sb_anchT", [6, 6], F32))
        cscrep = e(nc.sbuf_tensor("F_cscrep", [128, 1], F32))
        qtok = e(nc.sbuf_tensor("F_qtok", [128, 8], F32))
        dott = e(nc.sbuf_tensor("F_dott", [128, 8], F32))
        o1 = e(nc.sbuf_tensor("F_o1", [128, 8], F32))
        o2 = e(nc.sbuf_tensor("F_o2", [128, 8], F32))
        scr = e(nc.sbuf_tensor("F_scr", [128, 8], F32))
        stat = e(nc.sbuf_tensor("F_stat", [128, 8], F32))
        onecol = e(nc.sbuf_tensor("F_onecol", [128, 8], F32))
        ewm = e(nc.sbuf_tensor("F_ewm", [128, 8], F32))
        ewT = e(nc.sbuf_tensor("F_ewT", [8, T], F32))
        mrep = e(nc.sbuf_tensor("F_mrep", [128, T], BF16))
        h_e = e(nc.sbuf_tensor("F_h_e", [128, 4 * T], BF16))
        yacc = e(nc.sbuf_tensor("F_yacc", [128, 8 * T], F32))
        u_f = e(nc.sbuf_tensor("F_u_f", [128, 512], F32))
        sq_f = e(nc.sbuf_tensor("F_sq_f", [128, 512], F32))
        th_f = e(nc.sbuf_tensor("F_th_f", [128, 512], F32))
        gtile = e(nc.sbuf_tensor("F_gtile", [128, 8 * 512], BF16))
        s1c = e(nc.sbuf_tensor("F_s1c", [128, 8 * 512], BF16))
        x1c = e(nc.sbuf_tensor("F_x1c", [128, 8 * 512], F32))
        outtok = e(nc.sbuf_tensor("F_outtok", [128, D], F32))
        ps8 = [e(nc.psum_tensor(f"F_ps8_{i}", [128, 512], F32)) for i in range(8)]

        b.dma("sync", eye[:], P["eye_in"], writes=("eye",))
        b.dma("sync", ones[:], P["ones_in"], writes=("ones",))
        b.dma("sync", sel8[:], P["sel8_in"], writes=("sel8",))
        b.dma("gpsimd", projP[:], P["projP_in"], writes=("projP",))
        b.dma("sync", anchT[:], P["anchT_in"], writes=("anchT",))
        b.dma("sync", cscrep[0:1, 0:1], P["temp_in"], writes=("csc0",))
        b.op("vector", lambda E: E.memset(onecol[:], 1.0), writes=("onecol",))
        b.op("vector", lambda E: E.tensor_scalar(
            cscrep[0:1, 0:1], cscrep[0:1, 0:1], 0.1, None, ALU.max),
            reads=("csc0",), writes=("csc0",))
        b.op("vector", lambda E: E.reciprocal(cscrep[0:1, 0:1], cscrep[0:1, 0:1]),
             reads=("csc0",), writes=("csc0",))
        b.op("vector", lambda E: E.tensor_scalar(
            cscrep[0:1, 0:1], cscrep[0:1, 0:1], 0.5, None, ALU.mult),
            reads=("csc0",), writes=("csc0",))
        b.op("tensor", lambda E: E.matmul(ps8[0][:, 0:1], ones[0:1, :],
                                          cscrep[0:1, 0:1], start=True, stop=True),
             reads=("ones", "csc0"), writes=("ps0",))
        b.op("scalar", lambda E: E.activation(cscrep[:], ps8[0][:, 0:1], AF.Copy),
             reads=("ps0",), writes=("cscrep",))
        b.dma("sync", x1n[:].rearrange("p (g t) -> p g t", g=8),
              P["x1n_hbm"][:, :, :], writes=("x1n",))

        for i in range(NT):
            for k in range(8):
                b.op("tensor", lambda E, i=i, k=k: E.matmul(
                    ps8[0][:, 0:6], x1n[:, k * T + i * 128:k * T + (i + 1) * 128],
                    projP[:, k * 6:(k + 1) * 6],
                    start=(k == 0), stop=(k == 7)),
                    reads=("x1n", "projP"),
                    writes=("ps0",) if k in (0, 7) else (),
                    inc=1 if k in (0, 7) else 0)
            b.op("scalar", lambda E: E.activation(
                qtok[:, 0:6], ps8[0][:, 0:6], AF.Tanh, scale=float(1.0 / 0.3)),
                reads=("ps0",), writes=("qtok",))
            b.op("tensor", lambda E: E.matmul(ps8[1][0:6, 0:128], qtok[:, 0:6],
                                              eye[:], start=True, stop=True),
                 reads=("qtok", "eye"), writes=("ps1",))
            b.op("scalar", lambda E: E.activation(
                u_f[0:6, 0:128], ps8[1][0:6, 0:128], AF.Copy),
                reads=("ps1",), writes=("u_f",))
            b.op("tensor", lambda E: E.matmul(ps8[2][:, 0:6], u_f[0:6, 0:128],
                                              anchT[:], start=True, stop=True),
                 reads=("u_f", "anchT"), writes=("ps2",))
            b.op("vector", lambda E: E.tensor_copy(dott[:, 0:6], ps8[2][:, 0:6]),
                 reads=("ps2",), writes=("dott",))

            def top1(dst, dstk, srck, src, gcol):
                b.op("vector", lambda E: E.reduce_max(
                    stat[:, gcol:gcol + 1], src[:, 0:6], axis=AX.X),
                    reads=(srck,), writes=("stat",))
                b.op("vector", lambda E: E.tensor_scalar(
                    dst[:, 0:6], src[:, 0:6], stat[:, gcol:gcol + 1], None,
                    ALU.is_equal),
                    reads=(srck, "stat"), writes=(dstk,))
                b.op("vector", lambda E: E.tensor_tensor_scan(
                    scr[:, 0:6], onecol[:, 0:6], dst[:, 0:6], 0.0,
                    ALU.mult, ALU.add),
                    reads=(dstk, "onecol"), writes=("scr",))
                b.op("vector", lambda E: E.tensor_scalar(
                    scr[:, 0:6], scr[:, 0:6], 1.0, None, ALU.is_equal),
                    reads=("scr",), writes=("scr",))
                b.op("vector", lambda E: E.tensor_mul(
                    dst[:, 0:6], dst[:, 0:6], scr[:, 0:6]),
                    reads=(dstk, "scr"), writes=(dstk,))

            top1(o1, "o1", "dott", dott, 0)
            b.op("vector", lambda E: E.scalar_tensor_tensor(
                qtok[:, 0:6], o1[:, 0:6], -1e30, dott[:, 0:6], ALU.mult, ALU.add),
                reads=("o1", "dott"), writes=("qtok",))
            top1(o2, "o2", "qtok", qtok, 1)
            b.op("vector", lambda E: E.tensor_sub(
                stat[:, 2:3], stat[:, 0:1], stat[:, 1:2]),
                reads=("stat",), writes=("stat",))
            b.op("vector", lambda E: E.tensor_mul(
                stat[:, 2:3], stat[:, 2:3], cscrep[:, 0:1]),
                reads=("stat", "cscrep"), writes=("stat",))
            b.op("scalar", lambda E: E.activation(
                stat[:, 3:4], stat[:, 2:3], AF.Sigmoid),
                reads=("stat",), writes=("stat",))
            b.op("vector", lambda E: E.tensor_scalar(
                stat[:, 4:5], stat[:, 3:4], -1.0, 1.0, ALU.mult, ALU.add),
                reads=("stat",), writes=("stat",))
            b.op("vector", lambda E: E.tensor_scalar(
                ewm[:, 0:6], o1[:, 0:6], stat[:, 3:4], None, ALU.mult),
                reads=("o1", "stat"), writes=("ewm",))
            b.op("vector", lambda E: E.tensor_scalar(
                scr[:, 0:6], o2[:, 0:6], stat[:, 4:5], None, ALU.mult),
                reads=("o2", "stat"), writes=("scr",))
            b.op("vector", lambda E: E.tensor_add(
                ewm[:, 0:6], ewm[:, 0:6], scr[:, 0:6]),
                reads=("ewm", "scr"), writes=("ewm",))
            b.op("vector", lambda E: E.memset(ewm[:, 6:8], 0.0), writes=("ewm",))
            b.op("tensor", lambda E: E.matmul(ps8[3][0:8, 0:128], ewm[:],
                                              eye[:], start=True, stop=True),
                 reads=("ewm", "eye"), writes=("ps3",))
            b.op("scalar", lambda E, i=i: E.activation(
                ewT[:, i * 128:(i + 1) * 128], ps8[3][0:8, 0:128], AF.Copy),
                reads=("ps3",), writes=("ewT",))

        b.op("vector", lambda E: E.memset(yacc[:], 0.0), writes=("yacc",))
        for ex in range(E1):
            for nt in range(4):
                b.op("tensor", lambda E, ex=ex, nt=nt: E.matmul(
                    ps8[4][:], sel8[0:6, ex * 128:(ex + 1) * 128],
                    ewT[0:6, nt * 512:(nt + 1) * 512], start=True, stop=True),
                    reads=("sel8", "ewT"), writes=("ps4",))
                b.op("scalar", lambda E, nt=nt: E.activation(
                    mrep[:, nt * 512:(nt + 1) * 512], ps8[4][:], AF.Copy),
                    reads=("ps4",), writes=("mrep",))
            for k in range(8):
                b.dma("gpsimd", wp[k // 4][:, (k % 4) * 512:(k % 4 + 1) * 512],
                      P["w1f_in"][ex, k * 128:(k + 1) * 128, :],
                      writes=(f"wp{k // 4}",))
            for ogl in range(4):
                for nt in range(4):
                    for k in range(8):
                        b.op("tensor", lambda E, ogl=ogl, nt=nt, k=k: E.matmul(
                            ps8[0][:],
                            wp[k // 4][:, (k % 4) * 512 + ogl * 128:
                                       (k % 4) * 512 + (ogl + 1) * 128],
                            x1n[:, k * T + nt * 512:k * T + (nt + 1) * 512],
                            start=(k == 0), stop=(k == 7)),
                            reads=(f"wp{k // 4}", "x1n"),
                            writes=("ps0",) if k in (0, 7) else (),
                            inc=1 if k in (0, 7) else 0)
                    b.op("scalar", lambda E: E.activation(u_f[:], ps8[0][:], AF.Copy),
                         reads=("ps0",), writes=("u_f",))
                    b.op("scalar", lambda E: E.activation(sq_f[:], u_f[:], AF.Square),
                         reads=("u_f",), writes=("sq_f",))
                    b.op("vector", lambda E: E.tensor_mul(sq_f[:], sq_f[:], u_f[:]),
                         reads=("sq_f", "u_f"), writes=("sq_f",))
                    b.op("vector", lambda E: E.scalar_tensor_tensor(
                        sq_f[:], sq_f[:], 0.044715, u_f[:], ALU.mult, ALU.add),
                        reads=("sq_f", "u_f"), writes=("sq_f",))
                    b.op("scalar", lambda E: E.activation(
                        th_f[:], sq_f[:], AF.Tanh, scale=0.7978845608028654),
                        reads=("sq_f",), writes=("th_f",))
                    b.op("vector", lambda E: E.tensor_scalar(
                        th_f[:], th_f[:], 1.0, 0.5, ALU.add, ALU.mult),
                        reads=("th_f",), writes=("th_f",))
                    b.op("vector", lambda E: E.tensor_mul(th_f[:], th_f[:], u_f[:]),
                         reads=("th_f", "u_f"), writes=("th_f",))
                    b.op("vector", lambda E, ogl=ogl, nt=nt: E.tensor_mul(
                        h_e[:, ogl * T + nt * 512:ogl * T + (nt + 1) * 512],
                        th_f[:], mrep[:, nt * 512:(nt + 1) * 512]),
                        reads=("th_f", "mrep"), writes=("h_e",))
            for k in range(4):
                b.dma("gpsimd", wp[4 + k // 2][:, (k % 2) * 1024:(k % 2 + 1) * 1024],
                      P["w2f_in"][ex, k * 128:(k + 1) * 128, :],
                      writes=(f"wp{4 + k // 2}",))
            for nt in range(4):
                for k in range(4):
                    for og in range(8):
                        b.op("tensor", lambda E, k=k, og=og, nt=nt: E.matmul(
                            ps8[og][:],
                            wp[4 + k // 2][:, (k % 2) * 1024 + og * 128:
                                           (k % 2) * 1024 + (og + 1) * 128],
                            h_e[:, k * T + nt * 512:k * T + (nt + 1) * 512],
                            start=(k == 0), stop=(k == 3)),
                            reads=(f"wp{4 + k // 2}", "h_e"),
                            writes=(f"ps{og}",) if k in (0, 3) else (),
                            inc=1 if k in (0, 3) else 0)
                for og in range(8):
                    b.op("vector", lambda E, og=og, nt=nt: E.scalar_tensor_tensor(
                        yacc[:, og * T + nt * 512:og * T + (nt + 1) * 512],
                        ps8[og][:], 1.0,
                        yacc[:, og * T + nt * 512:og * T + (nt + 1) * 512],
                        ALU.mult, ALU.add),
                        reads=(f"ps{og}", "yacc"), writes=("yacc",))
        for nt in range(4):
            c0 = nt * 512
            for og in range(8):
                b.op("scalar", lambda E, og=og, nt=nt: E.activation(
                    gtile[:, og * 512:(og + 1) * 512],
                    yacc[:, og * T + nt * 512:og * T + (nt + 1) * 512], AF.Silu),
                    reads=("yacc",), writes=("gtile",))
            b.dma("sync", s1c[:].rearrange("p (g t) -> p g t", g=8),
                  P["s1_hbm"][:, :, c0:c0 + 512], writes=("s1c",))
            b.dma("sync", x1c[:].rearrange("p (g t) -> p g t", g=8),
                  P["x1_hbm"][:, :, c0:c0 + 512], writes=("x1c",))
            for og in range(8):
                b.op("vector", lambda E, og=og: E.tensor_mul(
                    gtile[:, og * 512:(og + 1) * 512],
                    gtile[:, og * 512:(og + 1) * 512],
                    s1c[:, og * 512:(og + 1) * 512]),
                    reads=("gtile", "s1c"), writes=("gtile",))
                b.op("vector", lambda E, og=og: E.tensor_add(
                    x1c[:, og * 512:(og + 1) * 512],
                    x1c[:, og * 512:(og + 1) * 512],
                    gtile[:, og * 512:(og + 1) * 512]),
                    reads=("gtile", "x1c"), writes=("x1c",))
            for tt in range(4):
                for jj in (0, 4):
                    for j in range(jj, jj + 4):
                        b.op("tensor", lambda E, j=j, jj=jj, tt=tt: E.transpose(
                            ps8[1][:, (j - jj) * 128:(j - jj) * 128 + 128],
                            x1c[:, j * 512 + tt * 128:j * 512 + (tt + 1) * 128],
                            eye[:]),
                            reads=("x1c", "eye"), writes=("ps1",))
                    for j in range(jj, jj + 4):
                        b.op("scalar", lambda E, j=j, jj=jj: E.activation(
                            outtok[:, j * 128:(j + 1) * 128],
                            ps8[1][:, (j - jj) * 128:(j - jj) * 128 + 128], AF.Copy),
                            reads=("ps1",), writes=("outtok",))
                b.dma("sync", P["out_hbm"][c0 + tt * 128:c0 + (tt + 1) * 128, :],
                      outtok[:], reads=("outtok",))




def emit_counts_mk(b, nc, P, st):
    NTT = 4
    if True:
        e = st.enter_context
        xTc_ = [e(nc.sbuf_tensor(f"N_xTc{i}", [128, 8 * 512], F32)) for i in range(4)]
        wr2 = e(nc.sbuf_tensor("N_wr2", [128, E0 * 8], F32))
        onesN = e(nc.sbuf_tensor("N_ones", [128, 128], F32))
        srow_ = [e(nc.sbuf_tensor(f"N_srow{i}", [1, 4 * 512], F32)) for i in range(4)]
        Srep = e(nc.sbuf_tensor("N_Srep", [128, E0], F32))
        brrep = e(nc.sbuf_tensor("N_brrep", [128, E0], F32))
        Srow = e(nc.sbuf_tensor("N_Srow", [1, E0], F32))
        mucol_ = [e(nc.sbuf_tensor(f"N_mucol{i}", [128, 2], F32)) for i in range(4)]
        ltok_ = [e(nc.sbuf_tensor(f"N_ltok{i}", [128, E0], F32)) for i in range(4)]
        o1_ = [e(nc.sbuf_tensor(f"N_o1{i}", [128, E0], F32)) for i in range(4)]
        o2_ = [e(nc.sbuf_tensor(f"N_o2{i}", [128, E0], F32)) for i in range(4)]
        scr_ = [e(nc.sbuf_tensor(f"N_scr{i}", [128, E0], F32)) for i in range(4)]
        stat_ = [e(nc.sbuf_tensor(f"N_stat{i}", [128, 8], F32)) for i in range(4)]
        onecol = e(nc.sbuf_tensor("N_onecol", [128, E0], F32))
        pm64 = e(nc.sbuf_tensor("N_pm64", [1, 64], F32))
        acnt = e(nc.sbuf_tensor("N_acnt", [1, 64], F32))
        offsr = e(nc.sbuf_tensor("N_offsr", [1, E0], F32))
        cval = e(nc.sbuf_tensor("N_cval", [128, 1], F32))
        ps_sqx_ = [e(nc.psum_tensor(f"N_ps_sq{i}", [64, 512], F32)) for i in range(2)]
        ps_l2x_ = [e(nc.psum_tensor(f"N_ps_l2{i}", [128, 32], F32)) for i in range(2)]
        ps_mx_ = [e(nc.psum_tensor(f"N_ps_misc{i}", [128, 32], F32)) for i in range(2)]
        ps_b = ps_mx_[0][:, 16:24]

        b.dma("sync", wr2[:], P["wr_in"], writes=("Nwr",))
        b.dma("sync", onesN[:], P["ones_in"], writes=("Nones",))
        b.dma("sync", pm64[:], P["pm64_in"], writes=("Npm",))
        b.dma("sync", brrep[0:1, :], P["br_in"].rearrange("(o e) -> o e", o=1),
              writes=("Nbr",))
        b.op("vector", lambda E: E.memset(acnt[:], 0.0), writes=("Nacnt",))
        b.op("vector", lambda E: E.memset(onecol[:], 1.0), writes=("Nonecol",))
        b.op("vector", lambda E: E.memset(cval[:], 1.0), writes=("Ncval",))
        b.op("tensor", lambda E: E.matmul(ps_sqx_[0][0:1, 0:64], onesN[:, 0:1], wr2[:],
                                          start=True, stop=True),
             reads=("Nones", "Nwr"), writes=("Nps_sq0",))
        b.op("vector", lambda E: E.tensor_copy(srow_[0][:, 0:64], ps_sqx_[0][0:1, 0:64]),
             reads=("Nps_sq0",), writes=("Nsrow0",))
        b.op("vector", lambda E: E.tensor_reduce(
            Srow[:], srow_[0][0:1, 0:64].rearrange("p (k e) -> p e k", k=8),
            axis=AX.X, op=ALU.add),
            reads=("Nsrow0",), writes=("NSrow",))
        b.op("tensor", lambda E: E.matmul(ps_b[:], onesN[0:1, :], Srow[:],
                                          start=True, stop=True),
             reads=("Nones", "NSrow"), writes=("Nps_misc0",))
        b.op("scalar", lambda E: E.activation(Srep[:], ps_b[:], AF.Copy),
             reads=("Nps_misc0",), writes=("NSrep",))
        b.op("tensor", lambda E: E.matmul(ps_b[:], onesN[0:1, :], brrep[0:1, :],
                                          start=True, stop=True),
             reads=("Nones", "Nbr"), writes=("Nps_misc0",))
        b.op("scalar", lambda E: E.activation(brrep[:], ps_b[:], AF.Copy),
             reads=("Nps_misc0",), writes=("Nbrrep",))

        def top1(dst, dstk, src_t, srck, gcol, stat, scr, Z):
            b.op("vector", lambda E: E.reduce_max(
                stat[:, gcol:gcol + 1], src_t[:], axis=AX.X),
                reads=(srck,), writes=("Nstat" + Z,))
            b.op("vector", lambda E: E.tensor_scalar(
                dst[:], src_t[:], stat[:, gcol:gcol + 1], None, ALU.is_equal),
                reads=(srck, "Nstat" + Z), writes=(dstk,))
            b.op("vector", lambda E: E.tensor_tensor_scan(
                scr[:], onecol[:], dst[:], 0.0, ALU.mult, ALU.add),
                reads=(dstk, "Nonecol"), writes=("Nscr" + Z,))
            b.op("vector", lambda E: E.tensor_scalar(
                scr[:], scr[:], 1.0, None, ALU.is_equal),
                reads=("Nscr" + Z,), writes=("Nscr" + Z,))
            b.op("vector", lambda E: E.tensor_mul(dst[:], dst[:], scr[:]),
                 reads=(dstk, "Nscr" + Z), writes=(dstk,))

        def _c_s0(bo, ci, c0, xTc, srow, mucol, ltok, o1, o2, scr, stat,
                  ps_s, ps_q, ps_l2, ps_m, ps_c, Z, Z2):
            if True:
                b.dma("sync", xTc[:].rearrange("p (g t) -> p g t", g=8),
                      P["xTf_in"][bo].rearrange("(g p) t -> p g t", p=128)[:, :, c0:c0 + 512],
                      writes=("NxTc" + Z,))
                for g in range(8):
                    b.op("tensor", lambda E, g=g: E.matmul(
                        ps_s[:], onesN[:, 0:1], xTc[:, g * 512:(g + 1) * 512],
                        start=(g == 0), stop=(g == 7)),
                        reads=("Nones", "NxTc" + Z),
                        writes=("Nps_sq" + Z2,) if g in (0, 7) else (),
                        inc=1 if g in (0, 7) else 0)

        def _c_s1(bo, ci, c0, xTc, srow, mucol, ltok, o1, o2, scr, stat,
                  ps_s, ps_q, ps_l2, ps_m, ps_c, Z, Z2):
            if True:
                for tt in range(NTT):
                    for k in range(8):
                        b.op("tensor", lambda E, tt=tt, k=k: E.matmul(
                            ps_l2[tt][:], xTc[:, k * 512 + tt * 128:k * 512 + (tt + 1) * 128],
                            wr2[:, k * 8:(k + 1) * 8],
                            start=(k == 0), stop=(k == 7)),
                            reads=("NxTc" + Z, "Nwr"),
                            writes=("Nps_l2" + Z2,) if k in (0, 7) else (),
                            inc=1 if k in (0, 7) else 0)

        def _c_s2(bo, ci, c0, xTc, srow, mucol, ltok, o1, o2, scr, stat,
                  ps_s, ps_q, ps_l2, ps_m, ps_c, Z, Z2):
            if True:
                b.op("scalar", lambda E: E.activation(xTc[:], xTc[:], AF.Square),
                     reads=("NxTc" + Z,), writes=("NxTc" + Z,))
                for g in range(8):
                    b.op("tensor", lambda E, g=g: E.matmul(
                        ps_q[:], onesN[:, 0:1], xTc[:, g * 512:(g + 1) * 512],
                        start=(g == 0), stop=(g == 7)),
                        reads=("Nones", "NxTc" + Z),
                        writes=("Nps_sq" + Z2,) if g in (0, 7) else (),
                        inc=1 if g in (0, 7) else 0)

        def _c_s3(bo, ci, c0, xTc, srow, mucol, ltok, o1, o2, scr, stat,
                  ps_s, ps_q, ps_l2, ps_m, ps_c, Z, Z2):
            if True:
                b.op("scalar", lambda E: E.activation(
                    srow[:, 0:512], ps_s[:], AF.Copy, scale=1.0 / D),
                    reads=("Nps_sq" + Z2,), writes=("Nsrow" + Z,))
                b.op("scalar", lambda E: E.activation(
                    srow[:, 512:1024], ps_q[:], AF.Copy, scale=1.0 / D),
                    reads=("Nps_sq" + Z2,), writes=("Nsrow" + Z,))
                b.op("vector", lambda E: E.tensor_mul(
                    srow[:, 1024:1536], srow[:, 0:512], srow[:, 0:512]),
                    reads=("Nsrow" + Z,), writes=("Nsrow" + Z,))
                b.op("vector", lambda E: E.tensor_sub(
                    srow[:, 1024:1536], srow[:, 512:1024], srow[:, 1024:1536]),
                    reads=("Nsrow" + Z,), writes=("Nsrow" + Z,))
                b.op("scalar", lambda E: E.activation(
                    srow[:, 1536:2048], srow[:, 1024:1536], AF.Ln,
                    bias=P["cval_eps"]),
                    reads=("Nsrow" + Z,), writes=("Nsrow" + Z,))
                b.op("scalar", lambda E: E.activation(
                    srow[:, 1536:2048], srow[:, 1536:2048], AF.Exp, scale=-0.5),
                    reads=("Nsrow" + Z,), writes=("Nsrow" + Z,))

        def _c_s4(bo, ci, c0, xTc, srow, mucol, ltok, o1, o2, scr, stat,
                  ps_s, ps_q, ps_l2, ps_m, ps_c, Z, Z2):
            if True:
                for tt in range(NTT):
                    b.op("tensor", lambda E, tt=tt: E.matmul(
                        ps_m[:, 0:1], srow[0:1, tt * 128:(tt + 1) * 128],
                        cval[0:1, 0:1], start=True, stop=False),
                        reads=("Nsrow" + Z, "Ncval"), writes=("Nps_misc" + Z2,))
                    b.op("tensor", lambda E, tt=tt: E.matmul(
                        ps_m[:, 1:2], srow[0:1, 1536 + tt * 128:1536 + (tt + 1) * 128],
                        cval[0:1, 0:1], start=False, stop=True),
                        reads=("Nsrow" + Z, "Ncval"), writes=("Nps_misc" + Z2,))
                    b.op("scalar", lambda E: E.activation(mucol[:], ps_m[:], AF.Copy),
                         reads=("Nps_misc" + Z2,), writes=("Nmucol" + Z,))
                    b.op("vector", lambda E, tt=tt: E.scalar_tensor_tensor(
                        ltok[:], Srep[:], mucol[:, 0:1], ps_l2[tt][:],
                        ALU.mult, ALU.subtract),
                        reads=("NSrep", "Nmucol" + Z, "Nps_l2" + Z2), writes=("Nltok" + Z,))
                    b.op("vector", lambda E: E.tensor_scalar(
                        ltok[:], ltok[:], mucol[:, 1:2], -1.0, ALU.mult, ALU.mult),
                        reads=("Nltok" + Z, "Nmucol" + Z), writes=("Nltok" + Z,))
                    b.op("vector", lambda E: E.tensor_add(ltok[:], ltok[:], brrep[:]),
                         reads=("Nltok" + Z, "Nbrrep"), writes=("Nltok" + Z,))
                    top1(o1, "No1" + Z, ltok, "Nltok" + Z, 4, stat, scr, Z)
                    b.op("vector", lambda E: E.scalar_tensor_tensor(
                        ltok[:], o1[:], -1e30, ltok[:], ALU.mult, ALU.add),
                        reads=("No1" + Z, "Nltok" + Z), writes=("Nltok" + Z,))
                    top1(o2, "No2" + Z, ltok, "Nltok" + Z, 5, stat, scr, Z)
                    b.op("vector", lambda E: E.tensor_add(o1[:], o1[:], o2[:]),
                         reads=("No1" + Z, "No2" + Z), writes=("No1" + Z,))
                    b.op("tensor", lambda E: E.matmul(
                        ps_c[:], onesN[:, 0:1], o1[:], start=True, stop=True),
                        reads=("Nones", "No1" + Z), writes=("Nps_misc" + Z2,))
                    b.op("vector", lambda E, bo=bo: E.tensor_add(
                        acnt[0:1, bo * 8:(bo + 1) * 8],
                        acnt[0:1, bo * 8:(bo + 1) * 8], ps_c[:]),
                        reads=("Nps_misc" + Z2, "Nacnt"), writes=("Nacnt",))

        def cargs(bo, ci):
            pz = (bo * 4 + ci) % 4
            pz2 = (bo * 4 + ci) % 2
            return (bo, ci, ci * 512, xTc_[pz], srow_[pz], mucol_[pz],
                    ltok_[pz], o1_[pz], o2_[pz], scr_[pz], stat_[pz],
                    ps_sqx_[pz2][0:1, 0:512], ps_sqx_[pz2][32:33, 0:512],
                    [ps_l2x_[pz2][:, i * 8:(i + 1) * 8] for i in range(4)],
                    ps_mx_[pz2][:, 0:2], ps_mx_[pz2][0:1, 8:16],
                    f"{pz}", f"{pz2}")

        def do_bo(bo):
            for base in (0, 2):
                arglist = [cargs(bo, base), cargs(bo, base + 1)]
                for st_fn in (_c_s0, _c_s1, _c_s2, _c_s3, _c_s4):
                    for a in arglist:
                        st_fn(*a)

        def finale():
            b.op("vector", lambda E: E.tensor_mul(acnt[:], acnt[:], pm64[:]),
                 reads=("Nacnt", "Npm"), writes=("Nacnt",))
            b.op("vector", lambda E: E.tensor_reduce(
                offsr[:], acnt[0:1, 0:64].rearrange("p (k e) -> p e k", k=8),
                axis=AX.X, op=ALU.add),
                reads=("Nacnt",), writes=("Noffsr",))
            b.op("tensor", lambda E: E.matmul(ps_b[:], onesN[0:1, :], offsr[:],
                                              start=True, stop=True),
                 reads=("Nones", "Noffsr"), writes=("Nps_misc0",))
            b.op("scalar", lambda E: E.activation(P["offs_rep"][:], ps_b[:], AF.Copy),
                 reads=("Nps_misc0",), writes=("offs_rep",))


        return do_bo, finale

def build_fused():
    nc = bass.Bass("TRN2", target_bir_lowering=False, debug=False)
    ap = lambda name, shape, dt_, kind: nc.dram_tensor(name, shape, dt_, kind=kind).ap()
    P = {}
    P["x_in"] = ap("x", [T, D], F32, "ExternalInput")
    P["wr_in"] = ap("w_router", [128, E0 * 8], F32, "ExternalInput")
    P["br_in"] = ap("b_router", [E0], F32, "ExternalInput")
    P["eye_in"] = ap("eye", [128, 128], F32, "ExternalInput")
    P["triu_in"] = ap("triu", [128, 128], F32, "ExternalInput")
    P["ones_in"] = ap("ones", [128, 128], F32, "ExternalInput")
    P["wip_in"] = ap("wipT", [D, 4096], F32, "ExternalInput")
    P["wxp_in"] = ap("wxpP", [128, 16 * 96], F32, "ExternalInput")
    P["wdt_in"] = ap("wdtT", [64, DI], F32, "ExternalInput")
    P["wop_in"] = ap("wopT", [DI, D], F32, "ExternalInput")
    P["wglu_in"] = ap("wgluT", [D, DI], F32, "ExternalInput")
    P["conv_in"] = ap("convP", [128, 64], F32, "ExternalInput")
    P["convb_in"] = ap("convbP", [128, 16], F32, "ExternalInput")
    P["dtb_in"] = ap("dtbP", [128, 16], F32, "ExternalInput")
    P["alog_in"] = ap("alogP", [128, 256], F32, "ExternalInput")
    P["dv_in"] = ap("DP", [128, 16], F32, "ExternalInput")
    P["glub_in"] = ap("glubP", [128, 16], F32, "ExternalInput")
    P["lng_in"] = ap("lngP", [128, 8], F32, "ExternalInput")
    P["lnb_in"] = ap("lnbP", [128, 8], F32, "ExternalInput")
    P["sel_in"] = ap("sel16", [16, 16 * 128], F32, "ExternalInput")
    P["pm64_in"] = ap("pm64", [1, 64], F32, "ExternalInput")
    P["xTf_in"] = ap("xTf", [NCORE, D, T], F32, "ExternalInput")
    P["w1_in"] = ap("w1", [E0, D, 4096], F32, "ExternalInput")
    P["b1_in"] = ap("b1P", [E0, 128, 32], F32, "ExternalInput")
    P["w2_in"] = ap("w2", [E0, 4096, D], F32, "ExternalInput")
    P["b2_in"] = ap("b2", [E0, D], F32, "ExternalInput")
    P["sel8_in"] = ap("sel8", [E0, E0 * 128], F32, "ExternalInput")
    P["qkw_in"] = ap("qk_w", [D, 2 * D], F32, "ExternalInput")
    P["qkb_in"] = ap("qk_b", [2 * D], F32, "ExternalInput")
    P["lepe_in"] = ap("lepe_w", [3, D, D], F32, "ExternalInput")
    P["lepeb_in"] = ap("lepebP", [128, 8], F32, "ExternalInput")
    P["projP_in"] = ap("projP", [128, 48], F32, "ExternalInput")
    P["anchT_in"] = ap("anchT", [6, 6], F32, "ExternalInput")
    P["temp_in"] = ap("rtemp", [1, 1], F32, "ExternalInput")
    P["w1f_in"] = ap("w1f", [E1, D, DE1], F32, "ExternalInput")
    P["w2f_in"] = ap("w2f", [E1, DE1, D], F32, "ExternalInput")
    xnb_t = nc.dram_tensor("i_xnT_b", [D, T], BF16, kind="Internal")
    P["xnTb_hbm"] = xnb_t.ap().rearrange("(g p) t -> p g t", p=128)
    sT_t = nc.dram_tensor("i_sT", [D, T], BF16, kind="Internal")
    P["sT_hbm"] = sT_t.ap().rearrange("(g p) t -> p g t", p=128)
    gT_t = nc.dram_tensor("i_gT", [D, T], BF16, kind="Internal")
    P["gT_ap"] = gT_t.ap()
    P["gT_hbm"] = gT_t.ap().rearrange("(g p) t -> p g t", p=128)
    x1_t = nc.dram_tensor("i_x1T", [D, T], F32, kind="Internal")
    P["x1_hbm"] = x1_t.ap().rearrange("(g p) t -> p g t", p=128)
    x1n_t = nc.dram_tensor("i_x1nT_b", [D, T], BF16, kind="Internal")
    P["x1n_hbm"] = x1n_t.ap().rearrange("(g p) t -> p g t", p=128)
    s1_t = nc.dram_tensor("i_s1T", [D, T], BF16, kind="Internal")
    P["s1_hbm"] = s1_t.ap().rearrange("(g p) t -> p g t", p=128)
    P["out_hbm"] = ap("out", [T, D], F32, "ExternalOutput")

    b = B(nc)
    with contextlib.ExitStack() as top:
        route_sb = top.enter_context(nc.sbuf_tensor("route_sb", [128, 16 * 20], F32))
        P["route_sb"] = route_sb
        P["offs_rep"] = top.enter_context(nc.sbuf_tensor("offs_rep", [128, E0], F32))
        cve = top.enter_context(nc.sbuf_tensor("cval_eps", [1, 1], F32))
        P["cval_eps"] = cve[0:1, 0:1]
        b.op("vector", lambda E: E.memset(cve[:], 1e-5), writes=("cval_eps",))
        emit_A(b, nc, P)
        b.barrier()
        with contextlib.ExitStack() as cstk:
            do_bo, cnt_finale = emit_counts_mk(b, nc, P, cstk)
            for bo in range(NCORE):
                do_bo(bo)
            cnt_finale()
        b.barrier()
        with contextlib.ExitStack() as bstk:
            do_chunk = emit_B_mk(b, nc, P, bstk)
            for ci in range(4):
                do_chunk(ci)
        b.barrier()
        emit_C(b, nc, P)
        b.barrier()
        emit_D(b, nc, P)
        b.barrier()
        emit_E(b, nc, P)
        b.barrier()
        emit_F(b, nc, P)
        b.finish_into(top)
    return nc


def kernel(**inputs):
    cst = host_prep(inputs)
    x_full = np.ascontiguousarray(inputs["L0_attn_x"])
    ncX = build_fused()
    in_maps = []
    for core in range(NCORE):
        pm64 = np.zeros((1, 64), np.float32)
        pm64[0, :core * 8] = 1.0
        m = {"x": np.ascontiguousarray(x_full[core]),
             "w_router": cst["w_router_p"],
             "b_router": np.ascontiguousarray(inputs["L0_ffn_b_router"].astype(np.float32)),
             "eye": cst["eye"], "triu": cst["triu"], "ones": cst["ones"],
             "pm64": pm64, "xTf": cst["xTf"]}
        for k in ["wipT", "wxpP", "wdtT", "wopT", "wgluT", "convP", "convbP",
                  "dtbP", "alogP", "DP", "glubP", "lngP", "lnbP", "sel16",
                  "w1c", "b1P", "w2c", "b2c", "sel8", "qk_w", "qk_b", "lepe_w",
                  "lepebP", "projP", "anchT", "rtemp", "w1f", "w2f"]:
            name = {"w1c": "w1", "w2c": "w2", "b2c": "b2"}.get(k, k)
            m[name] = cst[k]
        in_maps.append(m)
    res = run_bass_kernel_spmd(ncX, in_maps, core_ids=list(range(NCORE)))
    out = np.stack([np.asarray(res.results[c]["out"]) for c in range(NCORE)], 0)
    return out.astype(np.float32)


# version 14
# speedup vs baseline: 1.0322x; 1.0322x over previous
import contextlib
import numpy as np
import ml_dtypes
import concourse.bass as bass
import concourse.mybir as mybir
from concourse.bass_utils import run_bass_kernel_spmd

F32 = mybir.dt.float32
BF16 = mybir.dt.bfloat16
I32 = mybir.dt.int32
AF = mybir.ActivationFunctionType
ALU = mybir.AluOpType
AX = mybir.AxisListType
BF = ml_dtypes.bfloat16

T = 2048
D = 1024
DI = 2048
NS = 16
NCORE = 8
CAP = 2560
E0 = 8
E1 = 6
DE1 = 512
NH = 8
HD = 128
TWOPI = 6.283185307179586
ENGS = ("sync", "tensor", "vector", "scalar", "gpsimd")


class B:

    def __init__(self, nc):
        self.nc = nc
        self.ops = {e: [] for e in ENGS}
        self.cnt = {e: 0 for e in ENGS}
        self.wm = {e: {o: 0 for o in ENGS} for e in ENGS}
        self.regions = {}
        self.pending_reads = {e: [] for e in ENGS}

    def _need(self, eng, waits, dep):
        deng, dcnt = dep
        if dcnt <= self.wm[eng][deng]:
            return
        waits.append((deng, dcnt))
        self.wm[eng][deng] = dcnt

    def op(self, eng, emit, reads=(), writes=(), inc=1):
        waits = []
        for r in reads:
            st = self.regions.get(r)
            if st and st["w"]:
                self._need(eng, waits, st["w"])
        for w in writes:
            st = self.regions.get(w)
            if st:
                if st["w"]:
                    self._need(eng, waits, st["w"])
                for reng, rcnt in st["r"].items():
                    self._need(eng, waits, (reng, rcnt))
        self.cnt[eng] += inc
        after = self.cnt[eng]
        self.ops[eng].append((waits, emit, inc))
        if inc:
            for r in list(self.pending_reads[eng]) + list(reads):
                st = self.regions.setdefault(r, {"w": None, "r": {}})
                st["r"][eng] = max(st["r"].get(eng, 0), after)
            self.pending_reads[eng] = []
            for w in writes:
                self.regions[w] = {"w": (eng, after), "r": {}}
        else:
            self.pending_reads[eng].extend(reads)
            assert not writes, "quiet ops must not declare writes"
        return after

    def barrier(self):
        snap = dict(self.cnt)
        for eng in ENGS:
            waits = []
            for o in ENGS:
                if o != eng and snap[o] > self.wm[eng][o]:
                    waits.append((o, snap[o]))
                    self.wm[eng][o] = snap[o]
            self.cnt[eng] += 1
            self.ops[eng].append((waits, lambda E: E.nop(), 1))

    def dma(self, eng, out, in_, reads=(), writes=(), **kw):
        return self.op(eng, lambda E: E.dma_start(out=out, in_=in_, **kw),
                       reads=reads, writes=writes, inc=16)

    def finish(self):
        nc = self.nc
        with contextlib.ExitStack() as st:
            sems = {e: st.enter_context(nc.semaphore(f"s_{e}")) for e in ENGS}
            st.enter_context(nc.Block()).__class__
        raise RuntimeError("use finish_into instead")

    def finish_into(self, stack):
        nc = self.nc
        sems = {e: stack.enter_context(nc.semaphore(f"s_{e}")) for e in ENGS}
        blk = stack.enter_context(nc.Block())
        final = {e: self.cnt[e] for e in ENGS}
        for eng in ENGS:
            ops = self.ops[eng]

            def section(E, eng=eng, ops=ops):
                for waits, emit, inc in ops:
                    for deng, dcnt in waits:
                        E.wait_ge(sems[deng], dcnt)
                    ins = emit(E)
                    if inc:
                        ins.then_inc(sems[eng], inc)
                if eng == "sync":
                    for o in ENGS:
                        if final[o]:
                            E.wait_ge(sems[o], final[o])

            getattr(blk, eng)(section)


def pack_rows(w):
    G = w.shape[0] // 128
    return np.ascontiguousarray(
        w.reshape(G, 128, w.shape[1]).transpose(1, 0, 2).reshape(128, -1))


def host_prep(inputs):
    f32c = lambda a: np.ascontiguousarray(a.astype(np.float32))
    c = {}
    c["eye"] = np.eye(128, dtype=np.float32)
    c["triu"] = np.triu(np.ones((128, 128), np.float32), 1)
    c["ones"] = np.ones((128, 128), np.float32)
    c["w_router_p"] = pack_rows(f32c(inputs["L0_ffn_w_router"]))
    c["wipT"] = f32c(inputs["L0_attn_in_proj_w"].T)
    c["wxpP"] = pack_rows(f32c(inputs["L0_attn_x_proj_w"].T))
    c["wdtT"] = f32c(inputs["L0_attn_dt_proj_w"].T)
    c["wopT"] = f32c(inputs["L0_attn_out_proj_w"].T)
    c["wgluT"] = f32c(inputs["L0_attn_glu_w"].T)
    c["convP"] = pack_rows(f32c(inputs["L0_attn_conv_w"][:, 0, :]))
    c["convbP"] = pack_rows(f32c(inputs["L0_attn_conv_b"][:, None]))
    c["dtbP"] = pack_rows(f32c(inputs["L0_attn_dt_proj_b"][:, None]))
    c["alogP"] = pack_rows(f32c(inputs["L0_attn_A_log"]))
    c["DP"] = pack_rows(f32c(inputs["L0_attn_D"][:, None]))
    c["glubP"] = pack_rows(f32c(inputs["L0_attn_glu_b"][:, None]))
    c["lngP"] = pack_rows(f32c(inputs["L0_attn_ln_g"][:, None]))
    c["lnbP"] = pack_rows(f32c(inputs["L0_attn_ln_b"][:, None]))
    sel16 = np.zeros((16, 16 * 128), np.float32)
    for n in range(16):
        sel16[n, n * 128:(n + 1) * 128] = 1.0
    c["sel16"] = sel16
    c["w1c"] = f32c(inputs["L0_ffn_w1"])
    c["w2c"] = f32c(inputs["L0_ffn_w2"])
    c["b1P"] = np.ascontiguousarray(
        inputs["L0_ffn_b1"].astype(np.float32).reshape(E0, 32, 128).transpose(0, 2, 1))
    c["b2c"] = f32c(inputs["L0_ffn_b2"])
    sel8 = np.zeros((E0, E0 * 128), np.float32)
    for n in range(E0):
        sel8[n, n * 128:(n + 1) * 128] = 1.0
    c["sel8"] = sel8
    c["qk_w"] = f32c(inputs["L1_attn_qk_w"])
    c["qk_b"] = f32c(inputs["L1_attn_qk_b"])
    c["lepe_w"] = f32c(inputs["L1_attn_lepe_w"])
    c["lepebP"] = pack_rows(f32c(inputs["L1_attn_lepe_b"][:, None]))
    c["projP"] = pack_rows(f32c(inputs["L1_ffn_proj_w"].T))
    c["anchT"] = f32c(inputs["L1_ffn_anchors"].T)
    c["rtemp"] = np.asarray(inputs["L1_ffn_routing_temp"], np.float32).reshape(1, 1)
    c["w1f"] = f32c(inputs["L1_ffn_w1"])
    c["w2f"] = f32c(inputs["L1_ffn_w2"])
    c["xTf"] = np.ascontiguousarray(
        inputs["L0_attn_x"].astype(np.float32).transpose(0, 2, 1))
    return c





def emit_A(b, nc, P):
    NT = T // 128
    NT = T // 128
    with contextlib.ExitStack() as st:
        e = st.enter_context
        xt = [e(nc.sbuf_tensor(f"A_xt{i}", [128, D], F32)) for i in range(2)]
        eye = e(nc.sbuf_tensor("A_sb_eye", [128, 128], F32))
        triu = e(nc.sbuf_tensor("A_sb_triu", [128, 128], F32))
        ones = e(nc.sbuf_tensor("A_sb_ones", [128, 128], F32))
        wr = e(nc.sbuf_tensor("A_sb_wr", [128, E0 * 8], F32))
        br_rep = e(nc.sbuf_tensor("A_br_rep", [128, E0], F32))
        stat = e(nc.sbuf_tensor("A_stat", [128, 8], F32))
        xsq = e(nc.sbuf_tensor("A_xsq", [128, D], F32))
        xnf = e(nc.sbuf_tensor("A_xnf", [128, D], F32))
        xnTf = e(nc.sbuf_tensor("A_xnTf", [128, D], F32))
        xnTb = e(nc.sbuf_tensor("A_xnTb", [128, D], BF16))
        gat = e(nc.sbuf_tensor("A_gat", [128, E0], F32))
        oh1 = e(nc.sbuf_tensor("A_oh1", [128, E0], F32))
        oh2 = e(nc.sbuf_tensor("A_oh2", [128, E0], F32))
        oht = e(nc.sbuf_tensor("A_oht", [128, E0], F32))
        posx = e(nc.sbuf_tensor("A_posx", [128, E0], F32))
        scr = e(nc.sbuf_tensor("A_scr", [128, E0], F32))
        scr2 = e(nc.sbuf_tensor("A_scr2", [128, E0], F32))
        rpack = e(nc.sbuf_tensor("A_rpack", [128, 20], F32))
        carry = e(nc.sbuf_tensor("A_carry", [1, E0], F32))
        onecol = e(nc.sbuf_tensor("A_onecol", [128, E0], F32))
        cvals = e(nc.sbuf_tensor("A_cvals", [128, 2], F32))
        ps_t = e(nc.psum_tensor("A_ps_t", [128, 512], F32))
        ps_l = e(nc.psum_tensor("A_ps_l", [128, E0], F32))
        ps_p = e(nc.psum_tensor("A_ps_p", [128, E0], F32))
        ps_c = e(nc.psum_tensor("A_ps_c", [1, E0], F32))
        ps_b = e(nc.psum_tensor("A_ps_b", [128, E0], F32))

        b.dma("sync", eye[:], P["eye_in"], writes=("eye",))
        b.dma("sync", triu[:], P["triu_in"], writes=("triu",))
        b.dma("sync", ones[:], P["ones_in"], writes=("ones",))
        b.dma("sync", wr[:], P["wr_in"], writes=("wr",))
        b.dma("sync", br_rep[0:1, :], P["br_in"].rearrange("(o e) -> o e", o=1), writes=("br",))
        b.op("tensor", lambda E: E.matmul(ps_b[:], ones[0:1, :], br_rep[0:1, :],
                                          start=True, stop=True),
             reads=("ones", "br"), writes=("ps_b",))
        b.op("vector", lambda E: E.tensor_copy(br_rep[:], ps_b[:]),
             reads=("ps_b",), writes=("br_rep",))
        b.op("vector", lambda E: E.memset(carry[:], 0.0), writes=("carry",))
        b.op("vector", lambda E: E.memset(onecol[:], 1.0), writes=("onecol",))
        b.op("vector", lambda E: E.memset(cvals[:, 0:1], 1e-5), writes=("cvals",))

        for i in range(NT):
            cur = xt[i % 2]
            xk = f"xt{i % 2}"
            b.dma("sync", cur[:], P["x_in"][i * 128:(i + 1) * 128, :], writes=(xk,))
            b.op("vector", lambda E, cur=cur: E.tensor_reduce(
                stat[:, 0:1], cur[:], axis=AX.X, op=ALU.add),
                reads=(xk,), writes=("stat",))
            b.op("scalar", lambda E, cur=cur: E.activation(
                xsq[:], cur[:], AF.Square), reads=(xk,), writes=("xsq",))
            b.op("vector", lambda E: E.tensor_reduce(
                stat[:, 1:2], xsq[:], axis=AX.X, op=ALU.add),
                reads=("xsq",), writes=("stat",))
            b.op("vector", lambda E: E.tensor_scalar(
                stat[:, 2:3], stat[:, 0:1], 1.0 / D, None, ALU.mult),
                reads=("stat",), writes=("stat",))
            b.op("vector", lambda E: E.tensor_scalar(
                stat[:, 3:4], stat[:, 1:2], 1.0 / D, None, ALU.mult),
                reads=("stat",), writes=("stat",))
            b.op("vector", lambda E: E.tensor_mul(
                stat[:, 4:5], stat[:, 2:3], stat[:, 2:3]),
                reads=("stat",), writes=("stat",))
            b.op("vector", lambda E: E.tensor_sub(
                stat[:, 4:5], stat[:, 3:4], stat[:, 4:5]),
                reads=("stat",), writes=("stat",))
            b.op("scalar", lambda E: E.activation(
                stat[:, 5:6], stat[:, 4:5], AF.Ln, bias=cvals[:, 0:1]),
                reads=("stat", "cvals"), writes=("stat",))
            b.op("scalar", lambda E: E.activation(
                stat[:, 5:6], stat[:, 5:6], AF.Exp, scale=-0.5),
                reads=("stat",), writes=("stat",))
            b.op("vector", lambda E: E.tensor_mul(
                stat[:, 6:7], stat[:, 2:3], stat[:, 5:6]),
                reads=("stat",), writes=("stat",))
            b.op("vector", lambda E: E.tensor_scalar(
                stat[:, 6:7], stat[:, 6:7], -1.0, None, ALU.mult),
                reads=("stat",), writes=("stat",))
            b.op("vector", lambda E, cur=cur: E.tensor_scalar(
                xnf[:], cur[:], stat[:, 5:6], stat[:, 6:7], ALU.mult, ALU.add),
                reads=(xk, "stat"), writes=("xnf",))
            for jj in (0, 4):
                for j in range(jj, jj + 4):
                    b.op("tensor", lambda E, j=j, jj=jj: E.transpose(
                        ps_t[:, (j - jj) * 128:(j - jj) * 128 + 128],
                        xnf[:, j * 128:(j + 1) * 128], eye[:]),
                        reads=("xnf", "eye"), writes=("ps_t",))
                for j in range(jj, jj + 4):
                    b.op("scalar", lambda E, j=j, jj=jj: E.activation(
                        xnTf[:, j * 128:(j + 1) * 128],
                        ps_t[:, (j - jj) * 128:(j - jj) * 128 + 128], AF.Copy),
                        reads=("ps_t",), writes=("xnTf",))
            b.op("vector", lambda E: E.tensor_copy(xnTb[:], xnTf[:]),
                 reads=("xnTf",), writes=("xnTb",))
            b.dma("sync", P["xnTb_hbm"][:, :, i * 128:(i + 1) * 128],
                  xnTb[:].rearrange("p (g t) -> p g t", g=8), reads=("xnTb",))
            for k in range(8):
                b.op("tensor", lambda E, k=k: E.matmul(
                    ps_l[:], xnTf[:, k * 128:(k + 1) * 128],
                    wr[:, k * 8:(k + 1) * 8],
                    start=(k == 0), stop=(k == 7)),
                    reads=("xnTf", "wr"),
                    writes=("ps_l",) if k in (0, 7) else (),
                    inc=1 if k in (0, 7) else 0)
            b.op("vector", lambda E: E.tensor_add(gat[:], ps_l[:], br_rep[:]),
                 reads=("ps_l", "br_rep"), writes=("gat",))

            def top1(dst, dstk, src, srck, gcol):
                b.op("vector", lambda E: E.reduce_max(
                    stat[:, gcol:gcol + 1], src[:], axis=AX.X),
                    reads=(srck,), writes=("stat",))
                b.op("vector", lambda E: E.tensor_scalar(
                    dst[:], src[:], stat[:, gcol:gcol + 1], None, ALU.is_equal),
                    reads=(srck, "stat"), writes=(dstk,))
                b.op("vector", lambda E: E.tensor_tensor_scan(
                    scr[:], onecol[:], dst[:], 0.0, ALU.mult, ALU.add),
                    reads=(dstk, "onecol"), writes=("scr",))
                b.op("vector", lambda E: E.tensor_scalar(
                    scr[:], scr[:], 1.0, None, ALU.is_equal),
                    reads=("scr",), writes=("scr",))
                b.op("vector", lambda E: E.tensor_mul(dst[:], dst[:], scr[:]),
                     reads=(dstk, "scr"), writes=(dstk,))

            top1(oh1, "oh1", gat, "gat", 0)
            b.op("vector", lambda E: E.scalar_tensor_tensor(
                scr2[:], oh1[:], -1e30, gat[:], ALU.mult, ALU.add),
                reads=("oh1", "gat"), writes=("scr2",))
            top1(oh2, "oh2", scr2, "scr2", 1)
            b.op("vector", lambda E: E.tensor_add(oht[:], oh1[:], oh2[:]),
                 reads=("oh1", "oh2"), writes=("oht",))
            b.op("vector", lambda E: E.reduce_max(stat[:, 7:8], gat[:], axis=AX.X),
                 reads=("gat",), writes=("stat",))
            b.op("vector", lambda E: E.tensor_scalar(
                stat[:, 7:8], stat[:, 7:8], -1.0, None, ALU.mult),
                reads=("stat",), writes=("stat",))
            b.op("scalar", lambda E: E.activation(
                gat[:], gat[:], AF.Exp, bias=stat[:, 7:8]),
                reads=("gat", "stat"), writes=("gat",))
            b.op("vector", lambda E: E.tensor_reduce(
                stat[:, 7:8], gat[:], axis=AX.X, op=ALU.add),
                reads=("gat",), writes=("stat",))
            b.op("vector", lambda E: E.reciprocal(stat[:, 7:8], stat[:, 7:8]),
                 reads=("stat",), writes=("stat",))
            b.op("vector", lambda E: E.tensor_scalar(
                gat[:], gat[:], stat[:, 7:8], None, ALU.mult),
                reads=("gat", "stat"), writes=("gat",))
            b.op("vector", lambda E: E.tensor_mul(scr[:], gat[:], oh1[:]),
                 reads=("gat", "oh1"), writes=("scr",))
            b.op("vector", lambda E: E.tensor_reduce(
                stat[:, 0:1], scr[:], axis=AX.X, op=ALU.add),
                reads=("scr",), writes=("stat",))
            b.op("vector", lambda E: E.tensor_mul(scr[:], gat[:], oh2[:]),
                 reads=("gat", "oh2"), writes=("scr",))
            b.op("vector", lambda E: E.tensor_reduce(
                stat[:, 1:2], scr[:], axis=AX.X, op=ALU.add),
                reads=("scr",), writes=("stat",))
            b.op("tensor", lambda E: E.matmul(ps_p[:], triu[:], oht[:],
                                              start=True, stop=False),
                 reads=("triu", "oht"), writes=("ps_p",))
            b.op("tensor", lambda E: E.matmul(ps_p[:], ones[0:1, :], carry[:],
                                              start=False, stop=True),
                 reads=("ones", "carry"), writes=("ps_p",))
            b.op("vector", lambda E: E.tensor_copy(posx[:], ps_p[:]),
                 reads=("ps_p",), writes=("posx",))
            b.op("tensor", lambda E: E.matmul(ps_c[:], ones[:, 0:1], oht[:],
                                              start=True, stop=True),
                 reads=("ones", "oht"), writes=("ps_c",))
            b.op("vector", lambda E: E.tensor_add(carry[:], carry[:], ps_c[:]),
                 reads=("ps_c", "carry"), writes=("carry",))
            b.op("vector", lambda E: E.tensor_mul(scr[:], posx[:], oh1[:]),
                 reads=("posx", "oh1"), writes=("scr",))
            b.op("vector", lambda E: E.tensor_reduce(
                stat[:, 2:3], scr[:], axis=AX.X, op=ALU.add),
                reads=("scr",), writes=("stat",))
            b.op("vector", lambda E: E.tensor_add(scr[:], posx[:], oh1[:]),
                 reads=("posx", "oh1"), writes=("scr",))
            b.op("vector", lambda E: E.tensor_mul(scr[:], scr[:], oh2[:]),
                 reads=("scr", "oh2"), writes=("scr",))
            b.op("vector", lambda E: E.tensor_reduce(
                stat[:, 3:4], scr[:], axis=AX.X, op=ALU.add),
                reads=("scr",), writes=("stat",))
            b.op("vector", lambda E: E.tensor_copy(rpack[:, 0:8], oh1[:]),
                 reads=("oh1",), writes=("rpack",))
            b.op("vector", lambda E: E.tensor_copy(rpack[:, 8:16], oh2[:]),
                 reads=("oh2",), writes=("rpack",))
            b.op("vector", lambda E: E.tensor_copy(rpack[:, 16:20], stat[:, 0:4]),
                 reads=("stat",), writes=("rpack",))
            b.op("vector", lambda E, i=i: E.tensor_copy(
                P["route_sb"][:, i * 20:(i + 1) * 20], rpack[:]),
                reads=("rpack",), writes=("route_sb",))



def emit_B_mk(b, nc, P, st):
    CH = 512
    NCH = T // CH
    if True:
        e = st.enter_context
        xn2 = e(nc.sbuf_tensor("B_xn2", [128, 8 * CH], BF16))
        wp = [e(nc.sbuf_tensor(f"B_wp{i}", [128, 2048], BF16)) for i in range(8)]
        wxp = e(nc.sbuf_tensor("B_wxp", [128, 16 * 96], BF16))
        wdt = e(nc.sbuf_tensor("B_wdt", [64, DI], BF16))
        cw = e(nc.sbuf_tensor("B_cw", [128, 64], F32))
        cb = e(nc.sbuf_tensor("B_cb", [128, 16], F32))
        dtbc = e(nc.sbuf_tensor("B_dtbc", [128, 16], F32))
        Aneg = e(nc.sbuf_tensor("B_Aneg", [128, 256], F32))
        Dv = e(nc.sbuf_tensor("B_Dv", [128, 16], F32))
        glub = e(nc.sbuf_tensor("B_glub", [128, 16], F32))
        lng = e(nc.sbuf_tensor("B_lng", [128, 8], F32))
        lnb = e(nc.sbuf_tensor("B_lnb", [128, 8], F32))
        sel = e(nc.sbuf_tensor("B_sel", [16, 16 * 128], F32))
        tails = e(nc.sbuf_tensor("B_tails", [128, 48], BF16))
        xpct = e(nc.sbuf_tensor("B_xpct", [128, 515], BF16))
        convacc = e(nc.sbuf_tensor("B_convacc", [128, CH], F32))
        xc = e(nc.sbuf_tensor("B_xc", [128, 16 * CH], BF16))
        siluz = e(nc.sbuf_tensor("B_siluz", [128, 16 * CH], BF16))
        dtraw = e(nc.sbuf_tensor("B_dtraw", [64, CH], BF16))
        Bmt = e(nc.sbuf_tensor("B_Bmt", [16, CH], F32))
        Cmt = e(nc.sbuf_tensor("B_Cmt", [16, CH], F32))
        brep = e(nc.sbuf_tensor("B_brep", [128, 16 * CH], BF16))
        crep = e(nc.sbuf_tensor("B_crep", [128, 16 * CH], BF16))
        qsp = e(nc.sbuf_tensor("B_qsp", [128, CH], F32))
        spa = e(nc.sbuf_tensor("B_spa", [128, CH], F32))
        spb = e(nc.sbuf_tensor("B_spb", [128, CH], F32))
        dtbg = [e(nc.sbuf_tensor(f"B_dtbg{i}", [128, CH], BF16)) for i in range(2)]
        dtu = [e(nc.sbuf_tensor(f"B_dtu{i}", [128, CH], BF16)) for i in range(2)]
        dA_t = [e(nc.sbuf_tensor(f"B_dA_t{i}", [128, CH], BF16)) for i in range(2)]
        dBu_t = [e(nc.sbuf_tensor(f"B_dBu_t{i}", [128, CH], BF16)) for i in range(2)]
        h_t = [e(nc.sbuf_tensor(f"B_h_t{i}", [128, CH], BF16)) for i in range(2)]
        hc_t = [e(nc.sbuf_tensor(f"B_hc_t{i}", [128, CH], BF16)) for i in range(2)]
        yacc = e(nc.sbuf_tensor("B_yacc", [128, CH], BF16))
        hcarry = e(nc.sbuf_tensor("B_hcarry", [128, 256], F32))
        y3 = e(nc.sbuf_tensor("B_y3", [128, 16 * CH], BF16))
        mT = e(nc.sbuf_tensor("B_mT", [128, 8 * CH], BF16))
        g1 = e(nc.sbuf_tensor("B_g1", [128, 8 * CH], BF16))
        sg2 = e(nc.sbuf_tensor("B_sg2", [128, 8 * CH], BF16))
        onescol = e(nc.sbuf_tensor("B_onescol", [128, 1], F32))
        psA2x = [e(nc.psum_tensor(f"B_psA{i}", [128, 512], F32)) for i in range(2)]
        psC = e(nc.psum_tensor("B_psC", [16, 512], F32))
        psD = e(nc.psum_tensor("B_psD", [16, 512], F32))
        psE = e(nc.psum_tensor("B_psE", [128, 512], F32))
        psBG = e(nc.psum_tensor("B_psBG", [128, 512], F32))

        b.dma("gpsimd", wxp[:], P["wxp_in"], writes=("wxp",))
        b.dma("gpsimd", wdt[:], P["wdt_in"], writes=("wdt",))
        for t_, a_, k_ in [(cw, P["conv_in"], "cw"), (cb, P["convb_in"], "cb"),
                           (dtbc, P["dtb_in"], "dtbc"), (Aneg, P["alog_in"], "Aneg"),
                           (Dv, P["dv_in"], "Dv"), (glub, P["glub_in"], "glub"),
                           (lng, P["lng_in"], "lng"), (lnb, P["lnb_in"], "lnb"),
                           (sel, P["sel_in"], "sel")]:
            b.dma("sync", t_[:], a_, writes=(k_,))
        b.op("scalar", lambda E: E.activation(Aneg[:], Aneg[:], AF.Exp),
             reads=("Aneg",), writes=("Aneg",))
        b.op("vector", lambda E: E.tensor_scalar(
            Aneg[:], Aneg[:], -1.0, None, ALU.mult),
            reads=("Aneg",), writes=("Aneg",))
        b.op("vector", lambda E: E.memset(tails[:], 0.0), writes=("tails",))
        b.op("vector", lambda E: E.memset(hcarry[:], 0.0), writes=tuple(f"hcarry{z}" for z in range(32)))
        b.op("vector", lambda E: E.memset(onescol[:], 1.0), writes=("onescol",))

        def do_chunk(ci):
            c0 = ci * CH
            b.dma("sync", xn2[:].rearrange("p (g t) -> p g t", g=8),
                  P["xnTb_hbm"][:, :, c0:c0 + CH], writes=("xn2",))
            for g in range(8):
                b.op("vector", lambda E, g=g: E.tensor_scalar(
                    xn2[:, g * CH:(g + 1) * CH], xn2[:, g * CH:(g + 1) * CH],
                    lng[:, g:g + 1], lnb[:, g:g + 1], ALU.mult, ALU.add),
                    reads=("xn2", "lng", "lnb"), writes=("xn2",))
            for half in range(2):
                for k in range(8):
                    b.dma("gpsimd", wp[k][:],
                          P["wip_in"][k * 128:(k + 1) * 128,
                                 half * 2048:half * 2048 + 2048],
                          writes=(f"wp{k}",))
                for ogl in range(16):
                    og = half * 16 + ogl
                    pA = psA2x[ogl % 2]
                    rA = f"psA{ogl % 2}"
                    for k in range(8):
                        b.op("tensor", lambda E, k=k, ogl=ogl, pA=pA: E.matmul(
                            pA[:], wp[k][:, ogl * 128:(ogl + 1) * 128],
                            xn2[:, k * CH:(k + 1) * CH],
                            start=(k == 0), stop=(k == 7)),
                            reads=(f"wp{k}", "xn2"),
                            writes=(rA,) if k in (0, 7) else (),
                            inc=1 if k in (0, 7) else 0)
                    if og < 16:
                        dg = og
                        b.op("scalar", lambda E, pA=pA: E.activation(
                            xpct[:, 3:515], pA[:], AF.Copy),
                            reads=(rA,), writes=("xpct",))
                        b.op("vector", lambda E, dg=dg: E.tensor_copy(
                            xpct[:, 0:3], tails[:, dg * 3:dg * 3 + 3]),
                            reads=("tails",), writes=("xpct",))
                        b.op("vector", lambda E, dg=dg: E.tensor_copy(
                            tails[:, dg * 3:dg * 3 + 3], xpct[:, 512:515]),
                            reads=("xpct",), writes=("tails",))
                        b.op("vector", lambda E, dg=dg: E.tensor_scalar(
                            convacc[:], xpct[:, 0:CH], cw[:, dg * 4:dg * 4 + 1],
                            None, ALU.mult),
                            reads=("xpct", "cw"), writes=("convacc",))
                        for kk in range(1, 4):
                            b.op("vector", lambda E, dg=dg, kk=kk: E.scalar_tensor_tensor(
                                convacc[:], xpct[:, kk:kk + CH],
                                cw[:, dg * 4 + kk:dg * 4 + kk + 1], convacc[:],
                                ALU.mult, ALU.add),
                                reads=("xpct", "cw", "convacc"), writes=("convacc",))
                        b.op("scalar", lambda E, dg=dg: E.activation(
                            xc[:, dg * CH:(dg + 1) * CH], convacc[:], AF.Silu,
                            bias=cb[:, dg:dg + 1]),
                            reads=("convacc", "cb"), writes=("xc",))
                    else:
                        zg = og - 16
                        b.op("scalar", lambda E, zg=zg, pA=pA: E.activation(
                            siluz[:, zg * CH:(zg + 1) * CH], pA[:], AF.Silu),
                            reads=(rA,), writes=("siluz",))
            for k in range(16):
                b.op("tensor", lambda E, k=k: E.matmul(
                    psBG[0:64, :], wxp[:, k * 96:k * 96 + 64],
                    xc[:, k * CH:(k + 1) * CH],
                    start=(k == 0), stop=(k == 15)),
                    reads=("wxp", "xc"),
                    writes=("psBG",) if k in (0, 15) else (),
                    inc=1 if k in (0, 15) else 0)
            for k in range(16):
                b.op("tensor", lambda E, k=k: E.matmul(
                    psC[:], wxp[:, k * 96 + 64:k * 96 + 80],
                    xc[:, k * CH:(k + 1) * CH],
                    start=(k == 0), stop=(k == 15)),
                    reads=("wxp", "xc"),
                    writes=("psC",) if k in (0, 15) else (),
                    inc=1 if k in (0, 15) else 0)
            for k in range(16):
                b.op("tensor", lambda E, k=k: E.matmul(
                    psD[:], wxp[:, k * 96 + 80:k * 96 + 96],
                    xc[:, k * CH:(k + 1) * CH],
                    start=(k == 0), stop=(k == 15)),
                    reads=("wxp", "xc"),
                    writes=("psD",) if k in (0, 15) else (),
                    inc=1 if k in (0, 15) else 0)
            b.op("scalar", lambda E: E.activation(dtraw[:], psBG[0:64, :], AF.Copy),
                 reads=("psBG",), writes=("dtraw",))
            b.op("scalar", lambda E: E.activation(Bmt[:], psC[:], AF.Copy),
                 reads=("psC",), writes=("Bmt",))
            b.op("scalar", lambda E: E.activation(Cmt[:], psD[:], AF.Copy),
                 reads=("psD",), writes=("Cmt",))
            for n in range(NS):
                b.op("tensor", lambda E, n=n: E.matmul(
                    psE[:], sel[:, n * 128:(n + 1) * 128], Bmt[:],
                    start=True, stop=True),
                    reads=("sel", "Bmt"), writes=("psE",))
                b.op("scalar", lambda E, n=n: E.activation(
                    brep[:, n * CH:(n + 1) * CH], psE[:], AF.Copy),
                    reads=("psE",), writes=("brep",))
                b.op("tensor", lambda E, n=n: E.matmul(
                    psE[:], sel[:, n * 128:(n + 1) * 128], Cmt[:],
                    start=True, stop=True),
                    reads=("sel", "Cmt"), writes=("psE",))
                b.op("scalar", lambda E, n=n: E.activation(
                    crep[:, n * CH:(n + 1) * CH], psE[:], AF.Copy),
                    reads=("psE",), writes=("crep",))
            for dg in range(16):
                b.op("tensor", lambda E, dg=dg: E.matmul(
                    psBG[:], wdt[:, dg * 128:(dg + 1) * 128], dtraw[:],
                    start=True, stop=True),
                    reads=("wdt", "dtraw"), writes=("psBG",))
                b.op("scalar", lambda E: E.activation(qsp[:], psBG[:], AF.Copy),
                     reads=("psBG",), writes=("qsp",))
                b.op("scalar", lambda E, dg=dg: E.activation(
                    spa[:], qsp[:], AF.Abs, bias=dtbc[:, dg:dg + 1]),
                    reads=("qsp", "dtbc"), writes=("spa",))
                b.op("scalar", lambda E: E.activation(
                    spa[:], spa[:], AF.Exp, scale=-1.0),
                    reads=("spa",), writes=("spa",))
                b.op("scalar", lambda E: E.activation(
                    spa[:], spa[:], AF.Ln, bias=onescol[:, 0:1]),
                    reads=("spa", "onescol"), writes=("spa",))
                b.op("vector", lambda E, dg=dg: E.tensor_scalar(
                    spb[:], qsp[:], dtbc[:, dg:dg + 1], 0.0, ALU.add, ALU.max),
                    reads=("qsp", "dtbc"), writes=("spb",))
                q2 = dg % 2
                b.op("vector", lambda E, q2=q2: E.tensor_add(dtbg[q2][:], spa[:], spb[:]),
                     reads=("spa", "spb"), writes=(f"dtbg{q2}",))
                b.op("vector", lambda E, dg=dg, q2=q2: E.tensor_mul(
                    dtu[q2][:], dtbg[q2][:], xc[:, dg * CH:(dg + 1) * CH]),
                    reads=(f"dtbg{q2}", "xc"), writes=(f"dtu{q2}",))
                b.op("vector", lambda E: E.memset(yacc[:], 0.0), writes=("yacc",))
                for n in range(NS):
                    p = n % 2
                    q2 = dg % 2
                    b.op("scalar", lambda E, dg=dg, n=n, p=p, q2=q2: E.activation(
                        dA_t[p][:], dtbg[q2][:], AF.Exp,
                        scale=Aneg[:, dg * 16 + n:dg * 16 + n + 1]),
                        reads=(f"dtbg{q2}", "Aneg"), writes=(f"dA_t{p}",))
                    b.op("vector", lambda E, n=n, p=p, q2=q2: E.tensor_mul(
                        dBu_t[p][:], dtu[q2][:], brep[:, n * CH:(n + 1) * CH]),
                        reads=(f"dtu{q2}", "brep"), writes=(f"dBu_t{p}",))
                    b.op("vector", lambda E, dg=dg, n=n, p=p: E.tensor_tensor_scan(
                        h_t[p][:], dA_t[p][:], dBu_t[p][:],
                        hcarry[:, dg * 16 + n:dg * 16 + n + 1],
                        ALU.mult, ALU.add),
                        reads=(f"dA_t{p}", f"dBu_t{p}", f"hcarry{(dg * 16 + n) % 32}"), writes=(f"h_t{p}",))
                    b.op("vector", lambda E, dg=dg, n=n, p=p: E.tensor_copy(
                        hcarry[:, dg * 16 + n:dg * 16 + n + 1],
                        h_t[p][:, CH - 1:CH]),
                        reads=(f"h_t{p}",), writes=(f"hcarry{(dg * 16 + n) % 32}",))
                    b.op("vector", lambda E, n=n, p=p: E.tensor_mul(
                        hc_t[p][:], h_t[p][:], crep[:, n * CH:(n + 1) * CH]),
                        reads=(f"h_t{p}", "crep"), writes=(f"hc_t{p}",))
                    b.op("vector", lambda E, p=p: E.tensor_add(
                        yacc[:], yacc[:], hc_t[p][:]),
                        reads=(f"hc_t{p}", "yacc"), writes=("yacc",))
                b.op("vector", lambda E, dg=dg: E.scalar_tensor_tensor(
                    yacc[:], xc[:, dg * CH:(dg + 1) * CH], Dv[:, dg:dg + 1],
                    yacc[:], ALU.mult, ALU.add),
                    reads=("xc", "Dv", "yacc"), writes=("yacc",))
                b.op("vector", lambda E, dg=dg: E.tensor_mul(
                    y3[:, dg * CH:(dg + 1) * CH], yacc[:],
                    siluz[:, dg * CH:(dg + 1) * CH]),
                    reads=("yacc", "siluz"), writes=("y3",))
            for j in range(8):
                for half in range(2):
                    k = j * 2 + half
                    b.dma("gpsimd", wp[j][:, half * 1024:half * 1024 + 1024],
                          P["wop_in"][k * 128:(k + 1) * 128, :], writes=(f"wp{j}",))
            for og in range(8):
                pA = psA2x[og % 2]
                rA = f"psA{og % 2}"
                for k in range(16):
                    b.op("tensor", lambda E, og=og, k=k, pA=pA: E.matmul(
                        pA[:], wp[k // 2][:, (k % 2) * 1024 + og * 128:
                                          (k % 2) * 1024 + (og + 1) * 128],
                        y3[:, k * CH:(k + 1) * CH],
                        start=(k == 0), stop=(k == 15)),
                        reads=(f"wp{k // 2}", "y3"),
                        writes=(rA,) if k in (0, 15) else (),
                        inc=1 if k in (0, 15) else 0)
                b.op("scalar", lambda E, og=og, pA=pA: E.activation(
                    mT[:, og * CH:(og + 1) * CH], pA[:], AF.Gelu),
                    reads=(rA,), writes=("mT",))
            for k in range(8):
                b.dma("gpsimd", wp[k][:], P["wglu_in"][k * 128:(k + 1) * 128, :],
                      writes=(f"wp{k}",))
            for og in range(16):
                pA = psA2x[og % 2]
                rA = f"psA{og % 2}"
                for k in range(8):
                    b.op("tensor", lambda E, og=og, k=k, pA=pA: E.matmul(
                        pA[:], wp[k][:, og * 128:(og + 1) * 128],
                        mT[:, k * CH:(k + 1) * CH],
                        start=(k == 0), stop=(k == 7)),
                        reads=(f"wp{k}", "mT"),
                        writes=(rA,) if k in (0, 7) else (),
                        inc=1 if k in (0, 7) else 0)
                if og < 8:
                    b.op("scalar", lambda E, og=og, pA=pA: E.activation(
                        g1[:, og * CH:(og + 1) * CH], pA[:], AF.Copy),
                        reads=(rA,), writes=("g1",))
                    b.op("vector", lambda E, og=og: E.tensor_scalar(
                        g1[:, og * CH:(og + 1) * CH],
                        g1[:, og * CH:(og + 1) * CH],
                        glub[:, og:og + 1], None, ALU.add),
                        reads=("g1", "glub"), writes=("g1",))
                else:
                    b.op("scalar", lambda E, og=og, pA=pA: E.activation(
                        sg2[:, (og - 8) * CH:(og - 7) * CH], pA[:], AF.Sigmoid,
                        bias=glub[:, og:og + 1]),
                        reads=(rA, "glub"), writes=("sg2",))
            b.dma("sync", xn2[:].rearrange("p (g t) -> p g t", g=8),
                  P["xnTb_hbm"][:, :, c0:c0 + CH], writes=("xn2",))
            for g in range(8):
                b.op("vector", lambda E, g=g: E.tensor_mul(
                    g1[:, g * CH:(g + 1) * CH], g1[:, g * CH:(g + 1) * CH],
                    sg2[:, g * CH:(g + 1) * CH]),
                    reads=("g1", "sg2"), writes=("g1",))
                b.op("vector", lambda E, g=g: E.tensor_add(
                    g1[:, g * CH:(g + 1) * CH], g1[:, g * CH:(g + 1) * CH],
                    xn2[:, g * CH:(g + 1) * CH]),
                    reads=("g1", "xn2"), writes=("g1",))
            b.dma("sync", P["sT_hbm"][:, :, c0:c0 + CH],
                  g1[:].rearrange("p (g t) -> p g t", g=8), reads=("g1",))
        return do_chunk

def emit_C(b, nc, P):
    NT = T // 128
    NT = T // 128
    with contextlib.ExitStack() as st:
        e = st.enter_context
        wp = [e(nc.sbuf_tensor(f"C_wp{i}", [128, 2048], BF16)) for i in range(8)]
        xnb = e(nc.sbuf_tensor("C_xnb", [128, 8 * T], BF16))
        hfb = e(nc.sbuf_tensor("C_hfb", [128, 16 * T], BF16))
        yacc = e(nc.sbuf_tensor("C_yacc", [128, 8 * T], BF16))
        eye = e(nc.sbuf_tensor("C_sb_eye", [128, 128], F32))
        ones = e(nc.sbuf_tensor("C_sb_ones", [128, 128], F32))
        sel8 = e(nc.sbuf_tensor("C_sb_sel8", [E0, E0 * 128], F32))
        b1t = e(nc.sbuf_tensor("C_b1t", [128, 32], F32))
        b2t = e(nc.sbuf_tensor("C_b2t", [E0, D], F32))
        rt = e(nc.sbuf_tensor("C_rt", [128, 20], F32))
        mtok = e(nc.sbuf_tensor("C_mtok", [128, E0], F32))
        scr = e(nc.sbuf_tensor("C_scr", [128, E0], F32))
        wcol = e(nc.sbuf_tensor("C_wcol", [128, 4], F32))
        mTt = e(nc.sbuf_tensor("C_mTt", [E0, T], F32))
        wnT = e(nc.sbuf_tensor("C_wnT", [1, T], F32))
        mrep = e(nc.sbuf_tensor("C_mrep", [128, T], BF16))
        wnrep = e(nc.sbuf_tensor("C_wnrep", [128, 512], BF16))
        gtmp = e(nc.sbuf_tensor("C_gtmp", [128, 8 * 512], BF16))
        ps8 = [e(nc.psum_tensor(f"C_ps8_{i}", [128, 512], F32)) for i in range(8)]

        b.dma("sync", eye[:], P["eye_in"], writes=("eye",))
        b.dma("sync", ones[:], P["ones_in"], writes=("ones",))
        b.dma("sync", sel8[:], P["sel8_in"], writes=("sel8",))
        b.dma("sync", b2t[:], P["b2_in"], writes=("b2t",))

        for i in range(NT):
            b.op("vector", lambda E, i=i: E.tensor_copy(
                rt[:], P["route_sb"][:, i * 20:(i + 1) * 20]),
                reads=("route_sb",), writes=("rt",))
            b.op("vector", lambda E: E.tensor_mul(scr[:], P["offs_rep"][:], rt[:, 0:8]),
                 reads=("offs_rep", "rt"), writes=("scr",))
            b.op("vector", lambda E: E.tensor_reduce(
                wcol[:, 0:1], scr[:], axis=AX.X, op=ALU.add),
                reads=("scr",), writes=("wcol",))
            b.op("vector", lambda E: E.tensor_add(wcol[:, 0:1], wcol[:, 0:1],
                                                  rt[:, 18:19]),
                 reads=("wcol", "rt"), writes=("wcol",))
            b.op("vector", lambda E: E.tensor_scalar(
                wcol[:, 0:1], wcol[:, 0:1], float(CAP), None, ALU.is_lt),
                reads=("wcol",), writes=("wcol",))
            b.op("vector", lambda E: E.tensor_mul(wcol[:, 0:1], wcol[:, 0:1],
                                                  rt[:, 16:17]),
                 reads=("wcol", "rt"), writes=("wcol",))
            b.op("vector", lambda E: E.tensor_mul(scr[:], P["offs_rep"][:], rt[:, 8:16]),
                 reads=("offs_rep", "rt"), writes=("scr",))
            b.op("vector", lambda E: E.tensor_reduce(
                wcol[:, 1:2], scr[:], axis=AX.X, op=ALU.add),
                reads=("scr",), writes=("wcol",))
            b.op("vector", lambda E: E.tensor_add(wcol[:, 1:2], wcol[:, 1:2],
                                                  rt[:, 19:20]),
                 reads=("wcol", "rt"), writes=("wcol",))
            b.op("vector", lambda E: E.tensor_scalar(
                wcol[:, 1:2], wcol[:, 1:2], float(CAP), None, ALU.is_lt),
                reads=("wcol",), writes=("wcol",))
            b.op("vector", lambda E: E.tensor_mul(wcol[:, 1:2], wcol[:, 1:2],
                                                  rt[:, 17:18]),
                 reads=("wcol", "rt"), writes=("wcol",))
            b.op("vector", lambda E: E.tensor_scalar(
                mtok[:], rt[:, 0:8], wcol[:, 0:1], None, ALU.mult),
                reads=("rt", "wcol"), writes=("mtok",))
            b.op("vector", lambda E: E.tensor_scalar(
                scr[:], rt[:, 8:16], wcol[:, 1:2], None, ALU.mult),
                reads=("rt", "wcol"), writes=("scr",))
            b.op("vector", lambda E: E.tensor_add(mtok[:], mtok[:], scr[:]),
                 reads=("mtok", "scr"), writes=("mtok",))
            b.op("vector", lambda E: E.tensor_add(wcol[:, 2:3], wcol[:, 0:1],
                                                  wcol[:, 1:2]),
                 reads=("wcol",), writes=("wcol",))
            b.op("vector", lambda E: E.tensor_scalar(
                wcol[:, 2:3], wcol[:, 2:3], 1e-6, None, ALU.max),
                reads=("wcol",), writes=("wcol",))
            b.op("vector", lambda E: E.reciprocal(wcol[:, 3:4], wcol[:, 2:3]),
                 reads=("wcol",), writes=("wcol",))
            b.op("tensor", lambda E: E.matmul(ps8[2][0:8, 0:128], mtok[:], eye[:],
                                              start=True, stop=True),
                 reads=("mtok", "eye"), writes=("ps2",))
            b.op("scalar", lambda E, i=i: E.activation(
                mTt[:, i * 128:(i + 1) * 128], ps8[2][0:8, 0:128], AF.Copy),
                reads=("ps2",), writes=("mTt",))
            b.op("tensor", lambda E: E.matmul(ps8[3][0:1, 0:128], wcol[:, 3:4],
                                              eye[:], start=True, stop=True),
                 reads=("wcol", "eye"), writes=("ps3",))
            b.op("scalar", lambda E, i=i: E.activation(
                wnT[:, i * 128:(i + 1) * 128], ps8[3][0:1, 0:128], AF.Copy),
                reads=("ps3",), writes=("wnT",))
        b.dma("sync", xnb[:].rearrange("p (g t) -> p g t", g=8),
              P["xnTb_hbm"][:, :, :], writes=("xnb",))
        b.op("vector", lambda E: E.memset(yacc[:], 0.0), writes=("yacc",))
        for ex in range(E0):
            b.dma("sync", b1t[:], P["b1_in"][ex], writes=("b1t",))
            for nt in range(4):
                b.op("tensor", lambda E, ex=ex, nt=nt: E.matmul(
                    ps8[4][:], sel8[:, ex * 128:(ex + 1) * 128],
                    mTt[:, nt * 512:(nt + 1) * 512], start=True, stop=True),
                    reads=("sel8", "mTt"), writes=("ps4",))
                b.op("scalar", lambda E, nt=nt: E.activation(
                    mrep[:, nt * 512:(nt + 1) * 512], ps8[4][:], AF.Copy),
                    reads=("ps4",), writes=("mrep",))
            for fb in range(2):
                for k in range(8):
                    b.dma("gpsimd", wp[k][:],
                          P["w1_in"][ex, k * 128:(k + 1) * 128,
                                fb * 2048:fb * 2048 + 2048],
                          writes=(f"wp{k}",))
                for ogl in range(16):
                    og = fb * 16 + ogl
                    for nt in range(4):
                        pb = (ogl * 4 + nt) % 2
                        for k in range(8):
                            b.op("tensor", lambda E, ogl=ogl, nt=nt, k=k, pb=pb: E.matmul(
                                ps8[pb][:], wp[k][:, ogl * 128:(ogl + 1) * 128],
                                xnb[:, k * T + nt * 512:k * T + (nt + 1) * 512],
                                start=(k == 0), stop=(k == 7)),
                                reads=(f"wp{k}", "xnb"),
                                writes=(f"ps{pb}",) if k in (0, 7) else (),
                                inc=1 if k in (0, 7) else 0)
                        b.op("scalar", lambda E, og=og, ogl=ogl, nt=nt, pb=pb: E.activation(
                            hfb[:, ogl * T + nt * 512:ogl * T + (nt + 1) * 512],
                            ps8[pb][:], AF.Gelu, bias=b1t[:, og:og + 1]),
                            reads=(f"ps{pb}", "b1t"), writes=("hfb",))
                    b.op("vector", lambda E, ogl=ogl: E.tensor_mul(
                        hfb[:, ogl * T:(ogl + 1) * T],
                        hfb[:, ogl * T:(ogl + 1) * T], mrep[:]),
                        reads=("hfb", "mrep"), writes=("hfb",))
                for k2 in range(16):
                    b.dma("gpsimd", wp[k2 // 2][:, (k2 % 2) * 1024:(k2 % 2 + 1) * 1024],
                          P["w2_in"][ex, fb * 2048 + k2 * 128:fb * 2048 + (k2 + 1) * 128, :],
                          writes=(f"wp{k2 // 2}",))
                for nt in range(4):
                    for k2 in range(16):
                        for og in range(8):
                            b.op("tensor", lambda E, k2=k2, og=og, nt=nt: E.matmul(
                                ps8[og][:],
                                wp[k2 // 2][:, (k2 % 2) * 1024 + og * 128:
                                            (k2 % 2) * 1024 + (og + 1) * 128],
                                hfb[:, k2 * T + nt * 512:k2 * T + (nt + 1) * 512],
                                start=(k2 == 0), stop=(k2 == 15)),
                                reads=(f"wp{k2 // 2}", "hfb"),
                                writes=(f"ps{og}",) if k2 in (0, 15) else (),
                                inc=1 if k2 in (0, 15) else 0)
                    for og in range(8):
                        b.op("vector", lambda E, og=og, nt=nt: E.scalar_tensor_tensor(
                            yacc[:, og * T + nt * 512:og * T + (nt + 1) * 512],
                            ps8[og][:], 1.0,
                            yacc[:, og * T + nt * 512:og * T + (nt + 1) * 512],
                            ALU.mult, ALU.add),
                            reads=(f"ps{og}", "yacc"), writes=("yacc",))
        for og in range(8):
            for nt in range(4):
                b.op("tensor", lambda E, og=og, nt=nt: E.matmul(
                    ps8[og][:], b2t[:, og * 128:(og + 1) * 128],
                    mTt[:, nt * 512:(nt + 1) * 512], start=True, stop=True),
                    reads=("b2t", "mTt"), writes=(f"ps{og}",))
                b.op("vector", lambda E, og=og, nt=nt: E.scalar_tensor_tensor(
                    yacc[:, og * T + nt * 512:og * T + (nt + 1) * 512],
                    ps8[og][:], 1.0,
                    yacc[:, og * T + nt * 512:og * T + (nt + 1) * 512],
                    ALU.mult, ALU.add),
                    reads=(f"ps{og}", "yacc"), writes=("yacc",))
        for nt in range(4):
            b.op("tensor", lambda E, nt=nt: E.matmul(
                ps8[4][:], ones[0:1, :], wnT[:, nt * 512:(nt + 1) * 512],
                start=True, stop=True),
                reads=("ones", "wnT"), writes=("ps4",))
            b.op("scalar", lambda E: E.activation(wnrep[:], ps8[4][:], AF.Copy),
                 reads=("ps4",), writes=("wnrep",))
            for og in range(8):
                b.op("vector", lambda E, og=og, nt=nt: E.tensor_mul(
                    gtmp[:, og * 512:(og + 1) * 512],
                    yacc[:, og * T + nt * 512:og * T + (nt + 1) * 512], wnrep[:]),
                    reads=("yacc", "wnrep"), writes=("gtmp",))
                b.op("scalar", lambda E, og=og: E.activation(
                    gtmp[:, og * 512:(og + 1) * 512],
                    gtmp[:, og * 512:(og + 1) * 512], AF.Silu),
                    reads=("gtmp",), writes=("gtmp",))
            b.dma("sync", P["gT_ap"].rearrange("(g p) t -> p g t", p=128)[:, :, nt * 512:(nt + 1) * 512],
                  gtmp[:, 0:8 * 512].rearrange("p (g t) -> p g t", g=8),
                  reads=("gtmp",))


def emit_D(b, nc, P):
    CH = 512
    CH = 512
    with contextlib.ExitStack() as st:
        e = st.enter_context
        eye = e(nc.sbuf_tensor("D_sb_eye", [128, 128], F32))
        ones = e(nc.sbuf_tensor("D_sb_ones", [128, 128], F32))
        xt_ = [e(nc.sbuf_tensor(f"D_xt{i}", [128, D], F32)) for i in range(2)]
        xT_ = [e(nc.sbuf_tensor(f"D_xT{i}", [128, 8 * CH], F32)) for i in range(2)]
        s_c_ = [e(nc.sbuf_tensor(f"D_s_c{i}", [128, 8 * CH], BF16)) for i in range(2)]
        g_c_ = [e(nc.sbuf_tensor(f"D_g_c{i}", [128, 8 * CH], BF16)) for i in range(2)]
        x1c_ = [e(nc.sbuf_tensor(f"D_x1c{i}", [128, 8 * CH], F32)) for i in range(2)]
        xsq_ = [e(nc.sbuf_tensor(f"D_xsq{i}", [128, 8 * CH], F32)) for i in range(2)]
        strow_ = [e(nc.sbuf_tensor(f"D_strow{i}", [1, 6 * CH], F32)) for i in range(2)]
        rrep_ = [e(nc.sbuf_tensor(f"D_rrep{i}", [128, CH], F32)) for i in range(2)]
        mrep2_ = [e(nc.sbuf_tensor(f"D_mrep2{i}", [128, CH], F32)) for i in range(2)]
        cvals = e(nc.sbuf_tensor("D_cvals", [128, 1], F32))
        ps_t_ = [e(nc.psum_tensor(f"D_ps_t{i}", [128, 512], F32)) for i in range(2)]
        ps_sq_ = [e(nc.psum_tensor(f"D_ps_sq{i}", [64, 512], F32)) for i in range(2)]
        ps_s_ = [t[0:1, :] for t in ps_sq_]
        ps_q_ = [t[32:33, :] for t in ps_sq_]
        ps_b12_ = [e(nc.psum_tensor(f"D_ps_b12{i}", [128, 512], F32)) for i in range(2)]
        ps_b1_ = ps_b12_
        ps_b2_ = ps_b12_

        b.dma("sync", eye[:], P["eye_in"], writes=("eye",))
        b.dma("sync", ones[:], P["ones_in"], writes=("ones",))
        b.op("vector", lambda E: E.memset(cvals[:], 1e-5), writes=("cvals",))
        def _d_s0(ci, c0, xt, xT, s_c, g_c, x1c, xsq, strow, rrep, mrep2,
                  ps_t, ps_s, ps_q, ps_b1, ps_b2, Z):
            b.dma("sync", s_c[:].rearrange("p (g t) -> p g t", g=8),
                  P["sT_hbm"][:, :, c0:c0 + CH], writes=("Ds_c" + Z,))
            b.dma("sync", g_c[:].rearrange("p (g t) -> p g t", g=8),
                  P["gT_hbm"][:, :, c0:c0 + CH], writes=("Dg_c" + Z,))
            for tt in range(4):
                b.dma("sync", xt[:], P["x_in"][c0 + tt * 128:c0 + (tt + 1) * 128, :],
                      writes=("Dxt" + Z,))
                for jj in (0, 4):
                    for j in range(jj, jj + 4):
                        b.op("tensor", lambda E, j=j, jj=jj: E.transpose(
                            ps_t[:, (j - jj) * 128:(j - jj) * 128 + 128],
                            xt[:, j * 128:(j + 1) * 128], eye[:]),
                            reads=("Dxt" + Z, "eye"), writes=("Dps_t" + Z,))
                    for j in range(jj, jj + 4):
                        b.op("scalar", lambda E, j=j, jj=jj, tt=tt: E.activation(
                            xT[:, j * CH + tt * 128:j * CH + (tt + 1) * 128],
                            ps_t[:, (j - jj) * 128:(j - jj) * 128 + 128], AF.Copy),
                            reads=("Dps_t" + Z,), writes=("DxT" + Z,))

        def _d_s1(ci, c0, xt, xT, s_c, g_c, x1c, xsq, strow, rrep, mrep2,
                  ps_t, ps_s, ps_q, ps_b1, ps_b2, Z):
            for g in range(8):
                b.op("vector", lambda E, g=g: E.tensor_mul(
                    x1c[:, g * CH:(g + 1) * CH], s_c[:, g * CH:(g + 1) * CH],
                    g_c[:, g * CH:(g + 1) * CH]),
                    reads=("Ds_c" + Z, "Dg_c" + Z), writes=("Dx1c" + Z,))
                b.op("vector", lambda E, g=g: E.tensor_add(
                    x1c[:, g * CH:(g + 1) * CH], x1c[:, g * CH:(g + 1) * CH],
                    xT[:, g * CH:(g + 1) * CH]),
                    reads=("Dx1c" + Z, "DxT" + Z), writes=("Dx1c" + Z,))
            b.dma("sync", P["x1_hbm"][:, :, c0:c0 + CH],
                  x1c[:].rearrange("p (g t) -> p g t", g=8), reads=("Dx1c" + Z,))

        def _d_s2(ci, c0, xt, xT, s_c, g_c, x1c, xsq, strow, rrep, mrep2,
                  ps_t, ps_s, ps_q, ps_b1, ps_b2, Z):
            b.op("scalar", lambda E: E.activation(xsq[:], x1c[:], AF.Square),
                 reads=("Dx1c" + Z,), writes=("Dxsq" + Z,))
            for g in range(8):
                b.op("tensor", lambda E, g=g: E.matmul(
                    ps_s[:], ones[:, 0:1], x1c[:, g * CH:(g + 1) * CH],
                    start=(g == 0), stop=(g == 7)),
                    reads=("ones", "Dx1c" + Z),
                    writes=("Dps_sq" + Z,) if g in (0, 7) else (),
                    inc=1 if g in (0, 7) else 0)
            for g in range(8):
                b.op("tensor", lambda E, g=g: E.matmul(
                    ps_q[:], ones[:, 0:1], xsq[:, g * CH:(g + 1) * CH],
                    start=(g == 0), stop=(g == 7)),
                    reads=("ones", "Dxsq" + Z),
                    writes=("Dps_sq" + Z,) if g in (0, 7) else (),
                    inc=1 if g in (0, 7) else 0)

        def _d_s3(ci, c0, xt, xT, s_c, g_c, x1c, xsq, strow, rrep, mrep2,
                  ps_t, ps_s, ps_q, ps_b1, ps_b2, Z):
            b.op("scalar", lambda E: E.activation(
                strow[:, 0 * CH:1 * CH], ps_s[:], AF.Copy, scale=1.0 / D),
                reads=("Dps_sq" + Z,), writes=("Dstrow" + Z,))
            b.op("scalar", lambda E: E.activation(
                strow[:, 1 * CH:2 * CH], ps_q[:], AF.Copy, scale=1.0 / D),
                reads=("Dps_sq" + Z,), writes=("Dstrow" + Z,))
            b.op("vector", lambda E: E.tensor_mul(
                strow[:, 2 * CH:3 * CH], strow[:, 0:CH], strow[:, 0:CH]),
                reads=("Dstrow" + Z,), writes=("Dstrow" + Z,))
            b.op("vector", lambda E: E.tensor_sub(
                strow[:, 2 * CH:3 * CH], strow[:, CH:2 * CH], strow[:, 2 * CH:3 * CH]),
                reads=("Dstrow" + Z,), writes=("Dstrow" + Z,))
            b.op("scalar", lambda E: E.activation(
                strow[:, 3 * CH:4 * CH], strow[:, 2 * CH:3 * CH], AF.Ln,
                bias=cvals[0:1, 0:1]),
                reads=("Dstrow" + Z, "cvals"), writes=("Dstrow" + Z,))
            b.op("scalar", lambda E: E.activation(
                strow[:, 3 * CH:4 * CH], strow[:, 3 * CH:4 * CH], AF.Exp, scale=-0.5),
                reads=("Dstrow" + Z,), writes=("Dstrow" + Z,))
            b.op("vector", lambda E: E.tensor_mul(
                strow[:, 4 * CH:5 * CH], strow[:, 0:CH], strow[:, 3 * CH:4 * CH]),
                reads=("Dstrow" + Z,), writes=("Dstrow" + Z,))
            b.op("vector", lambda E: E.tensor_scalar(
                strow[:, 4 * CH:5 * CH], strow[:, 4 * CH:5 * CH], -1.0, None,
                ALU.mult),
                reads=("Dstrow" + Z,), writes=("Dstrow" + Z,))
            b.op("tensor", lambda E: E.matmul(
                ps_b1[:], ones[0:1, :], strow[:, 3 * CH:4 * CH],
                start=True, stop=True),
                reads=("ones", "Dstrow" + Z), writes=("Dps_b" + Z,))
            b.op("scalar", lambda E: E.activation(rrep[:], ps_b1[:], AF.Copy),
                 reads=("Dps_b" + Z,), writes=("Drrep" + Z,))
            b.op("tensor", lambda E: E.matmul(
                ps_b2[:], ones[0:1, :], strow[:, 4 * CH:5 * CH],
                start=True, stop=True),
                reads=("ones", "Dstrow" + Z), writes=("Dps_b" + Z,))
            b.op("scalar", lambda E: E.activation(mrep2[:], ps_b2[:], AF.Copy),
                 reads=("Dps_b" + Z,), writes=("Dmrep2" + Z,))

        def _d_s4(ci, c0, xt, xT, s_c, g_c, x1c, xsq, strow, rrep, mrep2,
                  ps_t, ps_s, ps_q, ps_b1, ps_b2, Z):
            for g in range(8):
                b.op("vector", lambda E, g=g: E.tensor_mul(
                    xsq[:, g * CH:(g + 1) * CH], x1c[:, g * CH:(g + 1) * CH],
                    rrep[:]),
                    reads=("Dx1c" + Z, "Drrep" + Z), writes=("Dxsq" + Z,))
                b.op("vector", lambda E, g=g: E.tensor_add(
                    g_c[:, g * CH:(g + 1) * CH], xsq[:, g * CH:(g + 1) * CH],
                    mrep2[:]),
                    reads=("Dxsq" + Z, "Dmrep2" + Z), writes=("Dg_c" + Z,))
            b.dma("sync", P["x1n_hbm"][:, :, c0:c0 + CH],
                  g_c[:].rearrange("p (g t) -> p g t", g=8), reads=("Dg_c" + Z,))

        def dargs(ci):
            p = ci % 2
            return (ci, ci * CH, xt_[p], xT_[p], s_c_[p], g_c_[p], x1c_[p],
                    xsq_[p], strow_[p], rrep_[p], mrep2_[p], ps_t_[p], ps_s_[p],
                    ps_q_[p], ps_b1_[p], ps_b2_[p], f"{p}")

        for base in (0, 2):
            pair = [dargs(base), dargs(base + 1)]
            for st_fn in (_d_s0, _d_s1, _d_s2, _d_s3, _d_s4):
                for a in pair:
                    st_fn(*a)

def emit_E(b, nc, P):
    NT = T // 128
    SC = 1.0 / np.sqrt(2048.0)
    NT = T // 128
    SC = 1.0 / np.sqrt(2048.0)
    with contextlib.ExitStack() as st:
        e = st.enter_context
        x1n = e(nc.sbuf_tensor("E_x1n", [128, 8 * T], BF16))
        s1 = e(nc.sbuf_tensor("E_s1", [128, 8 * T], BF16))
        wpool = e(nc.sbuf_tensor("E_wpool", [128, 24 * 1024], BF16))
        eye = e(nc.sbuf_tensor("E_sb_eye", [128, 128], F32))
        eyeb = e(nc.sbuf_tensor("E_sb_eyeb", [128, 128], BF16))
        ones = e(nc.sbuf_tensor("E_sb_ones", [128, 128], F32))
        onesb = e(nc.sbuf_tensor("E_sb_onesb", [128, 128], BF16))
        sel8 = e(nc.sbuf_tensor("E_sb_sel8", [E0, E0 * 128], F32))
        lepeb = e(nc.sbuf_tensor("E_lepeb", [128, 8], F32))
        qkbr = e(nc.sbuf_tensor("E_qkbr", [1, 2048], F32))
        qkbrep = e(nc.sbuf_tensor("E_qkbrep", [128, 2048], F32))
        threp = e(nc.sbuf_tensor("E_threp", [128, 512], F32))
        tcol_ = [e(nc.sbuf_tensor(f"E_tcol{i}", [128, 1], F32)) for i in range(2)]
        ang_ = [e(nc.sbuf_tensor(f"E_ang{i}", [128, 512], F32)) for i in range(2)]
        ang2_ = [e(nc.sbuf_tensor(f"E_ang2{i}", [128, 512], F32)) for i in range(2)]
        angi_ = [e(nc.sbuf_tensor(f"E_angi{i}", [128, 512], I32)) for i in range(2)]
        sint_ = [e(nc.sbuf_tensor(f"E_sint{i}", [128, 512], BF16)) for i in range(2)]
        cost_ = [e(nc.sbuf_tensor(f"E_cost{i}", [128, 512], BF16)) for i in range(2)]
        kqf_ = [e(nc.sbuf_tensor(f"E_kqf{i}", [128, 1024], F32)) for i in range(2)]
        sca_ = [e(nc.sbuf_tensor(f"E_sca{i}", [128, 1024], F32)) for i in range(2)]
        kq_ = [e(nc.sbuf_tensor(f"E_kq{i}", [128, 1024], BF16)) for i in range(2)]
        kqr_ = [e(nc.sbuf_tensor(f"E_kqr{i}", [128, 1024], BF16)) for i in range(2)]
        v_t_ = [e(nc.sbuf_tensor(f"E_v_t{i}", [128, 1024], BF16)) for i in range(2)]
        qrT_ = [e(nc.sbuf_tensor(f"E_qrT{i}", [128, 1024], BF16)) for i in range(2)]
        kv_sb = e(nc.sbuf_tensor("E_kv_sb", [128, 1024], BF16))
        kmean = e(nc.sbuf_tensor("E_kmean", [1, 1024], F32))
        kmrep = e(nc.sbuf_tensor("E_kmrep", [128, 1024], BF16))
        zden_ = [e(nc.sbuf_tensor(f"E_zden{i}", [128, 8], F32)) for i in range(2)]
        zT = e(nc.sbuf_tensor("E_zT", [8, T], F32))
        zrep_ = [e(nc.sbuf_tensor(f"E_zrep{i}", [128, 512], BF16)) for i in range(2)]
        otmp_ = [e(nc.sbuf_tensor(f"E_otmp{i}", [128, 1024], BF16)) for i in range(2)]
        ps_a = e(nc.psum_tensor("E_ps_a", [128, 512], F32))
        ps_trb = e(nc.psum_tensor("E_ps_trb", [128, 512], BF16))
        ps_kv = e(nc.psum_tensor("E_ps_kv", [128, 1024], F32))
        ps_km = e(nc.psum_tensor("E_ps_km", [1, 1024], F32))
        ps_o = e(nc.psum_tensor("E_ps_o", [128, 1024], F32))

        b.dma("sync", eye[:], P["eye_in"], writes=("eye",))
        b.dma("sync", ones[:], P["ones_in"], writes=("ones",))
        b.dma("gpsimd", eyeb[:], P["eye_in"], writes=("eyeb",))
        b.dma("gpsimd", onesb[:], P["ones_in"], writes=("onesb",))
        b.dma("sync", sel8[:], P["sel8_in"], writes=("sel8",))
        b.dma("sync", lepeb[:], P["lepeb_in"], writes=("lepeb",))
        b.dma("sync", qkbr[:], P["qkb_in"].rearrange("(o t) -> o t", o=1), writes=("qkbr",))
        b.dma("sync", x1n[:].rearrange("p (g t) -> p g t", g=8),
              P["x1n_hbm"][:, :, :], writes=("x1n",))
        for nt in range(4):
            b.op("tensor", lambda E, nt=nt: E.matmul(
                ps_a[:], ones[0:1, :], qkbr[:, nt * 512:(nt + 1) * 512],
                start=True, stop=True),
                reads=("ones", "qkbr"), writes=("ps_a",))
            b.op("scalar", lambda E, nt=nt: E.activation(
                qkbrep[:, nt * 512:(nt + 1) * 512], ps_a[:], AF.Copy),
                reads=("ps_a",), writes=("qkbrep",))
        b.op("gpsimd", lambda E: E.iota(ang_[0][0:1, :], [[1, 512]],
                                        channel_multiplier=0,
                                        allow_small_or_imprecise_dtypes=True),
             writes=("Eang0",))
        b.op("scalar", lambda E: E.activation(
            ang_[0][0:1, :], ang_[0][0:1, :], AF.Exp, scale=float(-np.log(10000.0) / 512)),
            reads=("Eang0",), writes=("Eang0",))
        b.op("tensor", lambda E: E.matmul(ps_a[:], ones[0:1, :], ang_[0][0:1, :],
                                          start=True, stop=True),
             reads=("ones", "Eang0"), writes=("ps_a",))
        b.op("scalar", lambda E: E.activation(threp[:], ps_a[:], AF.Copy),
             reads=("ps_a",), writes=("threp",))

        for w in range(3):
            for k in range(8):
                b.dma("gpsimd", wpool[:, (w * 8 + k) * 1024:(w * 8 + k + 1) * 1024],
                      P["lepe_in"][w, k * 128:(k + 1) * 128, :], writes=("wpool",))
        for og in range(8):
            for nt in range(4):
                first = True
                for w in range(3):
                    lo = nt * 512 + (w - 1)
                    hi = lo + 512
                    plo, phi = 0, 512
                    if lo < 0:
                        plo = 1; lo = 0
                    if hi > T:
                        phi = 511; hi = T
                    for k in range(8):
                        last = (w == 2 and k == 7)
                        b.op("tensor", lambda E, og=og, w=w, k=k, lo=lo, hi=hi,
                             plo=plo, phi=phi, first=first: E.matmul(
                            ps_a[:, plo:phi],
                            wpool[:, (w * 8 + k) * 1024 + og * 128:
                                  (w * 8 + k) * 1024 + (og + 1) * 128],
                            x1n[:, k * T + lo:k * T + hi],
                            start=first, stop=(w == 2 and k == 7)),
                            reads=("wpool", "x1n"),
                            writes=("ps_a",) if (first or last) else (),
                            inc=1 if (first or last) else 0)
                        first = False
                b.op("scalar", lambda E, og=og, nt=nt: E.activation(
                    s1[:, og * T + nt * 512:og * T + (nt + 1) * 512],
                    ps_a[:], AF.Copy),
                    reads=("ps_a",), writes=("s1",))
                b.op("vector", lambda E, og=og, nt=nt: E.tensor_scalar(
                    s1[:, og * T + nt * 512:og * T + (nt + 1) * 512],
                    s1[:, og * T + nt * 512:og * T + (nt + 1) * 512],
                    lepeb[:, og:og + 1], None, ALU.add),
                    reads=("s1", "lepeb"), writes=("s1",))

        for k in range(8):
            for half in range(2):
                b.dma("gpsimd", wpool[:, (k * 2 + half) * 1024:(k * 2 + half + 1) * 1024],
                      P["qkw_in"][k * 128:(k + 1) * 128, half * 1024:(half + 1) * 1024],
                      writes=("wpool",))

        def rope_tables(i, tcol, ang, ang2, angi, sint, cost, sca, Z):
            b.op("gpsimd", lambda E, i=i: E.iota(tcol[:], [[1, 1]], base=i * 128,
                                                 channel_multiplier=1,
                                                 allow_small_or_imprecise_dtypes=True),
                 writes=("Etcol" + Z,))
            b.op("vector", lambda E: E.tensor_scalar(
                ang[:], threp[:], tcol[:, 0:1], None, ALU.mult),
                reads=("threp", "Etcol" + Z), writes=("Eang" + Z,))
            for src_, dst, dstk in ((ang, sint, "Esint" + Z), (ang, cost, "Ecost" + Z)):
                if dst is cost:
                    b.op("vector", lambda E: E.tensor_scalar(
                        ang[:], ang[:], float(np.pi / 2), None, ALU.add),
                        reads=("Eang" + Z,), writes=("Eang" + Z,))
                b.op("vector", lambda E: E.tensor_scalar(
                    ang2[:], ang[:], float(1.0 / TWOPI), None, ALU.mult),
                    reads=("Eang" + Z,), writes=("Eang2" + Z,))
                b.op("vector", lambda E: E.tensor_copy(angi[:], ang2[:]),
                     reads=("Eang2" + Z,), writes=("Eangi" + Z,))
                b.op("vector", lambda E: E.tensor_copy(ang2[:], angi[:]),
                     reads=("Eangi" + Z,), writes=("Eang2" + Z,))
                b.op("vector", lambda E: E.scalar_tensor_tensor(
                    ang2[:], ang2[:], float(-TWOPI), ang[:], ALU.mult, ALU.add),
                    reads=("Eang2" + Z, "Eang" + Z), writes=("Eang2" + Z,))
                b.op("vector", lambda E: E.tensor_scalar(
                    sca[:, 0:512], ang2[:], float(np.pi), None, ALU.is_gt),
                    reads=("Eang2" + Z,), writes=("Esca" + Z,))
                b.op("vector", lambda E: E.scalar_tensor_tensor(
                    ang2[:], sca[:, 0:512], float(-TWOPI), ang2[:], ALU.mult, ALU.add),
                    reads=("Esca" + Z, "Eang2" + Z), writes=("Eang2" + Z,))
                b.op("vector", lambda E: E.tensor_scalar(
                    sca[:, 0:512], ang2[:], float(-np.pi), None, ALU.is_lt),
                    reads=("Eang2" + Z,), writes=("Esca" + Z,))
                b.op("vector", lambda E: E.scalar_tensor_tensor(
                    ang2[:], sca[:, 0:512], float(TWOPI), ang2[:], ALU.mult, ALU.add),
                    reads=("Esca" + Z, "Eang2" + Z), writes=("Eang2" + Z,))
                b.op("scalar", lambda E, dst=dst: E.activation(
                    dst[:], ang2[:], AF.Sin),
                    reads=("Eang2" + Z,), writes=(dstk,))

        def qk_stage(i, half, kqf, sca, kq, kqr, sint, cost, Z):
            for nsub in range(2):
                for k in range(8):
                    b.op("tensor", lambda E, i=i, half=half, nsub=nsub, k=k: E.matmul(
                        ps_a[:], x1n[:, k * T + i * 128:k * T + (i + 1) * 128],
                        wpool[:, (k * 2 + half) * 1024 + nsub * 512:
                              (k * 2 + half) * 1024 + (nsub + 1) * 512],
                        start=(k == 0), stop=(k == 7)),
                        reads=("x1n", "wpool"),
                        writes=("ps_a",) if k in (0, 7) else (),
                        inc=1 if k in (0, 7) else 0)
                b.op("scalar", lambda E, nsub=nsub: E.activation(
                    kqf[:, nsub * 512:(nsub + 1) * 512], ps_a[:], AF.Copy),
                    reads=("ps_a",), writes=("Ekqf" + Z,))
            b.op("vector", lambda E, half=half: E.tensor_add(
                kqf[:], kqf[:], qkbrep[:, half * 1024:(half + 1) * 1024]),
                reads=("Ekqf" + Z, "qkbrep"), writes=("Ekqf" + Z,))
            b.op("vector", lambda E: E.tensor_scalar(
                sca[:], kqf[:], 0.0, None, ALU.min),
                reads=("Ekqf" + Z,), writes=("Esca" + Z,))
            b.op("scalar", lambda E: E.activation(sca[:], sca[:], AF.Exp),
                 reads=("Esca" + Z,), writes=("Esca" + Z,))
            b.op("vector", lambda E: E.tensor_scalar(
                kqf[:], kqf[:], 0.0, None, ALU.max),
                reads=("Ekqf" + Z,), writes=("Ekqf" + Z,))
            b.op("vector", lambda E: E.tensor_add(kq[:], kqf[:], sca[:]),
                 reads=("Ekqf" + Z, "Esca" + Z), writes=("Ekq" + Z,))
            b.op("vector", lambda E: E.tensor_mul(
                kqr[:, 0:512], cost[:], kq[:, 0:1024:2]),
                reads=("Ecost" + Z, "Ekq" + Z), writes=("Ekqr" + Z,))
            b.op("vector", lambda E: E.tensor_mul(
                sca[:, 0:512], sint[:], kq[:, 1:1024:2]),
                reads=("Esint" + Z, "Ekq" + Z), writes=("Esca" + Z,))
            b.op("vector", lambda E: E.tensor_sub(
                kqr[:, 0:512], kqr[:, 0:512], sca[:, 0:512]),
                reads=("Ekqr" + Z, "Esca" + Z), writes=("Ekqr" + Z,))
            b.op("vector", lambda E: E.tensor_mul(
                kqr[:, 512:1024], cost[:], kq[:, 1:1024:2]),
                reads=("Ecost" + Z, "Ekq" + Z), writes=("Ekqr" + Z,))
            b.op("vector", lambda E: E.tensor_mul(
                sca[:, 0:512], sint[:], kq[:, 0:1024:2]),
                reads=("Esint" + Z, "Ekq" + Z), writes=("Esca" + Z,))
            b.op("vector", lambda E: E.tensor_add(
                kqr[:, 512:1024], kqr[:, 512:1024], sca[:, 0:512]),
                reads=("Ekqr" + Z, "Esca" + Z), writes=("Ekqr" + Z,))

        def p1_tail(i, kq, kqr, v_t, Z):
            for jj in (0, 4):
                for j in range(jj, jj + 4):
                    b.op("tensor", lambda E, i=i, j=j, jj=jj: E.transpose(
                        ps_trb[:, (j - jj) * 128:(j - jj) * 128 + 128],
                        x1n[:, j * T + i * 128:j * T + (i + 1) * 128], eyeb[:]),
                        reads=("x1n", "eyeb"), writes=("ps_trb",))
                for j in range(jj, jj + 4):
                    b.op("scalar", lambda E, j=j, jj=jj, v_t=v_t: E.activation(
                        v_t[:, j * 128:(j + 1) * 128],
                        ps_trb[:, (j - jj) * 128:(j - jj) * 128 + 128], AF.Copy),
                        reads=("ps_trb",), writes=("Ev_t" + Z,))
            for h in range(NH):
                b.op("tensor", lambda E, i=i, h=h, kqr=kqr, v_t=v_t: E.matmul(
                    ps_kv[:, h * 128:(h + 1) * 128], kqr[:, h * 128:(h + 1) * 128],
                    v_t[:, h * 128:(h + 1) * 128],
                    start=(i == 0 and h % 4 == 0),
                    stop=(i == NT - 1 and h % 4 == 3)),
                    reads=("Ekqr" + Z, "Ev_t" + Z),
                    writes=("ps_kv",) if i in (0, NT - 1) else (),
                    inc=1 if i in (0, NT - 1) else 0)
            for half in range(2):
                b.op("tensor", lambda E, i=i, half=half, kq=kq: E.matmul(
                    ps_km[:, half * 512:(half + 1) * 512], onesb[:, 0:1],
                    kq[:, half * 512:(half + 1) * 512],
                    start=(i == 0), stop=(i == NT - 1)),
                    reads=("onesb", "Ekq" + Z),
                    writes=("ps_km",) if i in (0, NT - 1) else (),
                    inc=1 if i in (0, NT - 1) else 0)

        def p1_args(i):
            p = i % 2
            return (p, tcol_[p], ang_[p], ang2_[p], angi_[p], sint_[p], cost_[p],
                    sca_[p], kqf_[p], kq_[p], kqr_[p], v_t_[p])

        for base in range(0, NT, 2):
            pr = [(base + u, p1_args(base + u)) for u in range(2)]
            for (i2, (p, tcol, ang, ang2, angi, sint, cost, sca, kqf, kq, kqr, v_t)) in pr:
                rope_tables(i2, tcol, ang, ang2, angi, sint, cost, sca, f"{p}")
            for (i2, (p, tcol, ang, ang2, angi, sint, cost, sca, kqf, kq, kqr, v_t)) in pr:
                qk_stage(i2, 1, kqf, sca, kq, kqr, sint, cost, f"{p}")
            for (i2, (p, tcol, ang, ang2, angi, sint, cost, sca, kqf, kq, kqr, v_t)) in pr:
                p1_tail(i2, kq, kqr, v_t, f"{p}")
        b.op("scalar", lambda E: E.activation(kv_sb[:], ps_kv[:], AF.Copy,
                                              scale=float(SC)),
             reads=("ps_kv",), writes=("kv_sb",))
        b.op("scalar", lambda E: E.activation(kmean[:], ps_km[:], AF.Copy,
                                              scale=float(1.0 / T)),
             reads=("ps_km",), writes=("kmean",))
        for half in range(2):
            b.op("tensor", lambda E, half=half: E.matmul(
                ps_o[:, half * 512:(half + 1) * 512], ones[0:1, :],
                kmean[:, half * 512:(half + 1) * 512], start=True, stop=True),
                reads=("ones", "kmean"), writes=("ps_o",))
        b.op("scalar", lambda E: E.activation(kmrep[:], ps_o[:], AF.Copy),
             reads=("ps_o",), writes=("kmrep",))

        def p2_tail(i, kq, kqr, sca, zden, qrT, otmp, zrep, Z):
            b.op("vector", lambda E, kq=kq, sca=sca: E.tensor_mul(sca[:], kq[:], kmrep[:]),
                 reads=("Ekq" + Z, "kmrep"), writes=("Esca" + Z,))
            b.op("vector", lambda E, sca=sca, zden=zden: E.tensor_reduce(
                zden[:], sca[:].rearrange("p (h d) -> p h d", h=8),
                axis=AX.X, op=ALU.add),
                reads=("Esca" + Z,), writes=("Ezden" + Z,))
            b.op("vector", lambda E, zden=zden: E.tensor_scalar(
                zden[:], zden[:], 1e-6, None, ALU.add),
                reads=("Ezden" + Z,), writes=("Ezden" + Z,))
            b.op("vector", lambda E, zden=zden: E.reciprocal(zden[:], zden[:]),
                 reads=("Ezden" + Z,), writes=("Ezden" + Z,))
            b.op("tensor", lambda E, zden=zden: E.matmul(ps_a[0:8, 0:128], zden[:], eye[:],
                                              start=True, stop=True),
                 reads=("Ezden" + Z, "eye"), writes=("ps_a",))
            b.op("scalar", lambda E, i=i: E.activation(
                zT[:, i * 128:(i + 1) * 128], ps_a[0:8, 0:128], AF.Copy),
                reads=("ps_a",), writes=("zT",))
            for jj in (0, 4):
                for j in range(jj, jj + 4):
                    b.op("tensor", lambda E, j=j, jj=jj, kqr=kqr: E.transpose(
                        ps_trb[:, (j - jj) * 128:(j - jj) * 128 + 128],
                        kqr[:, j * 128:(j + 1) * 128], eyeb[:]),
                        reads=("Ekqr" + Z, "eyeb"), writes=("ps_trb",))
                for j in range(jj, jj + 4):
                    b.op("scalar", lambda E, j=j, jj=jj, qrT=qrT: E.activation(
                        qrT[:, j * 128:(j + 1) * 128],
                        ps_trb[:, (j - jj) * 128:(j - jj) * 128 + 128], AF.Copy),
                        reads=("ps_trb",), writes=("EqrT" + Z,))
            for h in range(NH):
                b.op("tensor", lambda E, h=h, qrT=qrT: E.matmul(
                    ps_o[:, h * 128:(h + 1) * 128], kv_sb[:, h * 128:(h + 1) * 128],
                    qrT[:, h * 128:(h + 1) * 128],
                    start=(h % 4 == 0), stop=(h % 4 == 3)),
                    reads=("kv_sb", "EqrT" + Z), writes=("ps_o",))
            for hq in range(2):
                for h4 in range(4):
                    h = hq * 4 + h4
                    b.op("tensor", lambda E, h=h, h4=h4, i=i: E.matmul(
                        ps_a[:, h4 * 128:(h4 + 1) * 128],
                        sel8[:, h * 128:(h + 1) * 128],
                        zT[:, i * 128:(i + 1) * 128],
                        start=(h4 == 0), stop=(h4 == 3)),
                        reads=("sel8", "zT"), writes=("ps_a",))
                b.op("scalar", lambda E, hq=hq, zrep=zrep: E.activation(
                    zrep[:], ps_a[:], AF.Copy),
                    reads=("ps_a",), writes=("Ezrep" + Z,))
                for h4 in range(4):
                    h = hq * 4 + h4
                    b.op("vector", lambda E, h=h, h4=h4, otmp=otmp, zrep=zrep: E.tensor_mul(
                        otmp[:, h * 128:(h + 1) * 128],
                        ps_o[:, h * 128:(h + 1) * 128],
                        zrep[:, h4 * 128:(h4 + 1) * 128]),
                        reads=("ps_o", "Ezrep" + Z), writes=("Eotmp" + Z,))
            for h in range(NH):
                b.op("vector", lambda E, h=h, i=i, otmp=otmp: E.tensor_add(
                    s1[:, h * T + i * 128:h * T + (i + 1) * 128],
                    s1[:, h * T + i * 128:h * T + (i + 1) * 128],
                    otmp[:, h * 128:(h + 1) * 128]),
                    reads=("s1", "Eotmp" + Z), writes=("s1",))

        for base in range(0, NT, 2):
            pr = [(base + u, p1_args(base + u)) for u in range(2)]
            for (i2, (p, tcol, ang, ang2, angi, sint, cost, sca, kqf, kq, kqr, v_t)) in pr:
                rope_tables(i2, tcol, ang, ang2, angi, sint, cost, sca, f"{p}")
            for (i2, (p, tcol, ang, ang2, angi, sint, cost, sca, kqf, kq, kqr, v_t)) in pr:
                qk_stage(i2, 0, kqf, sca, kq, kqr, sint, cost, f"{p}")
            for (i2, (p, tcol, ang, ang2, angi, sint, cost, sca, kqf, kq, kqr, v_t)) in pr:
                p2_tail(i2, kq, kqr, sca, zden_[p], qrT_[p], otmp_[p], zrep_[p], f"{p}")
        for nt in range(4):
            b.dma("sync", P["s1_hbm"][:, :, nt * 512:(nt + 1) * 512],
                  s1[:, :].rearrange("p (g t) -> p g t", g=8)[:, :, nt * 512:(nt + 1) * 512],
                  reads=("s1",))


def emit_F(b, nc, P):
    NT = T // 128
    NT = T // 128
    with contextlib.ExitStack() as st:
        e = st.enter_context
        x1n = e(nc.sbuf_tensor("F_x1n", [128, 8 * T], BF16))
        wp = [e(nc.sbuf_tensor(f"F_wp{i}", [128, 2048], BF16)) for i in range(8)]
        eye = e(nc.sbuf_tensor("F_sb_eye", [128, 128], F32))
        ones = e(nc.sbuf_tensor("F_sb_ones", [128, 128], F32))
        sel8 = e(nc.sbuf_tensor("F_sb_sel8", [E0, E0 * 128], F32))
        projP = e(nc.sbuf_tensor("F_sb_projP", [128, 48], BF16))
        anchT = e(nc.sbuf_tensor("F_sb_anchT", [6, 6], F32))
        cscrep = e(nc.sbuf_tensor("F_cscrep", [128, 1], F32))
        qtok = e(nc.sbuf_tensor("F_qtok", [128, 8], F32))
        dott = e(nc.sbuf_tensor("F_dott", [128, 8], F32))
        o1 = e(nc.sbuf_tensor("F_o1", [128, 8], F32))
        o2 = e(nc.sbuf_tensor("F_o2", [128, 8], F32))
        scr = e(nc.sbuf_tensor("F_scr", [128, 8], F32))
        stat = e(nc.sbuf_tensor("F_stat", [128, 8], F32))
        onecol = e(nc.sbuf_tensor("F_onecol", [128, 8], F32))
        ewm = e(nc.sbuf_tensor("F_ewm", [128, 8], F32))
        ewT = e(nc.sbuf_tensor("F_ewT", [8, T], F32))
        mrep = e(nc.sbuf_tensor("F_mrep", [128, T], BF16))
        h_e = e(nc.sbuf_tensor("F_h_e", [128, 4 * T], BF16))
        yacc = e(nc.sbuf_tensor("F_yacc", [128, 8 * T], F32))
        u_f = e(nc.sbuf_tensor("F_u_f", [128, 512], F32))
        sq_f = e(nc.sbuf_tensor("F_sq_f", [128, 512], F32))
        th_f = e(nc.sbuf_tensor("F_th_f", [128, 512], F32))
        gtile = e(nc.sbuf_tensor("F_gtile", [128, 8 * 512], BF16))
        s1c = e(nc.sbuf_tensor("F_s1c", [128, 8 * 512], BF16))
        x1c = e(nc.sbuf_tensor("F_x1c", [128, 8 * 512], F32))
        outtok = e(nc.sbuf_tensor("F_outtok", [128, D], F32))
        ps8 = [e(nc.psum_tensor(f"F_ps8_{i}", [128, 512], F32)) for i in range(8)]

        b.dma("sync", eye[:], P["eye_in"], writes=("eye",))
        b.dma("sync", ones[:], P["ones_in"], writes=("ones",))
        b.dma("sync", sel8[:], P["sel8_in"], writes=("sel8",))
        b.dma("gpsimd", projP[:], P["projP_in"], writes=("projP",))
        b.dma("sync", anchT[:], P["anchT_in"], writes=("anchT",))
        b.dma("sync", cscrep[0:1, 0:1], P["temp_in"], writes=("csc0",))
        b.op("vector", lambda E: E.memset(onecol[:], 1.0), writes=("onecol",))
        b.op("vector", lambda E: E.tensor_scalar(
            cscrep[0:1, 0:1], cscrep[0:1, 0:1], 0.1, None, ALU.max),
            reads=("csc0",), writes=("csc0",))
        b.op("vector", lambda E: E.reciprocal(cscrep[0:1, 0:1], cscrep[0:1, 0:1]),
             reads=("csc0",), writes=("csc0",))
        b.op("vector", lambda E: E.tensor_scalar(
            cscrep[0:1, 0:1], cscrep[0:1, 0:1], 0.5, None, ALU.mult),
            reads=("csc0",), writes=("csc0",))
        b.op("tensor", lambda E: E.matmul(ps8[0][:, 0:1], ones[0:1, :],
                                          cscrep[0:1, 0:1], start=True, stop=True),
             reads=("ones", "csc0"), writes=("ps0",))
        b.op("scalar", lambda E: E.activation(cscrep[:], ps8[0][:, 0:1], AF.Copy),
             reads=("ps0",), writes=("cscrep",))
        b.dma("sync", x1n[:].rearrange("p (g t) -> p g t", g=8),
              P["x1n_hbm"][:, :, :], writes=("x1n",))

        for i in range(NT):
            for k in range(8):
                b.op("tensor", lambda E, i=i, k=k: E.matmul(
                    ps8[0][:, 0:6], x1n[:, k * T + i * 128:k * T + (i + 1) * 128],
                    projP[:, k * 6:(k + 1) * 6],
                    start=(k == 0), stop=(k == 7)),
                    reads=("x1n", "projP"),
                    writes=("ps0",) if k in (0, 7) else (),
                    inc=1 if k in (0, 7) else 0)
            b.op("scalar", lambda E: E.activation(
                qtok[:, 0:6], ps8[0][:, 0:6], AF.Tanh, scale=float(1.0 / 0.3)),
                reads=("ps0",), writes=("qtok",))
            b.op("tensor", lambda E: E.matmul(ps8[1][0:6, 0:128], qtok[:, 0:6],
                                              eye[:], start=True, stop=True),
                 reads=("qtok", "eye"), writes=("ps1",))
            b.op("scalar", lambda E: E.activation(
                u_f[0:6, 0:128], ps8[1][0:6, 0:128], AF.Copy),
                reads=("ps1",), writes=("u_f",))
            b.op("tensor", lambda E: E.matmul(ps8[2][:, 0:6], u_f[0:6, 0:128],
                                              anchT[:], start=True, stop=True),
                 reads=("u_f", "anchT"), writes=("ps2",))
            b.op("vector", lambda E: E.tensor_copy(dott[:, 0:6], ps8[2][:, 0:6]),
                 reads=("ps2",), writes=("dott",))

            def top1(dst, dstk, srck, src, gcol):
                b.op("vector", lambda E: E.reduce_max(
                    stat[:, gcol:gcol + 1], src[:, 0:6], axis=AX.X),
                    reads=(srck,), writes=("stat",))
                b.op("vector", lambda E: E.tensor_scalar(
                    dst[:, 0:6], src[:, 0:6], stat[:, gcol:gcol + 1], None,
                    ALU.is_equal),
                    reads=(srck, "stat"), writes=(dstk,))
                b.op("vector", lambda E: E.tensor_tensor_scan(
                    scr[:, 0:6], onecol[:, 0:6], dst[:, 0:6], 0.0,
                    ALU.mult, ALU.add),
                    reads=(dstk, "onecol"), writes=("scr",))
                b.op("vector", lambda E: E.tensor_scalar(
                    scr[:, 0:6], scr[:, 0:6], 1.0, None, ALU.is_equal),
                    reads=("scr",), writes=("scr",))
                b.op("vector", lambda E: E.tensor_mul(
                    dst[:, 0:6], dst[:, 0:6], scr[:, 0:6]),
                    reads=(dstk, "scr"), writes=(dstk,))

            top1(o1, "o1", "dott", dott, 0)
            b.op("vector", lambda E: E.scalar_tensor_tensor(
                qtok[:, 0:6], o1[:, 0:6], -1e30, dott[:, 0:6], ALU.mult, ALU.add),
                reads=("o1", "dott"), writes=("qtok",))
            top1(o2, "o2", "qtok", qtok, 1)
            b.op("vector", lambda E: E.tensor_sub(
                stat[:, 2:3], stat[:, 0:1], stat[:, 1:2]),
                reads=("stat",), writes=("stat",))
            b.op("vector", lambda E: E.tensor_mul(
                stat[:, 2:3], stat[:, 2:3], cscrep[:, 0:1]),
                reads=("stat", "cscrep"), writes=("stat",))
            b.op("scalar", lambda E: E.activation(
                stat[:, 3:4], stat[:, 2:3], AF.Sigmoid),
                reads=("stat",), writes=("stat",))
            b.op("vector", lambda E: E.tensor_scalar(
                stat[:, 4:5], stat[:, 3:4], -1.0, 1.0, ALU.mult, ALU.add),
                reads=("stat",), writes=("stat",))
            b.op("vector", lambda E: E.tensor_scalar(
                ewm[:, 0:6], o1[:, 0:6], stat[:, 3:4], None, ALU.mult),
                reads=("o1", "stat"), writes=("ewm",))
            b.op("vector", lambda E: E.tensor_scalar(
                scr[:, 0:6], o2[:, 0:6], stat[:, 4:5], None, ALU.mult),
                reads=("o2", "stat"), writes=("scr",))
            b.op("vector", lambda E: E.tensor_add(
                ewm[:, 0:6], ewm[:, 0:6], scr[:, 0:6]),
                reads=("ewm", "scr"), writes=("ewm",))
            b.op("vector", lambda E: E.memset(ewm[:, 6:8], 0.0), writes=("ewm",))
            b.op("tensor", lambda E: E.matmul(ps8[3][0:8, 0:128], ewm[:],
                                              eye[:], start=True, stop=True),
                 reads=("ewm", "eye"), writes=("ps3",))
            b.op("scalar", lambda E, i=i: E.activation(
                ewT[:, i * 128:(i + 1) * 128], ps8[3][0:8, 0:128], AF.Copy),
                reads=("ps3",), writes=("ewT",))

        b.op("vector", lambda E: E.memset(yacc[:], 0.0), writes=("yacc",))
        for ex in range(E1):
            for nt in range(4):
                b.op("tensor", lambda E, ex=ex, nt=nt: E.matmul(
                    ps8[4][:], sel8[0:6, ex * 128:(ex + 1) * 128],
                    ewT[0:6, nt * 512:(nt + 1) * 512], start=True, stop=True),
                    reads=("sel8", "ewT"), writes=("ps4",))
                b.op("scalar", lambda E, nt=nt: E.activation(
                    mrep[:, nt * 512:(nt + 1) * 512], ps8[4][:], AF.Copy),
                    reads=("ps4",), writes=("mrep",))
            for k in range(8):
                b.dma("gpsimd", wp[k // 4][:, (k % 4) * 512:(k % 4 + 1) * 512],
                      P["w1f_in"][ex, k * 128:(k + 1) * 128, :],
                      writes=(f"wp{k // 4}",))
            for ogl in range(4):
                for nt in range(4):
                    for k in range(8):
                        b.op("tensor", lambda E, ogl=ogl, nt=nt, k=k: E.matmul(
                            ps8[0][:],
                            wp[k // 4][:, (k % 4) * 512 + ogl * 128:
                                       (k % 4) * 512 + (ogl + 1) * 128],
                            x1n[:, k * T + nt * 512:k * T + (nt + 1) * 512],
                            start=(k == 0), stop=(k == 7)),
                            reads=(f"wp{k // 4}", "x1n"),
                            writes=("ps0",) if k in (0, 7) else (),
                            inc=1 if k in (0, 7) else 0)
                    b.op("scalar", lambda E: E.activation(u_f[:], ps8[0][:], AF.Copy),
                         reads=("ps0",), writes=("u_f",))
                    b.op("scalar", lambda E: E.activation(sq_f[:], u_f[:], AF.Square),
                         reads=("u_f",), writes=("sq_f",))
                    b.op("vector", lambda E: E.tensor_mul(sq_f[:], sq_f[:], u_f[:]),
                         reads=("sq_f", "u_f"), writes=("sq_f",))
                    b.op("vector", lambda E: E.scalar_tensor_tensor(
                        sq_f[:], sq_f[:], 0.044715, u_f[:], ALU.mult, ALU.add),
                        reads=("sq_f", "u_f"), writes=("sq_f",))
                    b.op("scalar", lambda E: E.activation(
                        th_f[:], sq_f[:], AF.Tanh, scale=0.7978845608028654),
                        reads=("sq_f",), writes=("th_f",))
                    b.op("vector", lambda E: E.tensor_scalar(
                        th_f[:], th_f[:], 1.0, 0.5, ALU.add, ALU.mult),
                        reads=("th_f",), writes=("th_f",))
                    b.op("vector", lambda E: E.tensor_mul(th_f[:], th_f[:], u_f[:]),
                         reads=("th_f", "u_f"), writes=("th_f",))
                    b.op("vector", lambda E, ogl=ogl, nt=nt: E.tensor_mul(
                        h_e[:, ogl * T + nt * 512:ogl * T + (nt + 1) * 512],
                        th_f[:], mrep[:, nt * 512:(nt + 1) * 512]),
                        reads=("th_f", "mrep"), writes=("h_e",))
            for k in range(4):
                b.dma("gpsimd", wp[4 + k // 2][:, (k % 2) * 1024:(k % 2 + 1) * 1024],
                      P["w2f_in"][ex, k * 128:(k + 1) * 128, :],
                      writes=(f"wp{4 + k // 2}",))
            for nt in range(4):
                for k in range(4):
                    for og in range(8):
                        b.op("tensor", lambda E, k=k, og=og, nt=nt: E.matmul(
                            ps8[og][:],
                            wp[4 + k // 2][:, (k % 2) * 1024 + og * 128:
                                           (k % 2) * 1024 + (og + 1) * 128],
                            h_e[:, k * T + nt * 512:k * T + (nt + 1) * 512],
                            start=(k == 0), stop=(k == 3)),
                            reads=(f"wp{4 + k // 2}", "h_e"),
                            writes=(f"ps{og}",) if k in (0, 3) else (),
                            inc=1 if k in (0, 3) else 0)
                for og in range(8):
                    b.op("vector", lambda E, og=og, nt=nt: E.scalar_tensor_tensor(
                        yacc[:, og * T + nt * 512:og * T + (nt + 1) * 512],
                        ps8[og][:], 1.0,
                        yacc[:, og * T + nt * 512:og * T + (nt + 1) * 512],
                        ALU.mult, ALU.add),
                        reads=(f"ps{og}", "yacc"), writes=("yacc",))
        for nt in range(4):
            c0 = nt * 512
            for og in range(8):
                b.op("scalar", lambda E, og=og, nt=nt: E.activation(
                    gtile[:, og * 512:(og + 1) * 512],
                    yacc[:, og * T + nt * 512:og * T + (nt + 1) * 512], AF.Silu),
                    reads=("yacc",), writes=("gtile",))
            b.dma("sync", s1c[:].rearrange("p (g t) -> p g t", g=8),
                  P["s1_hbm"][:, :, c0:c0 + 512], writes=("s1c",))
            b.dma("sync", x1c[:].rearrange("p (g t) -> p g t", g=8),
                  P["x1_hbm"][:, :, c0:c0 + 512], writes=("x1c",))
            for og in range(8):
                b.op("vector", lambda E, og=og: E.tensor_mul(
                    gtile[:, og * 512:(og + 1) * 512],
                    gtile[:, og * 512:(og + 1) * 512],
                    s1c[:, og * 512:(og + 1) * 512]),
                    reads=("gtile", "s1c"), writes=("gtile",))
                b.op("vector", lambda E, og=og: E.tensor_add(
                    x1c[:, og * 512:(og + 1) * 512],
                    x1c[:, og * 512:(og + 1) * 512],
                    gtile[:, og * 512:(og + 1) * 512]),
                    reads=("gtile", "x1c"), writes=("x1c",))
            for tt in range(4):
                for jj in (0, 4):
                    for j in range(jj, jj + 4):
                        b.op("tensor", lambda E, j=j, jj=jj, tt=tt: E.transpose(
                            ps8[1][:, (j - jj) * 128:(j - jj) * 128 + 128],
                            x1c[:, j * 512 + tt * 128:j * 512 + (tt + 1) * 128],
                            eye[:]),
                            reads=("x1c", "eye"), writes=("ps1",))
                    for j in range(jj, jj + 4):
                        b.op("scalar", lambda E, j=j, jj=jj: E.activation(
                            outtok[:, j * 128:(j + 1) * 128],
                            ps8[1][:, (j - jj) * 128:(j - jj) * 128 + 128], AF.Copy),
                            reads=("ps1",), writes=("outtok",))
                b.dma("sync", P["out_hbm"][c0 + tt * 128:c0 + (tt + 1) * 128, :],
                      outtok[:], reads=("outtok",))




def emit_counts_mk(b, nc, P, st):
    NTT = 4
    if True:
        e = st.enter_context
        xTc_ = [e(nc.sbuf_tensor(f"N_xTc{i}", [128, 8 * 512], F32)) for i in range(4)]
        wr2 = e(nc.sbuf_tensor("N_wr2", [128, E0 * 8], F32))
        onesN = e(nc.sbuf_tensor("N_ones", [128, 128], F32))
        srow_ = [e(nc.sbuf_tensor(f"N_srow{i}", [1, 4 * 512], F32)) for i in range(4)]
        Srep = e(nc.sbuf_tensor("N_Srep", [128, E0], F32))
        brrep = e(nc.sbuf_tensor("N_brrep", [128, E0], F32))
        Srow = e(nc.sbuf_tensor("N_Srow", [1, E0], F32))
        mucol_ = [e(nc.sbuf_tensor(f"N_mucol{i}", [128, 2], F32)) for i in range(4)]
        ltok_ = [e(nc.sbuf_tensor(f"N_ltok{i}", [128, E0], F32)) for i in range(4)]
        o1_ = [e(nc.sbuf_tensor(f"N_o1{i}", [128, E0], F32)) for i in range(4)]
        o2_ = [e(nc.sbuf_tensor(f"N_o2{i}", [128, E0], F32)) for i in range(4)]
        scr_ = [e(nc.sbuf_tensor(f"N_scr{i}", [128, E0], F32)) for i in range(4)]
        stat_ = [e(nc.sbuf_tensor(f"N_stat{i}", [128, 8], F32)) for i in range(4)]
        onecol = e(nc.sbuf_tensor("N_onecol", [128, E0], F32))
        pm64 = e(nc.sbuf_tensor("N_pm64", [1, 64], F32))
        acnt = e(nc.sbuf_tensor("N_acnt", [1, 64], F32))
        offsr = e(nc.sbuf_tensor("N_offsr", [1, E0], F32))
        cval = e(nc.sbuf_tensor("N_cval", [128, 1], F32))
        ps_sqx_ = [e(nc.psum_tensor(f"N_ps_sq{i}", [64, 512], F32)) for i in range(2)]
        ps_l2x_ = [e(nc.psum_tensor(f"N_ps_l2{i}", [128, 32], F32)) for i in range(2)]
        ps_mx_ = [e(nc.psum_tensor(f"N_ps_misc{i}", [128, 32], F32)) for i in range(2)]
        ps_b = ps_mx_[0][:, 16:24]

        b.dma("sync", wr2[:], P["wr_in"], writes=("Nwr",))
        b.dma("sync", onesN[:], P["ones_in"], writes=("Nones",))
        b.dma("sync", pm64[:], P["pm64_in"], writes=("Npm",))
        b.dma("sync", brrep[0:1, :], P["br_in"].rearrange("(o e) -> o e", o=1),
              writes=("Nbr",))
        b.op("vector", lambda E: E.memset(acnt[:], 0.0), writes=("Nacnt",))
        b.op("vector", lambda E: E.memset(onecol[:], 1.0), writes=("Nonecol",))
        b.op("vector", lambda E: E.memset(cval[:], 1.0), writes=("Ncval",))
        b.op("tensor", lambda E: E.matmul(ps_sqx_[0][0:1, 0:64], onesN[:, 0:1], wr2[:],
                                          start=True, stop=True),
             reads=("Nones", "Nwr"), writes=("Nps_sq0",))
        b.op("vector", lambda E: E.tensor_copy(srow_[0][:, 0:64], ps_sqx_[0][0:1, 0:64]),
             reads=("Nps_sq0",), writes=("Nsrow0",))
        b.op("vector", lambda E: E.tensor_reduce(
            Srow[:], srow_[0][0:1, 0:64].rearrange("p (k e) -> p e k", k=8),
            axis=AX.X, op=ALU.add),
            reads=("Nsrow0",), writes=("NSrow",))
        b.op("tensor", lambda E: E.matmul(ps_b[:], onesN[0:1, :], Srow[:],
                                          start=True, stop=True),
             reads=("Nones", "NSrow"), writes=("Nps_misc0",))
        b.op("scalar", lambda E: E.activation(Srep[:], ps_b[:], AF.Copy),
             reads=("Nps_misc0",), writes=("NSrep",))
        b.op("tensor", lambda E: E.matmul(ps_b[:], onesN[0:1, :], brrep[0:1, :],
                                          start=True, stop=True),
             reads=("Nones", "Nbr"), writes=("Nps_misc0",))
        b.op("scalar", lambda E: E.activation(brrep[:], ps_b[:], AF.Copy),
             reads=("Nps_misc0",), writes=("Nbrrep",))

        def top1(dst, dstk, src_t, srck, gcol, stat, scr, Z):
            b.op("vector", lambda E: E.reduce_max(
                stat[:, gcol:gcol + 1], src_t[:], axis=AX.X),
                reads=(srck,), writes=("Nstat" + Z,))
            b.op("vector", lambda E: E.tensor_scalar(
                dst[:], src_t[:], stat[:, gcol:gcol + 1], None, ALU.is_equal),
                reads=(srck, "Nstat" + Z), writes=(dstk,))
            b.op("vector", lambda E: E.tensor_tensor_scan(
                scr[:], onecol[:], dst[:], 0.0, ALU.mult, ALU.add),
                reads=(dstk, "Nonecol"), writes=("Nscr" + Z,))
            b.op("vector", lambda E: E.tensor_scalar(
                scr[:], scr[:], 1.0, None, ALU.is_equal),
                reads=("Nscr" + Z,), writes=("Nscr" + Z,))
            b.op("vector", lambda E: E.tensor_mul(dst[:], dst[:], scr[:]),
                 reads=(dstk, "Nscr" + Z), writes=(dstk,))

        def _c_s0(bo, ci, c0, xTc, srow, mucol, ltok, o1, o2, scr, stat,
                  ps_s, ps_q, ps_l2, ps_m, ps_c, Z, Z2):
            if True:
                b.dma("sync", xTc[:].rearrange("p (g t) -> p g t", g=8),
                      P["xTf_in"][bo].rearrange("(g p) t -> p g t", p=128)[:, :, c0:c0 + 512],
                      writes=("NxTc" + Z,))
                for g in range(8):
                    b.op("tensor", lambda E, g=g: E.matmul(
                        ps_s[:], onesN[:, 0:1], xTc[:, g * 512:(g + 1) * 512],
                        start=(g == 0), stop=(g == 7)),
                        reads=("Nones", "NxTc" + Z),
                        writes=("Nps_sq" + Z2,) if g in (0, 7) else (),
                        inc=1 if g in (0, 7) else 0)

        def _c_s1(bo, ci, c0, xTc, srow, mucol, ltok, o1, o2, scr, stat,
                  ps_s, ps_q, ps_l2, ps_m, ps_c, Z, Z2):
            if True:
                for tt in range(NTT):
                    for k in range(8):
                        b.op("tensor", lambda E, tt=tt, k=k: E.matmul(
                            ps_l2[tt][:], xTc[:, k * 512 + tt * 128:k * 512 + (tt + 1) * 128],
                            wr2[:, k * 8:(k + 1) * 8],
                            start=(k == 0), stop=(k == 7)),
                            reads=("NxTc" + Z, "Nwr"),
                            writes=("Nps_l2" + Z2,) if k in (0, 7) else (),
                            inc=1 if k in (0, 7) else 0)

        def _c_s2(bo, ci, c0, xTc, srow, mucol, ltok, o1, o2, scr, stat,
                  ps_s, ps_q, ps_l2, ps_m, ps_c, Z, Z2):
            if True:
                b.op("scalar", lambda E: E.activation(xTc[:], xTc[:], AF.Square),
                     reads=("NxTc" + Z,), writes=("NxTc" + Z,))
                for g in range(8):
                    b.op("tensor", lambda E, g=g: E.matmul(
                        ps_q[:], onesN[:, 0:1], xTc[:, g * 512:(g + 1) * 512],
                        start=(g == 0), stop=(g == 7)),
                        reads=("Nones", "NxTc" + Z),
                        writes=("Nps_sq" + Z2,) if g in (0, 7) else (),
                        inc=1 if g in (0, 7) else 0)

        def _c_s3(bo, ci, c0, xTc, srow, mucol, ltok, o1, o2, scr, stat,
                  ps_s, ps_q, ps_l2, ps_m, ps_c, Z, Z2):
            if True:
                b.op("scalar", lambda E: E.activation(
                    srow[:, 0:512], ps_s[:], AF.Copy, scale=1.0 / D),
                    reads=("Nps_sq" + Z2,), writes=("Nsrow" + Z,))
                b.op("scalar", lambda E: E.activation(
                    srow[:, 512:1024], ps_q[:], AF.Copy, scale=1.0 / D),
                    reads=("Nps_sq" + Z2,), writes=("Nsrow" + Z,))
                b.op("vector", lambda E: E.tensor_mul(
                    srow[:, 1024:1536], srow[:, 0:512], srow[:, 0:512]),
                    reads=("Nsrow" + Z,), writes=("Nsrow" + Z,))
                b.op("vector", lambda E: E.tensor_sub(
                    srow[:, 1024:1536], srow[:, 512:1024], srow[:, 1024:1536]),
                    reads=("Nsrow" + Z,), writes=("Nsrow" + Z,))
                b.op("scalar", lambda E: E.activation(
                    srow[:, 1536:2048], srow[:, 1024:1536], AF.Ln,
                    bias=P["cval_eps"]),
                    reads=("Nsrow" + Z,), writes=("Nsrow" + Z,))
                b.op("scalar", lambda E: E.activation(
                    srow[:, 1536:2048], srow[:, 1536:2048], AF.Exp, scale=-0.5),
                    reads=("Nsrow" + Z,), writes=("Nsrow" + Z,))

        def _c_s4(bo, ci, c0, xTc, srow, mucol, ltok, o1, o2, scr, stat,
                  ps_s, ps_q, ps_l2, ps_m, ps_c, Z, Z2):
            if True:
                for tt in range(NTT):
                    b.op("tensor", lambda E, tt=tt: E.matmul(
                        ps_m[:, 0:1], srow[0:1, tt * 128:(tt + 1) * 128],
                        cval[0:1, 0:1], start=True, stop=False),
                        reads=("Nsrow" + Z, "Ncval"), writes=("Nps_misc" + Z2,))
                    b.op("tensor", lambda E, tt=tt: E.matmul(
                        ps_m[:, 1:2], srow[0:1, 1536 + tt * 128:1536 + (tt + 1) * 128],
                        cval[0:1, 0:1], start=False, stop=True),
                        reads=("Nsrow" + Z, "Ncval"), writes=("Nps_misc" + Z2,))
                    b.op("scalar", lambda E: E.activation(mucol[:], ps_m[:], AF.Copy),
                         reads=("Nps_misc" + Z2,), writes=("Nmucol" + Z,))
                    b.op("vector", lambda E, tt=tt: E.scalar_tensor_tensor(
                        ltok[:], Srep[:], mucol[:, 0:1], ps_l2[tt][:],
                        ALU.mult, ALU.subtract),
                        reads=("NSrep", "Nmucol" + Z, "Nps_l2" + Z2), writes=("Nltok" + Z,))
                    b.op("vector", lambda E: E.tensor_scalar(
                        ltok[:], ltok[:], mucol[:, 1:2], -1.0, ALU.mult, ALU.mult),
                        reads=("Nltok" + Z, "Nmucol" + Z), writes=("Nltok" + Z,))
                    b.op("vector", lambda E: E.tensor_add(ltok[:], ltok[:], brrep[:]),
                         reads=("Nltok" + Z, "Nbrrep"), writes=("Nltok" + Z,))
                    top1(o1, "No1" + Z, ltok, "Nltok" + Z, 4, stat, scr, Z)
                    b.op("vector", lambda E: E.scalar_tensor_tensor(
                        ltok[:], o1[:], -1e30, ltok[:], ALU.mult, ALU.add),
                        reads=("No1" + Z, "Nltok" + Z), writes=("Nltok" + Z,))
                    top1(o2, "No2" + Z, ltok, "Nltok" + Z, 5, stat, scr, Z)
                    b.op("vector", lambda E: E.tensor_add(o1[:], o1[:], o2[:]),
                         reads=("No1" + Z, "No2" + Z), writes=("No1" + Z,))
                    b.op("tensor", lambda E: E.matmul(
                        ps_c[:], onesN[:, 0:1], o1[:], start=True, stop=True),
                        reads=("Nones", "No1" + Z), writes=("Nps_misc" + Z2,))
                    b.op("vector", lambda E, bo=bo: E.tensor_add(
                        acnt[0:1, bo * 8:(bo + 1) * 8],
                        acnt[0:1, bo * 8:(bo + 1) * 8], ps_c[:]),
                        reads=("Nps_misc" + Z2, "Nacnt"), writes=("Nacnt",))

        def cargs(bo, ci):
            pz = (bo * 4 + ci) % 4
            pz2 = (bo * 4 + ci) % 2
            return (bo, ci, ci * 512, xTc_[pz], srow_[pz], mucol_[pz],
                    ltok_[pz], o1_[pz], o2_[pz], scr_[pz], stat_[pz],
                    ps_sqx_[pz2][0:1, 0:512], ps_sqx_[pz2][32:33, 0:512],
                    [ps_l2x_[pz2][:, i * 8:(i + 1) * 8] for i in range(4)],
                    ps_mx_[pz2][:, 0:2], ps_mx_[pz2][0:1, 8:16],
                    f"{pz}", f"{pz2}")

        def do_bo(bo):
            for base in (0, 2):
                arglist = [cargs(bo, base), cargs(bo, base + 1)]
                for st_fn in (_c_s0, _c_s1, _c_s2, _c_s3, _c_s4):
                    for a in arglist:
                        st_fn(*a)

        def finale():
            b.op("vector", lambda E: E.tensor_mul(acnt[:], acnt[:], pm64[:]),
                 reads=("Nacnt", "Npm"), writes=("Nacnt",))
            b.op("vector", lambda E: E.tensor_reduce(
                offsr[:], acnt[0:1, 0:64].rearrange("p (k e) -> p e k", k=8),
                axis=AX.X, op=ALU.add),
                reads=("Nacnt",), writes=("Noffsr",))
            b.op("tensor", lambda E: E.matmul(ps_b[:], onesN[0:1, :], offsr[:],
                                              start=True, stop=True),
                 reads=("Nones", "Noffsr"), writes=("Nps_misc0",))
            b.op("scalar", lambda E: E.activation(P["offs_rep"][:], ps_b[:], AF.Copy),
                 reads=("Nps_misc0",), writes=("offs_rep",))


        return do_bo, finale

def build_fused():
    nc = bass.Bass("TRN2", target_bir_lowering=False, debug=False)
    ap = lambda name, shape, dt_, kind: nc.dram_tensor(name, shape, dt_, kind=kind).ap()
    P = {}
    P["x_in"] = ap("x", [T, D], F32, "ExternalInput")
    P["wr_in"] = ap("w_router", [128, E0 * 8], F32, "ExternalInput")
    P["br_in"] = ap("b_router", [E0], F32, "ExternalInput")
    P["eye_in"] = ap("eye", [128, 128], F32, "ExternalInput")
    P["triu_in"] = ap("triu", [128, 128], F32, "ExternalInput")
    P["ones_in"] = ap("ones", [128, 128], F32, "ExternalInput")
    P["wip_in"] = ap("wipT", [D, 4096], F32, "ExternalInput")
    P["wxp_in"] = ap("wxpP", [128, 16 * 96], F32, "ExternalInput")
    P["wdt_in"] = ap("wdtT", [64, DI], F32, "ExternalInput")
    P["wop_in"] = ap("wopT", [DI, D], F32, "ExternalInput")
    P["wglu_in"] = ap("wgluT", [D, DI], F32, "ExternalInput")
    P["conv_in"] = ap("convP", [128, 64], F32, "ExternalInput")
    P["convb_in"] = ap("convbP", [128, 16], F32, "ExternalInput")
    P["dtb_in"] = ap("dtbP", [128, 16], F32, "ExternalInput")
    P["alog_in"] = ap("alogP", [128, 256], F32, "ExternalInput")
    P["dv_in"] = ap("DP", [128, 16], F32, "ExternalInput")
    P["glub_in"] = ap("glubP", [128, 16], F32, "ExternalInput")
    P["lng_in"] = ap("lngP", [128, 8], F32, "ExternalInput")
    P["lnb_in"] = ap("lnbP", [128, 8], F32, "ExternalInput")
    P["sel_in"] = ap("sel16", [16, 16 * 128], F32, "ExternalInput")
    P["pm64_in"] = ap("pm64", [1, 64], F32, "ExternalInput")
    P["xTf_in"] = ap("xTf", [NCORE, D, T], F32, "ExternalInput")
    P["w1_in"] = ap("w1", [E0, D, 4096], F32, "ExternalInput")
    P["b1_in"] = ap("b1P", [E0, 128, 32], F32, "ExternalInput")
    P["w2_in"] = ap("w2", [E0, 4096, D], F32, "ExternalInput")
    P["b2_in"] = ap("b2", [E0, D], F32, "ExternalInput")
    P["sel8_in"] = ap("sel8", [E0, E0 * 128], F32, "ExternalInput")
    P["qkw_in"] = ap("qk_w", [D, 2 * D], F32, "ExternalInput")
    P["qkb_in"] = ap("qk_b", [2 * D], F32, "ExternalInput")
    P["lepe_in"] = ap("lepe_w", [3, D, D], F32, "ExternalInput")
    P["lepeb_in"] = ap("lepebP", [128, 8], F32, "ExternalInput")
    P["projP_in"] = ap("projP", [128, 48], F32, "ExternalInput")
    P["anchT_in"] = ap("anchT", [6, 6], F32, "ExternalInput")
    P["temp_in"] = ap("rtemp", [1, 1], F32, "ExternalInput")
    P["w1f_in"] = ap("w1f", [E1, D, DE1], F32, "ExternalInput")
    P["w2f_in"] = ap("w2f", [E1, DE1, D], F32, "ExternalInput")
    xnb_t = nc.dram_tensor("i_xnT_b", [D, T], BF16, kind="Internal")
    P["xnTb_hbm"] = xnb_t.ap().rearrange("(g p) t -> p g t", p=128)
    sT_t = nc.dram_tensor("i_sT", [D, T], BF16, kind="Internal")
    P["sT_hbm"] = sT_t.ap().rearrange("(g p) t -> p g t", p=128)
    gT_t = nc.dram_tensor("i_gT", [D, T], BF16, kind="Internal")
    P["gT_ap"] = gT_t.ap()
    P["gT_hbm"] = gT_t.ap().rearrange("(g p) t -> p g t", p=128)
    x1_t = nc.dram_tensor("i_x1T", [D, T], F32, kind="Internal")
    P["x1_hbm"] = x1_t.ap().rearrange("(g p) t -> p g t", p=128)
    x1n_t = nc.dram_tensor("i_x1nT_b", [D, T], BF16, kind="Internal")
    P["x1n_hbm"] = x1n_t.ap().rearrange("(g p) t -> p g t", p=128)
    s1_t = nc.dram_tensor("i_s1T", [D, T], BF16, kind="Internal")
    P["s1_hbm"] = s1_t.ap().rearrange("(g p) t -> p g t", p=128)
    P["out_hbm"] = ap("out", [T, D], F32, "ExternalOutput")

    b = B(nc)
    with contextlib.ExitStack() as top:
        route_sb = top.enter_context(nc.sbuf_tensor("route_sb", [128, 16 * 20], F32))
        P["route_sb"] = route_sb
        P["offs_rep"] = top.enter_context(nc.sbuf_tensor("offs_rep", [128, E0], F32))
        cve = top.enter_context(nc.sbuf_tensor("cval_eps", [1, 1], F32))
        P["cval_eps"] = cve[0:1, 0:1]
        b.op("vector", lambda E: E.memset(cve[:], 1e-5), writes=("cval_eps",))
        emit_A(b, nc, P)
        b.barrier()
        with contextlib.ExitStack() as cstk:
            do_bo, cnt_finale = emit_counts_mk(b, nc, P, cstk)
            for bo in range(NCORE):
                do_bo(bo)
            cnt_finale()
        b.barrier()
        with contextlib.ExitStack() as bstk:
            do_chunk = emit_B_mk(b, nc, P, bstk)
            for ci in range(4):
                do_chunk(ci)
        b.barrier()
        emit_C(b, nc, P)
        b.barrier()
        emit_D(b, nc, P)
        b.barrier()
        emit_E(b, nc, P)
        b.barrier()
        emit_F(b, nc, P)
        b.finish_into(top)
    return nc


def kernel(**inputs):
    cst = host_prep(inputs)
    x_full = np.ascontiguousarray(inputs["L0_attn_x"])
    ncX = build_fused()
    in_maps = []
    for core in range(NCORE):
        pm64 = np.zeros((1, 64), np.float32)
        pm64[0, :core * 8] = 1.0
        m = {"x": np.ascontiguousarray(x_full[core]),
             "w_router": cst["w_router_p"],
             "b_router": np.ascontiguousarray(inputs["L0_ffn_b_router"].astype(np.float32)),
             "eye": cst["eye"], "triu": cst["triu"], "ones": cst["ones"],
             "pm64": pm64, "xTf": cst["xTf"]}
        for k in ["wipT", "wxpP", "wdtT", "wopT", "wgluT", "convP", "convbP",
                  "dtbP", "alogP", "DP", "glubP", "lngP", "lnbP", "sel16",
                  "w1c", "b1P", "w2c", "b2c", "sel8", "qk_w", "qk_b", "lepe_w",
                  "lepebP", "projP", "anchT", "rtemp", "w1f", "w2f"]:
            name = {"w1c": "w1", "w2c": "w2", "b2c": "b2"}.get(k, k)
            m[name] = cst[k]
        in_maps.append(m)
    res = run_bass_kernel_spmd(ncX, in_maps, core_ids=list(range(NCORE)))
    out = np.stack([np.asarray(res.results[c]["out"]) for c in range(NCORE)], 0)
    return out.astype(np.float32)


# version 16 (final)
# speedup vs baseline: 1.0409x; 1.0085x over previous
import contextlib
import numpy as np
import ml_dtypes
import concourse.bass as bass
import concourse.mybir as mybir
from concourse.bass_utils import run_bass_kernel_spmd

F32 = mybir.dt.float32
BF16 = mybir.dt.bfloat16
I32 = mybir.dt.int32
AF = mybir.ActivationFunctionType
ALU = mybir.AluOpType
AX = mybir.AxisListType
BF = ml_dtypes.bfloat16

T = 2048
D = 1024
DI = 2048
NS = 16
NCORE = 8
CAP = 2560
E0 = 8
E1 = 6
DE1 = 512
NH = 8
HD = 128
TWOPI = 6.283185307179586
ENGS = ("sync", "tensor", "vector", "scalar", "gpsimd")


class B:

    NPOOL = 12

    def __init__(self, nc):
        self.nc = nc
        self.ops = {e: [] for e in ENGS}
        self.cnt = {e: 0 for e in ENGS}
        self.dma_keys = [f"dma:{e}:{r}" for e in ("sync", "gpsimd")
                         for r in range(self.NPOOL)]
        for k in self.dma_keys:
            self.cnt[k] = 0
        self.rotor = {"sync": 0, "gpsimd": 0}
        self.wm = {e: {o: 0 for o in list(ENGS) + self.dma_keys} for e in ENGS}
        self.regions = {}
        self.pending_reads = {e: [] for e in ENGS}

    def _need(self, eng, waits, dep):
        deng, dcnt = dep
        if dcnt <= self.wm[eng][deng]:
            return
        waits.append((deng, dcnt))
        self.wm[eng][deng] = dcnt

    def op(self, eng, emit, reads=(), writes=(), inc=1):
        waits = []
        for r in reads:
            st = self.regions.get(r)
            if st and st["w"]:
                self._need(eng, waits, st["w"])
        for w in writes:
            st = self.regions.get(w)
            if st:
                if st["w"]:
                    self._need(eng, waits, st["w"])
                for reng, rcnt in st["r"].items():
                    self._need(eng, waits, (reng, rcnt))
        self.cnt[eng] += inc
        after = self.cnt[eng]
        self.ops[eng].append((waits, emit, inc))
        if inc:
            for r in list(self.pending_reads[eng]) + list(reads):
                st = self.regions.setdefault(r, {"w": None, "r": {}})
                st["r"][eng] = max(st["r"].get(eng, 0), after)
            self.pending_reads[eng] = []
            for w in writes:
                self.regions[w] = {"w": (eng, after), "r": {}}
        else:
            self.pending_reads[eng].extend(reads)
            assert not writes, "quiet ops must not declare writes"
        return after

    def barrier(self):
        snap = dict(self.cnt)
        for eng in ENGS:
            waits = []
            for o in list(ENGS) + self.dma_keys:
                if o != eng and snap[o] > self.wm[eng][o]:
                    waits.append((o, snap[o]))
                    self.wm[eng][o] = snap[o]
            self.ops[eng].append((waits, lambda E: E.nop(), 0))

    def dma(self, eng, out, in_, reads=(), writes=(), **kw):
        r = self.rotor[eng]
        self.rotor[eng] = (r + 1) % self.NPOOL
        key = f"dma:{eng}:{r}"
        waits = []
        if self.cnt[key] > 0:
            self._need(eng, waits, (key, self.cnt[key]))
        for rr in reads:
            st = self.regions.get(rr)
            if st and st["w"]:
                self._need(eng, waits, st["w"])
        for w in writes:
            st = self.regions.get(w)
            if st:
                if st["w"]:
                    self._need(eng, waits, st["w"])
                for reng, rcnt in st["r"].items():
                    self._need(eng, waits, (reng, rcnt))
        self.cnt[key] += 16
        after = self.cnt[key]
        self.ops[eng].append((waits, lambda E: E.dma_start(out=out, in_=in_, **kw),
                              ("dma", key)))
        for rr in list(self.pending_reads[eng]) + list(reads):
            st = self.regions.setdefault(rr, {"w": None, "r": {}})
            st["r"][key] = max(st["r"].get(key, 0), after)
        self.pending_reads[eng] = []
        for w in writes:
            self.regions[w] = {"w": (key, after), "r": {}}
        return after

    def finish(self):
        nc = self.nc
        with contextlib.ExitStack() as st:
            sems = {e: st.enter_context(nc.semaphore(f"s_{e}")) for e in ENGS}
            st.enter_context(nc.Block()).__class__
        raise RuntimeError("use finish_into instead")

    def finish_into(self, stack):
        nc = self.nc
        sems = {e: stack.enter_context(nc.semaphore(f"s_{e}")) for e in ENGS}
        for k in self.dma_keys:
            sems[k] = stack.enter_context(
                nc.semaphore("s_" + k.replace(":", "_")))
        blk = stack.enter_context(nc.Block())
        final = {e: self.cnt[e] for e in list(ENGS) + self.dma_keys}
        for eng in ENGS:
            ops = self.ops[eng]

            def section(E, eng=eng, ops=ops):
                for waits, emit, inc in ops:
                    for deng, dcnt in waits:
                        E.wait_ge(sems[deng], dcnt)
                    ins = emit(E)
                    if isinstance(inc, tuple):
                        ins.then_inc(sems[inc[1]], 16)
                    elif inc:
                        ins.then_inc(sems[eng], inc)
                if eng == "sync":
                    for o in list(ENGS) + self.dma_keys:
                        if final[o]:
                            E.wait_ge(sems[o], final[o])

            getattr(blk, eng)(section)


def pack_rows(w):
    G = w.shape[0] // 128
    return np.ascontiguousarray(
        w.reshape(G, 128, w.shape[1]).transpose(1, 0, 2).reshape(128, -1))


def host_prep(inputs):
    f32c = lambda a: np.ascontiguousarray(a.astype(np.float32))
    c = {}
    c["eye"] = np.eye(128, dtype=np.float32)
    c["triu"] = np.triu(np.ones((128, 128), np.float32), 1)
    c["ones"] = np.ones((128, 128), np.float32)
    c["w_router_p"] = pack_rows(f32c(inputs["L0_ffn_w_router"]))
    c["wipT"] = f32c(inputs["L0_attn_in_proj_w"].T)
    c["wxpP"] = pack_rows(f32c(inputs["L0_attn_x_proj_w"].T))
    c["wdtT"] = f32c(inputs["L0_attn_dt_proj_w"].T)
    c["wopT"] = f32c(inputs["L0_attn_out_proj_w"].T)
    c["wgluT"] = f32c(inputs["L0_attn_glu_w"].T)
    c["convP"] = pack_rows(f32c(inputs["L0_attn_conv_w"][:, 0, :]))
    c["convbP"] = pack_rows(f32c(inputs["L0_attn_conv_b"][:, None]))
    c["dtbP"] = pack_rows(f32c(inputs["L0_attn_dt_proj_b"][:, None]))
    c["alogP"] = pack_rows(f32c(inputs["L0_attn_A_log"]))
    c["DP"] = pack_rows(f32c(inputs["L0_attn_D"][:, None]))
    c["glubP"] = pack_rows(f32c(inputs["L0_attn_glu_b"][:, None]))
    c["lngP"] = pack_rows(f32c(inputs["L0_attn_ln_g"][:, None]))
    c["lnbP"] = pack_rows(f32c(inputs["L0_attn_ln_b"][:, None]))
    sel16 = np.zeros((16, 16 * 128), np.float32)
    for n in range(16):
        sel16[n, n * 128:(n + 1) * 128] = 1.0
    c["sel16"] = sel16
    c["w1c"] = f32c(inputs["L0_ffn_w1"])
    c["w2c"] = f32c(inputs["L0_ffn_w2"])
    c["b1P"] = np.ascontiguousarray(
        inputs["L0_ffn_b1"].astype(np.float32).reshape(E0, 32, 128).transpose(0, 2, 1))
    c["b2c"] = f32c(inputs["L0_ffn_b2"])
    sel8 = np.zeros((E0, E0 * 128), np.float32)
    for n in range(E0):
        sel8[n, n * 128:(n + 1) * 128] = 1.0
    c["sel8"] = sel8
    c["qk_w"] = f32c(inputs["L1_attn_qk_w"])
    c["qk_b"] = f32c(inputs["L1_attn_qk_b"])
    c["lepe_w"] = f32c(inputs["L1_attn_lepe_w"])
    c["lepebP"] = pack_rows(f32c(inputs["L1_attn_lepe_b"][:, None]))
    c["projP"] = pack_rows(f32c(inputs["L1_ffn_proj_w"].T))
    c["anchT"] = f32c(inputs["L1_ffn_anchors"].T)
    c["rtemp"] = np.asarray(inputs["L1_ffn_routing_temp"], np.float32).reshape(1, 1)
    c["w1f"] = f32c(inputs["L1_ffn_w1"])
    c["w2f"] = f32c(inputs["L1_ffn_w2"])
    c["xTf"] = np.ascontiguousarray(
        inputs["L0_attn_x"].astype(np.float32).transpose(0, 2, 1))
    return c





def emit_A(b, nc, P):
    NT = T // 128
    NT = T // 128
    with contextlib.ExitStack() as st:
        e = st.enter_context
        xt = [e(nc.sbuf_tensor(f"A_xt{i}", [128, D], F32)) for i in range(2)]
        eye = e(nc.sbuf_tensor("A_sb_eye", [128, 128], F32))
        triu = e(nc.sbuf_tensor("A_sb_triu", [128, 128], F32))
        ones = e(nc.sbuf_tensor("A_sb_ones", [128, 128], F32))
        wr = e(nc.sbuf_tensor("A_sb_wr", [128, E0 * 8], F32))
        br_rep = e(nc.sbuf_tensor("A_br_rep", [128, E0], F32))
        stat = e(nc.sbuf_tensor("A_stat", [128, 8], F32))
        xsq = e(nc.sbuf_tensor("A_xsq", [128, D], F32))
        xnf = e(nc.sbuf_tensor("A_xnf", [128, D], F32))
        xnTf = e(nc.sbuf_tensor("A_xnTf", [128, D], F32))
        xnTb = e(nc.sbuf_tensor("A_xnTb", [128, D], BF16))
        gat = e(nc.sbuf_tensor("A_gat", [128, E0], F32))
        oh1 = e(nc.sbuf_tensor("A_oh1", [128, E0], F32))
        oh2 = e(nc.sbuf_tensor("A_oh2", [128, E0], F32))
        oht = e(nc.sbuf_tensor("A_oht", [128, E0], F32))
        posx = e(nc.sbuf_tensor("A_posx", [128, E0], F32))
        scr = e(nc.sbuf_tensor("A_scr", [128, E0], F32))
        scr2 = e(nc.sbuf_tensor("A_scr2", [128, E0], F32))
        rpack = e(nc.sbuf_tensor("A_rpack", [128, 20], F32))
        carry = e(nc.sbuf_tensor("A_carry", [1, E0], F32))
        onecol = e(nc.sbuf_tensor("A_onecol", [128, E0], F32))
        cvals = e(nc.sbuf_tensor("A_cvals", [128, 2], F32))
        ps_t = e(nc.psum_tensor("A_ps_t", [128, 512], F32))
        ps_l = e(nc.psum_tensor("A_ps_l", [128, E0], F32))
        ps_p = e(nc.psum_tensor("A_ps_p", [128, E0], F32))
        ps_c = e(nc.psum_tensor("A_ps_c", [1, E0], F32))
        ps_b = e(nc.psum_tensor("A_ps_b", [128, E0], F32))

        b.dma("sync", eye[:], P["eye_in"], writes=("eye",))
        b.dma("sync", triu[:], P["triu_in"], writes=("triu",))
        b.dma("sync", ones[:], P["ones_in"], writes=("ones",))
        b.dma("sync", wr[:], P["wr_in"], writes=("wr",))
        b.dma("sync", br_rep[0:1, :], P["br_in"].rearrange("(o e) -> o e", o=1), writes=("br",))
        b.op("tensor", lambda E: E.matmul(ps_b[:], ones[0:1, :], br_rep[0:1, :],
                                          start=True, stop=True),
             reads=("ones", "br"), writes=("ps_b",))
        b.op("vector", lambda E: E.tensor_copy(br_rep[:], ps_b[:]),
             reads=("ps_b",), writes=("br_rep",))
        b.op("vector", lambda E: E.memset(carry[:], 0.0), writes=("carry",))
        b.op("vector", lambda E: E.memset(onecol[:], 1.0), writes=("onecol",))
        b.op("vector", lambda E: E.memset(cvals[:, 0:1], 1e-5), writes=("cvals",))

        for i in range(NT):
            cur = xt[i % 2]
            xk = f"xt{i % 2}"
            b.dma("sync", cur[:], P["x_in"][i * 128:(i + 1) * 128, :], writes=(xk,))
            b.op("vector", lambda E, cur=cur: E.tensor_reduce(
                stat[:, 0:1], cur[:], axis=AX.X, op=ALU.add),
                reads=(xk,), writes=("stat",))
            b.op("scalar", lambda E, cur=cur: E.activation(
                xsq[:], cur[:], AF.Square), reads=(xk,), writes=("xsq",))
            b.op("vector", lambda E: E.tensor_reduce(
                stat[:, 1:2], xsq[:], axis=AX.X, op=ALU.add),
                reads=("xsq",), writes=("stat",))
            b.op("vector", lambda E: E.tensor_scalar(
                stat[:, 2:3], stat[:, 0:1], 1.0 / D, None, ALU.mult),
                reads=("stat",), writes=("stat",))
            b.op("vector", lambda E: E.tensor_scalar(
                stat[:, 3:4], stat[:, 1:2], 1.0 / D, None, ALU.mult),
                reads=("stat",), writes=("stat",))
            b.op("vector", lambda E: E.tensor_mul(
                stat[:, 4:5], stat[:, 2:3], stat[:, 2:3]),
                reads=("stat",), writes=("stat",))
            b.op("vector", lambda E: E.tensor_sub(
                stat[:, 4:5], stat[:, 3:4], stat[:, 4:5]),
                reads=("stat",), writes=("stat",))
            b.op("scalar", lambda E: E.activation(
                stat[:, 5:6], stat[:, 4:5], AF.Ln, bias=cvals[:, 0:1]),
                reads=("stat", "cvals"), writes=("stat",))
            b.op("scalar", lambda E: E.activation(
                stat[:, 5:6], stat[:, 5:6], AF.Exp, scale=-0.5),
                reads=("stat",), writes=("stat",))
            b.op("vector", lambda E: E.tensor_mul(
                stat[:, 6:7], stat[:, 2:3], stat[:, 5:6]),
                reads=("stat",), writes=("stat",))
            b.op("vector", lambda E: E.tensor_scalar(
                stat[:, 6:7], stat[:, 6:7], -1.0, None, ALU.mult),
                reads=("stat",), writes=("stat",))
            b.op("vector", lambda E, cur=cur: E.tensor_scalar(
                xnf[:], cur[:], stat[:, 5:6], stat[:, 6:7], ALU.mult, ALU.add),
                reads=(xk, "stat"), writes=("xnf",))
            for jj in (0, 4):
                for j in range(jj, jj + 4):
                    b.op("tensor", lambda E, j=j, jj=jj: E.transpose(
                        ps_t[:, (j - jj) * 128:(j - jj) * 128 + 128],
                        xnf[:, j * 128:(j + 1) * 128], eye[:]),
                        reads=("xnf", "eye"), writes=("ps_t",))
                for j in range(jj, jj + 4):
                    b.op("scalar", lambda E, j=j, jj=jj: E.activation(
                        xnTf[:, j * 128:(j + 1) * 128],
                        ps_t[:, (j - jj) * 128:(j - jj) * 128 + 128], AF.Copy),
                        reads=("ps_t",), writes=("xnTf",))
            b.op("vector", lambda E: E.tensor_copy(xnTb[:], xnTf[:]),
                 reads=("xnTf",), writes=("xnTb",))
            b.dma("sync", P["xnTb_hbm"][:, :, i * 128:(i + 1) * 128],
                  xnTb[:].rearrange("p (g t) -> p g t", g=8), reads=("xnTb",))
            for k in range(8):
                b.op("tensor", lambda E, k=k: E.matmul(
                    ps_l[:], xnTf[:, k * 128:(k + 1) * 128],
                    wr[:, k * 8:(k + 1) * 8],
                    start=(k == 0), stop=(k == 7)),
                    reads=("xnTf", "wr"),
                    writes=("ps_l",) if k in (0, 7) else (),
                    inc=1 if k in (0, 7) else 0)
            b.op("vector", lambda E: E.tensor_add(gat[:], ps_l[:], br_rep[:]),
                 reads=("ps_l", "br_rep"), writes=("gat",))

            def top1(dst, dstk, src, srck, gcol):
                b.op("vector", lambda E: E.reduce_max(
                    stat[:, gcol:gcol + 1], src[:], axis=AX.X),
                    reads=(srck,), writes=("stat",))
                b.op("vector", lambda E: E.tensor_scalar(
                    dst[:], src[:], stat[:, gcol:gcol + 1], None, ALU.is_equal),
                    reads=(srck, "stat"), writes=(dstk,))
                b.op("vector", lambda E: E.tensor_tensor_scan(
                    scr[:], onecol[:], dst[:], 0.0, ALU.mult, ALU.add),
                    reads=(dstk, "onecol"), writes=("scr",))
                b.op("vector", lambda E: E.tensor_scalar(
                    scr[:], scr[:], 1.0, None, ALU.is_equal),
                    reads=("scr",), writes=("scr",))
                b.op("vector", lambda E: E.tensor_mul(dst[:], dst[:], scr[:]),
                     reads=(dstk, "scr"), writes=(dstk,))

            top1(oh1, "oh1", gat, "gat", 0)
            b.op("vector", lambda E: E.scalar_tensor_tensor(
                scr2[:], oh1[:], -1e30, gat[:], ALU.mult, ALU.add),
                reads=("oh1", "gat"), writes=("scr2",))
            top1(oh2, "oh2", scr2, "scr2", 1)
            b.op("vector", lambda E: E.tensor_add(oht[:], oh1[:], oh2[:]),
                 reads=("oh1", "oh2"), writes=("oht",))
            b.op("vector", lambda E: E.reduce_max(stat[:, 7:8], gat[:], axis=AX.X),
                 reads=("gat",), writes=("stat",))
            b.op("vector", lambda E: E.tensor_scalar(
                stat[:, 7:8], stat[:, 7:8], -1.0, None, ALU.mult),
                reads=("stat",), writes=("stat",))
            b.op("scalar", lambda E: E.activation(
                gat[:], gat[:], AF.Exp, bias=stat[:, 7:8]),
                reads=("gat", "stat"), writes=("gat",))
            b.op("vector", lambda E: E.tensor_reduce(
                stat[:, 7:8], gat[:], axis=AX.X, op=ALU.add),
                reads=("gat",), writes=("stat",))
            b.op("vector", lambda E: E.reciprocal(stat[:, 7:8], stat[:, 7:8]),
                 reads=("stat",), writes=("stat",))
            b.op("vector", lambda E: E.tensor_scalar(
                gat[:], gat[:], stat[:, 7:8], None, ALU.mult),
                reads=("gat", "stat"), writes=("gat",))
            b.op("vector", lambda E: E.tensor_mul(scr[:], gat[:], oh1[:]),
                 reads=("gat", "oh1"), writes=("scr",))
            b.op("vector", lambda E: E.tensor_reduce(
                stat[:, 0:1], scr[:], axis=AX.X, op=ALU.add),
                reads=("scr",), writes=("stat",))
            b.op("vector", lambda E: E.tensor_mul(scr[:], gat[:], oh2[:]),
                 reads=("gat", "oh2"), writes=("scr",))
            b.op("vector", lambda E: E.tensor_reduce(
                stat[:, 1:2], scr[:], axis=AX.X, op=ALU.add),
                reads=("scr",), writes=("stat",))
            b.op("tensor", lambda E: E.matmul(ps_p[:], triu[:], oht[:],
                                              start=True, stop=False),
                 reads=("triu", "oht"), writes=("ps_p",))
            b.op("tensor", lambda E: E.matmul(ps_p[:], ones[0:1, :], carry[:],
                                              start=False, stop=True),
                 reads=("ones", "carry"), writes=("ps_p",))
            b.op("vector", lambda E: E.tensor_copy(posx[:], ps_p[:]),
                 reads=("ps_p",), writes=("posx",))
            b.op("tensor", lambda E: E.matmul(ps_c[:], ones[:, 0:1], oht[:],
                                              start=True, stop=True),
                 reads=("ones", "oht"), writes=("ps_c",))
            b.op("vector", lambda E: E.tensor_add(carry[:], carry[:], ps_c[:]),
                 reads=("ps_c", "carry"), writes=("carry",))
            b.op("vector", lambda E: E.tensor_mul(scr[:], posx[:], oh1[:]),
                 reads=("posx", "oh1"), writes=("scr",))
            b.op("vector", lambda E: E.tensor_reduce(
                stat[:, 2:3], scr[:], axis=AX.X, op=ALU.add),
                reads=("scr",), writes=("stat",))
            b.op("vector", lambda E: E.tensor_add(scr[:], posx[:], oh1[:]),
                 reads=("posx", "oh1"), writes=("scr",))
            b.op("vector", lambda E: E.tensor_mul(scr[:], scr[:], oh2[:]),
                 reads=("scr", "oh2"), writes=("scr",))
            b.op("vector", lambda E: E.tensor_reduce(
                stat[:, 3:4], scr[:], axis=AX.X, op=ALU.add),
                reads=("scr",), writes=("stat",))
            b.op("vector", lambda E: E.tensor_copy(rpack[:, 0:8], oh1[:]),
                 reads=("oh1",), writes=("rpack",))
            b.op("vector", lambda E: E.tensor_copy(rpack[:, 8:16], oh2[:]),
                 reads=("oh2",), writes=("rpack",))
            b.op("vector", lambda E: E.tensor_copy(rpack[:, 16:20], stat[:, 0:4]),
                 reads=("stat",), writes=("rpack",))
            b.op("vector", lambda E, i=i: E.tensor_copy(
                P["route_sb"][:, i * 20:(i + 1) * 20], rpack[:]),
                reads=("rpack",), writes=("route_sb",))



def emit_B_mk(b, nc, P, st):
    CH = 512
    NCH = T // CH
    if True:
        e = st.enter_context
        xn2 = e(nc.sbuf_tensor("B_xn2", [128, 8 * CH], BF16))
        wp = [e(nc.sbuf_tensor(f"B_wp{i}", [128, 2048], BF16)) for i in range(8)]
        wxp = e(nc.sbuf_tensor("B_wxp", [128, 16 * 96], BF16))
        wdt = e(nc.sbuf_tensor("B_wdt", [64, DI], BF16))
        cw = e(nc.sbuf_tensor("B_cw", [128, 64], F32))
        cb = e(nc.sbuf_tensor("B_cb", [128, 16], F32))
        dtbc = e(nc.sbuf_tensor("B_dtbc", [128, 16], F32))
        Aneg = e(nc.sbuf_tensor("B_Aneg", [128, 256], F32))
        Dv = e(nc.sbuf_tensor("B_Dv", [128, 16], F32))
        glub = e(nc.sbuf_tensor("B_glub", [128, 16], F32))
        lng = e(nc.sbuf_tensor("B_lng", [128, 8], F32))
        lnb = e(nc.sbuf_tensor("B_lnb", [128, 8], F32))
        sel = e(nc.sbuf_tensor("B_sel", [16, 16 * 128], F32))
        tails = e(nc.sbuf_tensor("B_tails", [128, 48], BF16))
        xpct = e(nc.sbuf_tensor("B_xpct", [128, 515], BF16))
        convacc = e(nc.sbuf_tensor("B_convacc", [128, CH], F32))
        xc = e(nc.sbuf_tensor("B_xc", [128, 16 * CH], BF16))
        siluz = e(nc.sbuf_tensor("B_siluz", [128, 16 * CH], BF16))
        dtraw = e(nc.sbuf_tensor("B_dtraw", [64, CH], BF16))
        Bmt = e(nc.sbuf_tensor("B_Bmt", [16, CH], F32))
        Cmt = e(nc.sbuf_tensor("B_Cmt", [16, CH], F32))
        brep = e(nc.sbuf_tensor("B_brep", [128, 16 * CH], BF16))
        crep = e(nc.sbuf_tensor("B_crep", [128, 16 * CH], BF16))
        qsp = e(nc.sbuf_tensor("B_qsp", [128, CH], F32))
        spa = e(nc.sbuf_tensor("B_spa", [128, CH], F32))
        spb = e(nc.sbuf_tensor("B_spb", [128, CH], F32))
        dtbg = [e(nc.sbuf_tensor(f"B_dtbg{i}", [128, CH], BF16)) for i in range(2)]
        dtu = [e(nc.sbuf_tensor(f"B_dtu{i}", [128, CH], BF16)) for i in range(2)]
        dA_t = [e(nc.sbuf_tensor(f"B_dA_t{i}", [128, CH], BF16)) for i in range(2)]
        dBu_t = [e(nc.sbuf_tensor(f"B_dBu_t{i}", [128, CH], BF16)) for i in range(2)]
        h_t = [e(nc.sbuf_tensor(f"B_h_t{i}", [128, CH], BF16)) for i in range(2)]
        hc_t = [e(nc.sbuf_tensor(f"B_hc_t{i}", [128, CH], BF16)) for i in range(2)]
        yacc = e(nc.sbuf_tensor("B_yacc", [128, CH], BF16))
        hcarry = e(nc.sbuf_tensor("B_hcarry", [128, 256], F32))
        y3 = e(nc.sbuf_tensor("B_y3", [128, 16 * CH], BF16))
        mT = e(nc.sbuf_tensor("B_mT", [128, 8 * CH], BF16))
        g1 = e(nc.sbuf_tensor("B_g1", [128, 8 * CH], BF16))
        sg2 = e(nc.sbuf_tensor("B_sg2", [128, 8 * CH], BF16))
        onescol = e(nc.sbuf_tensor("B_onescol", [128, 1], F32))
        psA2x = [e(nc.psum_tensor(f"B_psA{i}", [128, 512], F32)) for i in range(2)]
        psC = e(nc.psum_tensor("B_psC", [16, 512], F32))
        psD = e(nc.psum_tensor("B_psD", [16, 512], F32))
        psE = e(nc.psum_tensor("B_psE", [128, 512], F32))
        psBG = e(nc.psum_tensor("B_psBG", [128, 512], F32))

        b.dma("gpsimd", wxp[:], P["wxp_in"], writes=("wxp",))
        b.dma("gpsimd", wdt[:], P["wdt_in"], writes=("wdt",))
        for t_, a_, k_ in [(cw, P["conv_in"], "cw"), (cb, P["convb_in"], "cb"),
                           (dtbc, P["dtb_in"], "dtbc"), (Aneg, P["alog_in"], "Aneg"),
                           (Dv, P["dv_in"], "Dv"), (glub, P["glub_in"], "glub"),
                           (lng, P["lng_in"], "lng"), (lnb, P["lnb_in"], "lnb"),
                           (sel, P["sel_in"], "sel")]:
            b.dma("sync", t_[:], a_, writes=(k_,))
        b.op("scalar", lambda E: E.activation(Aneg[:], Aneg[:], AF.Exp),
             reads=("Aneg",), writes=("Aneg",))
        b.op("vector", lambda E: E.tensor_scalar(
            Aneg[:], Aneg[:], -1.0, None, ALU.mult),
            reads=("Aneg",), writes=("Aneg",))
        b.op("vector", lambda E: E.memset(tails[:], 0.0), writes=("tails",))
        b.op("vector", lambda E: E.memset(hcarry[:], 0.0), writes=tuple(f"hcarry{z}" for z in range(32)))
        b.op("vector", lambda E: E.memset(onescol[:], 1.0), writes=("onescol",))

        def do_chunk(ci):
            c0 = ci * CH
            b.dma("sync", xn2[:].rearrange("p (g t) -> p g t", g=8),
                  P["xnTb_hbm"][:, :, c0:c0 + CH], writes=("xn2",))
            for g in range(8):
                b.op("vector", lambda E, g=g: E.tensor_scalar(
                    xn2[:, g * CH:(g + 1) * CH], xn2[:, g * CH:(g + 1) * CH],
                    lng[:, g:g + 1], lnb[:, g:g + 1], ALU.mult, ALU.add),
                    reads=("xn2", "lng", "lnb"), writes=("xn2",))
            for half in range(2):
                for k in range(8):
                    b.dma("gpsimd", wp[k][:],
                          P["wip_in"][k * 128:(k + 1) * 128,
                                 half * 2048:half * 2048 + 2048],
                          writes=(f"wp{k}",))
                for ogl in range(16):
                    og = half * 16 + ogl
                    pA = psA2x[ogl % 2]
                    rA = f"psA{ogl % 2}"
                    for k in range(8):
                        b.op("tensor", lambda E, k=k, ogl=ogl, pA=pA: E.matmul(
                            pA[:], wp[k][:, ogl * 128:(ogl + 1) * 128],
                            xn2[:, k * CH:(k + 1) * CH],
                            start=(k == 0), stop=(k == 7)),
                            reads=(f"wp{k}", "xn2"),
                            writes=(rA,) if k in (0, 7) else (),
                            inc=1 if k in (0, 7) else 0)
                    if og < 16:
                        dg = og
                        b.op("scalar", lambda E, pA=pA: E.activation(
                            xpct[:, 3:515], pA[:], AF.Copy),
                            reads=(rA,), writes=("xpct",))
                        b.op("vector", lambda E, dg=dg: E.tensor_copy(
                            xpct[:, 0:3], tails[:, dg * 3:dg * 3 + 3]),
                            reads=("tails",), writes=("xpct",))
                        b.op("vector", lambda E, dg=dg: E.tensor_copy(
                            tails[:, dg * 3:dg * 3 + 3], xpct[:, 512:515]),
                            reads=("xpct",), writes=("tails",))
                        b.op("vector", lambda E, dg=dg: E.tensor_scalar(
                            convacc[:], xpct[:, 0:CH], cw[:, dg * 4:dg * 4 + 1],
                            None, ALU.mult),
                            reads=("xpct", "cw"), writes=("convacc",))
                        for kk in range(1, 4):
                            b.op("vector", lambda E, dg=dg, kk=kk: E.scalar_tensor_tensor(
                                convacc[:], xpct[:, kk:kk + CH],
                                cw[:, dg * 4 + kk:dg * 4 + kk + 1], convacc[:],
                                ALU.mult, ALU.add),
                                reads=("xpct", "cw", "convacc"), writes=("convacc",))
                        b.op("scalar", lambda E, dg=dg: E.activation(
                            xc[:, dg * CH:(dg + 1) * CH], convacc[:], AF.Silu,
                            bias=cb[:, dg:dg + 1]),
                            reads=("convacc", "cb"), writes=("xc",))
                    else:
                        zg = og - 16
                        b.op("scalar", lambda E, zg=zg, pA=pA: E.activation(
                            siluz[:, zg * CH:(zg + 1) * CH], pA[:], AF.Silu),
                            reads=(rA,), writes=("siluz",))
            for k in range(16):
                b.op("tensor", lambda E, k=k: E.matmul(
                    psBG[0:64, :], wxp[:, k * 96:k * 96 + 64],
                    xc[:, k * CH:(k + 1) * CH],
                    start=(k == 0), stop=(k == 15)),
                    reads=("wxp", "xc"),
                    writes=("psBG",) if k in (0, 15) else (),
                    inc=1 if k in (0, 15) else 0)
            for k in range(16):
                b.op("tensor", lambda E, k=k: E.matmul(
                    psC[:], wxp[:, k * 96 + 64:k * 96 + 80],
                    xc[:, k * CH:(k + 1) * CH],
                    start=(k == 0), stop=(k == 15)),
                    reads=("wxp", "xc"),
                    writes=("psC",) if k in (0, 15) else (),
                    inc=1 if k in (0, 15) else 0)
            for k in range(16):
                b.op("tensor", lambda E, k=k: E.matmul(
                    psD[:], wxp[:, k * 96 + 80:k * 96 + 96],
                    xc[:, k * CH:(k + 1) * CH],
                    start=(k == 0), stop=(k == 15)),
                    reads=("wxp", "xc"),
                    writes=("psD",) if k in (0, 15) else (),
                    inc=1 if k in (0, 15) else 0)
            b.op("scalar", lambda E: E.activation(dtraw[:], psBG[0:64, :], AF.Copy),
                 reads=("psBG",), writes=("dtraw",))
            b.op("scalar", lambda E: E.activation(Bmt[:], psC[:], AF.Copy),
                 reads=("psC",), writes=("Bmt",))
            b.op("scalar", lambda E: E.activation(Cmt[:], psD[:], AF.Copy),
                 reads=("psD",), writes=("Cmt",))
            for n in range(NS):
                b.op("tensor", lambda E, n=n: E.matmul(
                    psE[:], sel[:, n * 128:(n + 1) * 128], Bmt[:],
                    start=True, stop=True),
                    reads=("sel", "Bmt"), writes=("psE",))
                b.op("scalar", lambda E, n=n: E.activation(
                    brep[:, n * CH:(n + 1) * CH], psE[:], AF.Copy),
                    reads=("psE",), writes=("brep",))
                b.op("tensor", lambda E, n=n: E.matmul(
                    psE[:], sel[:, n * 128:(n + 1) * 128], Cmt[:],
                    start=True, stop=True),
                    reads=("sel", "Cmt"), writes=("psE",))
                b.op("scalar", lambda E, n=n: E.activation(
                    crep[:, n * CH:(n + 1) * CH], psE[:], AF.Copy),
                    reads=("psE",), writes=("crep",))
            for dg in range(16):
                b.op("tensor", lambda E, dg=dg: E.matmul(
                    psBG[:], wdt[:, dg * 128:(dg + 1) * 128], dtraw[:],
                    start=True, stop=True),
                    reads=("wdt", "dtraw"), writes=("psBG",))
                b.op("scalar", lambda E: E.activation(qsp[:], psBG[:], AF.Copy),
                     reads=("psBG",), writes=("qsp",))
                b.op("scalar", lambda E, dg=dg: E.activation(
                    spa[:], qsp[:], AF.Abs, bias=dtbc[:, dg:dg + 1]),
                    reads=("qsp", "dtbc"), writes=("spa",))
                b.op("scalar", lambda E: E.activation(
                    spa[:], spa[:], AF.Exp, scale=-1.0),
                    reads=("spa",), writes=("spa",))
                b.op("scalar", lambda E: E.activation(
                    spa[:], spa[:], AF.Ln, bias=onescol[:, 0:1]),
                    reads=("spa", "onescol"), writes=("spa",))
                b.op("vector", lambda E, dg=dg: E.tensor_scalar(
                    spb[:], qsp[:], dtbc[:, dg:dg + 1], 0.0, ALU.add, ALU.max),
                    reads=("qsp", "dtbc"), writes=("spb",))
                q2 = dg % 2
                b.op("vector", lambda E, q2=q2: E.tensor_add(dtbg[q2][:], spa[:], spb[:]),
                     reads=("spa", "spb"), writes=(f"dtbg{q2}",))
                b.op("vector", lambda E, dg=dg, q2=q2: E.tensor_mul(
                    dtu[q2][:], dtbg[q2][:], xc[:, dg * CH:(dg + 1) * CH]),
                    reads=(f"dtbg{q2}", "xc"), writes=(f"dtu{q2}",))
                b.op("vector", lambda E: E.memset(yacc[:], 0.0), writes=("yacc",))
                for n in range(NS):
                    p = n % 2
                    q2 = dg % 2
                    b.op("scalar", lambda E, dg=dg, n=n, p=p, q2=q2: E.activation(
                        dA_t[p][:], dtbg[q2][:], AF.Exp,
                        scale=Aneg[:, dg * 16 + n:dg * 16 + n + 1]),
                        reads=(f"dtbg{q2}", "Aneg"), writes=(f"dA_t{p}",))
                    b.op("vector", lambda E, n=n, p=p, q2=q2: E.tensor_mul(
                        dBu_t[p][:], dtu[q2][:], brep[:, n * CH:(n + 1) * CH]),
                        reads=(f"dtu{q2}", "brep"), writes=(f"dBu_t{p}",))
                    b.op("vector", lambda E, dg=dg, n=n, p=p: E.tensor_tensor_scan(
                        h_t[p][:], dA_t[p][:], dBu_t[p][:],
                        hcarry[:, dg * 16 + n:dg * 16 + n + 1],
                        ALU.mult, ALU.add),
                        reads=(f"dA_t{p}", f"dBu_t{p}", f"hcarry{(dg * 16 + n) % 32}"), writes=(f"h_t{p}",))
                    b.op("vector", lambda E, dg=dg, n=n, p=p: E.tensor_copy(
                        hcarry[:, dg * 16 + n:dg * 16 + n + 1],
                        h_t[p][:, CH - 1:CH]),
                        reads=(f"h_t{p}",), writes=(f"hcarry{(dg * 16 + n) % 32}",))
                    b.op("vector", lambda E, n=n, p=p: E.tensor_mul(
                        hc_t[p][:], h_t[p][:], crep[:, n * CH:(n + 1) * CH]),
                        reads=(f"h_t{p}", "crep"), writes=(f"hc_t{p}",))
                    b.op("vector", lambda E, p=p: E.tensor_add(
                        yacc[:], yacc[:], hc_t[p][:]),
                        reads=(f"hc_t{p}", "yacc"), writes=("yacc",))
                b.op("vector", lambda E, dg=dg: E.scalar_tensor_tensor(
                    yacc[:], xc[:, dg * CH:(dg + 1) * CH], Dv[:, dg:dg + 1],
                    yacc[:], ALU.mult, ALU.add),
                    reads=("xc", "Dv", "yacc"), writes=("yacc",))
                b.op("vector", lambda E, dg=dg: E.tensor_mul(
                    y3[:, dg * CH:(dg + 1) * CH], yacc[:],
                    siluz[:, dg * CH:(dg + 1) * CH]),
                    reads=("yacc", "siluz"), writes=("y3",))
            for j in range(8):
                for half in range(2):
                    k = j * 2 + half
                    b.dma("gpsimd", wp[j][:, half * 1024:half * 1024 + 1024],
                          P["wop_in"][k * 128:(k + 1) * 128, :], writes=(f"wp{j}",))
            for og in range(8):
                pA = psA2x[og % 2]
                rA = f"psA{og % 2}"
                for k in range(16):
                    b.op("tensor", lambda E, og=og, k=k, pA=pA: E.matmul(
                        pA[:], wp[k // 2][:, (k % 2) * 1024 + og * 128:
                                          (k % 2) * 1024 + (og + 1) * 128],
                        y3[:, k * CH:(k + 1) * CH],
                        start=(k == 0), stop=(k == 15)),
                        reads=(f"wp{k // 2}", "y3"),
                        writes=(rA,) if k in (0, 15) else (),
                        inc=1 if k in (0, 15) else 0)
                b.op("scalar", lambda E, og=og, pA=pA: E.activation(
                    mT[:, og * CH:(og + 1) * CH], pA[:], AF.Gelu),
                    reads=(rA,), writes=("mT",))
            for k in range(8):
                b.dma("gpsimd", wp[k][:], P["wglu_in"][k * 128:(k + 1) * 128, :],
                      writes=(f"wp{k}",))
            for og in range(16):
                pA = psA2x[og % 2]
                rA = f"psA{og % 2}"
                for k in range(8):
                    b.op("tensor", lambda E, og=og, k=k, pA=pA: E.matmul(
                        pA[:], wp[k][:, og * 128:(og + 1) * 128],
                        mT[:, k * CH:(k + 1) * CH],
                        start=(k == 0), stop=(k == 7)),
                        reads=(f"wp{k}", "mT"),
                        writes=(rA,) if k in (0, 7) else (),
                        inc=1 if k in (0, 7) else 0)
                if og < 8:
                    b.op("scalar", lambda E, og=og, pA=pA: E.activation(
                        g1[:, og * CH:(og + 1) * CH], pA[:], AF.Copy),
                        reads=(rA,), writes=("g1",))
                    b.op("vector", lambda E, og=og: E.tensor_scalar(
                        g1[:, og * CH:(og + 1) * CH],
                        g1[:, og * CH:(og + 1) * CH],
                        glub[:, og:og + 1], None, ALU.add),
                        reads=("g1", "glub"), writes=("g1",))
                else:
                    b.op("scalar", lambda E, og=og, pA=pA: E.activation(
                        sg2[:, (og - 8) * CH:(og - 7) * CH], pA[:], AF.Sigmoid,
                        bias=glub[:, og:og + 1]),
                        reads=(rA, "glub"), writes=("sg2",))
            b.dma("sync", xn2[:].rearrange("p (g t) -> p g t", g=8),
                  P["xnTb_hbm"][:, :, c0:c0 + CH], writes=("xn2",))
            for g in range(8):
                b.op("vector", lambda E, g=g: E.tensor_mul(
                    g1[:, g * CH:(g + 1) * CH], g1[:, g * CH:(g + 1) * CH],
                    sg2[:, g * CH:(g + 1) * CH]),
                    reads=("g1", "sg2"), writes=("g1",))
                b.op("vector", lambda E, g=g: E.tensor_add(
                    g1[:, g * CH:(g + 1) * CH], g1[:, g * CH:(g + 1) * CH],
                    xn2[:, g * CH:(g + 1) * CH]),
                    reads=("g1", "xn2"), writes=("g1",))
            b.dma("sync", P["sT_hbm"][:, :, c0:c0 + CH],
                  g1[:].rearrange("p (g t) -> p g t", g=8), reads=("g1",))
        return do_chunk

def emit_C(b, nc, P):
    NT = T // 128
    NT = T // 128
    with contextlib.ExitStack() as st:
        e = st.enter_context
        wp = [e(nc.sbuf_tensor(f"C_wp{i}", [128, 2048], BF16)) for i in range(8)]
        xnb = e(nc.sbuf_tensor("C_xnb", [128, 8 * T], BF16))
        hfb = e(nc.sbuf_tensor("C_hfb", [128, 16 * T], BF16))
        yacc = e(nc.sbuf_tensor("C_yacc", [128, 8 * T], BF16))
        eye = e(nc.sbuf_tensor("C_sb_eye", [128, 128], F32))
        ones = e(nc.sbuf_tensor("C_sb_ones", [128, 128], F32))
        sel8 = e(nc.sbuf_tensor("C_sb_sel8", [E0, E0 * 128], F32))
        b1t = e(nc.sbuf_tensor("C_b1t", [128, 32], F32))
        b2t = e(nc.sbuf_tensor("C_b2t", [E0, D], F32))
        rt = e(nc.sbuf_tensor("C_rt", [128, 20], F32))
        mtok = e(nc.sbuf_tensor("C_mtok", [128, E0], F32))
        scr = e(nc.sbuf_tensor("C_scr", [128, E0], F32))
        wcol = e(nc.sbuf_tensor("C_wcol", [128, 4], F32))
        mTt = e(nc.sbuf_tensor("C_mTt", [E0, T], F32))
        wnT = e(nc.sbuf_tensor("C_wnT", [1, T], F32))
        mrep = e(nc.sbuf_tensor("C_mrep", [128, T], BF16))
        wnrep = e(nc.sbuf_tensor("C_wnrep", [128, 512], BF16))
        gtmp = e(nc.sbuf_tensor("C_gtmp", [128, 8 * 512], BF16))
        ps8 = [e(nc.psum_tensor(f"C_ps8_{i}", [128, 512], F32)) for i in range(8)]

        b.dma("sync", eye[:], P["eye_in"], writes=("eye",))
        b.dma("sync", ones[:], P["ones_in"], writes=("ones",))
        b.dma("sync", sel8[:], P["sel8_in"], writes=("sel8",))
        b.dma("sync", b2t[:], P["b2_in"], writes=("b2t",))

        for i in range(NT):
            b.op("vector", lambda E, i=i: E.tensor_copy(
                rt[:], P["route_sb"][:, i * 20:(i + 1) * 20]),
                reads=("route_sb",), writes=("rt",))
            b.op("vector", lambda E: E.tensor_mul(scr[:], P["offs_rep"][:], rt[:, 0:8]),
                 reads=("offs_rep", "rt"), writes=("scr",))
            b.op("vector", lambda E: E.tensor_reduce(
                wcol[:, 0:1], scr[:], axis=AX.X, op=ALU.add),
                reads=("scr",), writes=("wcol",))
            b.op("vector", lambda E: E.tensor_add(wcol[:, 0:1], wcol[:, 0:1],
                                                  rt[:, 18:19]),
                 reads=("wcol", "rt"), writes=("wcol",))
            b.op("vector", lambda E: E.tensor_scalar(
                wcol[:, 0:1], wcol[:, 0:1], float(CAP), None, ALU.is_lt),
                reads=("wcol",), writes=("wcol",))
            b.op("vector", lambda E: E.tensor_mul(wcol[:, 0:1], wcol[:, 0:1],
                                                  rt[:, 16:17]),
                 reads=("wcol", "rt"), writes=("wcol",))
            b.op("vector", lambda E: E.tensor_mul(scr[:], P["offs_rep"][:], rt[:, 8:16]),
                 reads=("offs_rep", "rt"), writes=("scr",))
            b.op("vector", lambda E: E.tensor_reduce(
                wcol[:, 1:2], scr[:], axis=AX.X, op=ALU.add),
                reads=("scr",), writes=("wcol",))
            b.op("vector", lambda E: E.tensor_add(wcol[:, 1:2], wcol[:, 1:2],
                                                  rt[:, 19:20]),
                 reads=("wcol", "rt"), writes=("wcol",))
            b.op("vector", lambda E: E.tensor_scalar(
                wcol[:, 1:2], wcol[:, 1:2], float(CAP), None, ALU.is_lt),
                reads=("wcol",), writes=("wcol",))
            b.op("vector", lambda E: E.tensor_mul(wcol[:, 1:2], wcol[:, 1:2],
                                                  rt[:, 17:18]),
                 reads=("wcol", "rt"), writes=("wcol",))
            b.op("vector", lambda E: E.tensor_scalar(
                mtok[:], rt[:, 0:8], wcol[:, 0:1], None, ALU.mult),
                reads=("rt", "wcol"), writes=("mtok",))
            b.op("vector", lambda E: E.tensor_scalar(
                scr[:], rt[:, 8:16], wcol[:, 1:2], None, ALU.mult),
                reads=("rt", "wcol"), writes=("scr",))
            b.op("vector", lambda E: E.tensor_add(mtok[:], mtok[:], scr[:]),
                 reads=("mtok", "scr"), writes=("mtok",))
            b.op("vector", lambda E: E.tensor_add(wcol[:, 2:3], wcol[:, 0:1],
                                                  wcol[:, 1:2]),
                 reads=("wcol",), writes=("wcol",))
            b.op("vector", lambda E: E.tensor_scalar(
                wcol[:, 2:3], wcol[:, 2:3], 1e-6, None, ALU.max),
                reads=("wcol",), writes=("wcol",))
            b.op("vector", lambda E: E.reciprocal(wcol[:, 3:4], wcol[:, 2:3]),
                 reads=("wcol",), writes=("wcol",))
            b.op("tensor", lambda E: E.matmul(ps8[2][0:8, 0:128], mtok[:], eye[:],
                                              start=True, stop=True),
                 reads=("mtok", "eye"), writes=("ps2",))
            b.op("scalar", lambda E, i=i: E.activation(
                mTt[:, i * 128:(i + 1) * 128], ps8[2][0:8, 0:128], AF.Copy),
                reads=("ps2",), writes=("mTt",))
            b.op("tensor", lambda E: E.matmul(ps8[3][0:1, 0:128], wcol[:, 3:4],
                                              eye[:], start=True, stop=True),
                 reads=("wcol", "eye"), writes=("ps3",))
            b.op("scalar", lambda E, i=i: E.activation(
                wnT[:, i * 128:(i + 1) * 128], ps8[3][0:1, 0:128], AF.Copy),
                reads=("ps3",), writes=("wnT",))
        b.dma("sync", xnb[:].rearrange("p (g t) -> p g t", g=8),
              P["xnTb_hbm"][:, :, :], writes=("xnb",))
        b.op("vector", lambda E: E.memset(yacc[:], 0.0), writes=("yacc",))
        for ex in range(E0):
            b.dma("sync", b1t[:], P["b1_in"][ex], writes=("b1t",))
            for nt in range(4):
                b.op("tensor", lambda E, ex=ex, nt=nt: E.matmul(
                    ps8[4][:], sel8[:, ex * 128:(ex + 1) * 128],
                    mTt[:, nt * 512:(nt + 1) * 512], start=True, stop=True),
                    reads=("sel8", "mTt"), writes=("ps4",))
                b.op("scalar", lambda E, nt=nt: E.activation(
                    mrep[:, nt * 512:(nt + 1) * 512], ps8[4][:], AF.Copy),
                    reads=("ps4",), writes=("mrep",))
            for fb in range(2):
                for k in range(8):
                    b.dma("gpsimd", wp[k][:],
                          P["w1_in"][ex, k * 128:(k + 1) * 128,
                                fb * 2048:fb * 2048 + 2048],
                          writes=(f"wp{k}",))
                for ogl in range(16):
                    og = fb * 16 + ogl
                    for nt in range(4):
                        pb = (ogl * 4 + nt) % 2
                        for k in range(8):
                            b.op("tensor", lambda E, ogl=ogl, nt=nt, k=k, pb=pb: E.matmul(
                                ps8[pb][:], wp[k][:, ogl * 128:(ogl + 1) * 128],
                                xnb[:, k * T + nt * 512:k * T + (nt + 1) * 512],
                                start=(k == 0), stop=(k == 7)),
                                reads=(f"wp{k}", "xnb"),
                                writes=(f"ps{pb}",) if k in (0, 7) else (),
                                inc=1 if k in (0, 7) else 0)
                        b.op("scalar", lambda E, og=og, ogl=ogl, nt=nt, pb=pb: E.activation(
                            hfb[:, ogl * T + nt * 512:ogl * T + (nt + 1) * 512],
                            ps8[pb][:], AF.Gelu, bias=b1t[:, og:og + 1]),
                            reads=(f"ps{pb}", "b1t"), writes=("hfb",))
                    b.op("vector", lambda E, ogl=ogl: E.tensor_mul(
                        hfb[:, ogl * T:(ogl + 1) * T],
                        hfb[:, ogl * T:(ogl + 1) * T], mrep[:]),
                        reads=("hfb", "mrep"), writes=("hfb",))
                for k2 in range(16):
                    b.dma("gpsimd", wp[k2 // 2][:, (k2 % 2) * 1024:(k2 % 2 + 1) * 1024],
                          P["w2_in"][ex, fb * 2048 + k2 * 128:fb * 2048 + (k2 + 1) * 128, :],
                          writes=(f"wp{k2 // 2}",))
                for nt in range(4):
                    for k2 in range(16):
                        for og in range(8):
                            b.op("tensor", lambda E, k2=k2, og=og, nt=nt: E.matmul(
                                ps8[og][:],
                                wp[k2 // 2][:, (k2 % 2) * 1024 + og * 128:
                                            (k2 % 2) * 1024 + (og + 1) * 128],
                                hfb[:, k2 * T + nt * 512:k2 * T + (nt + 1) * 512],
                                start=(k2 == 0), stop=(k2 == 15)),
                                reads=(f"wp{k2 // 2}", "hfb"),
                                writes=(f"ps{og}",) if k2 in (0, 15) else (),
                                inc=1 if k2 in (0, 15) else 0)
                    for og in range(8):
                        b.op("vector", lambda E, og=og, nt=nt: E.scalar_tensor_tensor(
                            yacc[:, og * T + nt * 512:og * T + (nt + 1) * 512],
                            ps8[og][:], 1.0,
                            yacc[:, og * T + nt * 512:og * T + (nt + 1) * 512],
                            ALU.mult, ALU.add),
                            reads=(f"ps{og}", "yacc"), writes=("yacc",))
        for og in range(8):
            for nt in range(4):
                b.op("tensor", lambda E, og=og, nt=nt: E.matmul(
                    ps8[og][:], b2t[:, og * 128:(og + 1) * 128],
                    mTt[:, nt * 512:(nt + 1) * 512], start=True, stop=True),
                    reads=("b2t", "mTt"), writes=(f"ps{og}",))
                b.op("vector", lambda E, og=og, nt=nt: E.scalar_tensor_tensor(
                    yacc[:, og * T + nt * 512:og * T + (nt + 1) * 512],
                    ps8[og][:], 1.0,
                    yacc[:, og * T + nt * 512:og * T + (nt + 1) * 512],
                    ALU.mult, ALU.add),
                    reads=(f"ps{og}", "yacc"), writes=("yacc",))
        for nt in range(4):
            b.op("tensor", lambda E, nt=nt: E.matmul(
                ps8[4][:], ones[0:1, :], wnT[:, nt * 512:(nt + 1) * 512],
                start=True, stop=True),
                reads=("ones", "wnT"), writes=("ps4",))
            b.op("scalar", lambda E: E.activation(wnrep[:], ps8[4][:], AF.Copy),
                 reads=("ps4",), writes=("wnrep",))
            for og in range(8):
                b.op("vector", lambda E, og=og, nt=nt: E.tensor_mul(
                    gtmp[:, og * 512:(og + 1) * 512],
                    yacc[:, og * T + nt * 512:og * T + (nt + 1) * 512], wnrep[:]),
                    reads=("yacc", "wnrep"), writes=("gtmp",))
                b.op("scalar", lambda E, og=og: E.activation(
                    gtmp[:, og * 512:(og + 1) * 512],
                    gtmp[:, og * 512:(og + 1) * 512], AF.Silu),
                    reads=("gtmp",), writes=("gtmp",))
            b.dma("sync", P["gT_ap"].rearrange("(g p) t -> p g t", p=128)[:, :, nt * 512:(nt + 1) * 512],
                  gtmp[:, 0:8 * 512].rearrange("p (g t) -> p g t", g=8),
                  reads=("gtmp",))


def emit_D(b, nc, P):
    CH = 512
    CH = 512
    with contextlib.ExitStack() as st:
        e = st.enter_context
        eye = e(nc.sbuf_tensor("D_sb_eye", [128, 128], F32))
        ones = e(nc.sbuf_tensor("D_sb_ones", [128, 128], F32))
        xt_ = [e(nc.sbuf_tensor(f"D_xt{i}", [128, D], F32)) for i in range(2)]
        xT_ = [e(nc.sbuf_tensor(f"D_xT{i}", [128, 8 * CH], F32)) for i in range(2)]
        s_c_ = [e(nc.sbuf_tensor(f"D_s_c{i}", [128, 8 * CH], BF16)) for i in range(2)]
        g_c_ = [e(nc.sbuf_tensor(f"D_g_c{i}", [128, 8 * CH], BF16)) for i in range(2)]
        x1c_ = [e(nc.sbuf_tensor(f"D_x1c{i}", [128, 8 * CH], F32)) for i in range(2)]
        xsq_ = [e(nc.sbuf_tensor(f"D_xsq{i}", [128, 8 * CH], F32)) for i in range(2)]
        strow_ = [e(nc.sbuf_tensor(f"D_strow{i}", [1, 6 * CH], F32)) for i in range(2)]
        rrep_ = [e(nc.sbuf_tensor(f"D_rrep{i}", [128, CH], F32)) for i in range(2)]
        mrep2_ = [e(nc.sbuf_tensor(f"D_mrep2{i}", [128, CH], F32)) for i in range(2)]
        cvals = e(nc.sbuf_tensor("D_cvals", [128, 1], F32))
        ps_t_ = [e(nc.psum_tensor(f"D_ps_t{i}", [128, 512], F32)) for i in range(2)]
        ps_sq_ = [e(nc.psum_tensor(f"D_ps_sq{i}", [64, 512], F32)) for i in range(2)]
        ps_s_ = [t[0:1, :] for t in ps_sq_]
        ps_q_ = [t[32:33, :] for t in ps_sq_]
        ps_b12_ = [e(nc.psum_tensor(f"D_ps_b12{i}", [128, 512], F32)) for i in range(2)]
        ps_b1_ = ps_b12_
        ps_b2_ = ps_b12_

        b.dma("sync", eye[:], P["eye_in"], writes=("eye",))
        b.dma("sync", ones[:], P["ones_in"], writes=("ones",))
        b.op("vector", lambda E: E.memset(cvals[:], 1e-5), writes=("cvals",))
        def _d_s0(ci, c0, xt, xT, s_c, g_c, x1c, xsq, strow, rrep, mrep2,
                  ps_t, ps_s, ps_q, ps_b1, ps_b2, Z):
            b.dma("sync", s_c[:].rearrange("p (g t) -> p g t", g=8),
                  P["sT_hbm"][:, :, c0:c0 + CH], writes=("Ds_c" + Z,))
            b.dma("sync", g_c[:].rearrange("p (g t) -> p g t", g=8),
                  P["gT_hbm"][:, :, c0:c0 + CH], writes=("Dg_c" + Z,))
            for tt in range(4):
                b.dma("sync", xt[:], P["x_in"][c0 + tt * 128:c0 + (tt + 1) * 128, :],
                      writes=("Dxt" + Z,))
                for jj in (0, 4):
                    for j in range(jj, jj + 4):
                        b.op("tensor", lambda E, j=j, jj=jj: E.transpose(
                            ps_t[:, (j - jj) * 128:(j - jj) * 128 + 128],
                            xt[:, j * 128:(j + 1) * 128], eye[:]),
                            reads=("Dxt" + Z, "eye"), writes=("Dps_t" + Z,))
                    for j in range(jj, jj + 4):
                        b.op("scalar", lambda E, j=j, jj=jj, tt=tt: E.activation(
                            xT[:, j * CH + tt * 128:j * CH + (tt + 1) * 128],
                            ps_t[:, (j - jj) * 128:(j - jj) * 128 + 128], AF.Copy),
                            reads=("Dps_t" + Z,), writes=("DxT" + Z,))

        def _d_s1(ci, c0, xt, xT, s_c, g_c, x1c, xsq, strow, rrep, mrep2,
                  ps_t, ps_s, ps_q, ps_b1, ps_b2, Z):
            for g in range(8):
                b.op("vector", lambda E, g=g: E.tensor_mul(
                    x1c[:, g * CH:(g + 1) * CH], s_c[:, g * CH:(g + 1) * CH],
                    g_c[:, g * CH:(g + 1) * CH]),
                    reads=("Ds_c" + Z, "Dg_c" + Z), writes=("Dx1c" + Z,))
                b.op("vector", lambda E, g=g: E.tensor_add(
                    x1c[:, g * CH:(g + 1) * CH], x1c[:, g * CH:(g + 1) * CH],
                    xT[:, g * CH:(g + 1) * CH]),
                    reads=("Dx1c" + Z, "DxT" + Z), writes=("Dx1c" + Z,))
            b.dma("sync", P["x1_hbm"][:, :, c0:c0 + CH],
                  x1c[:].rearrange("p (g t) -> p g t", g=8), reads=("Dx1c" + Z,))

        def _d_s2(ci, c0, xt, xT, s_c, g_c, x1c, xsq, strow, rrep, mrep2,
                  ps_t, ps_s, ps_q, ps_b1, ps_b2, Z):
            b.op("scalar", lambda E: E.activation(xsq[:], x1c[:], AF.Square),
                 reads=("Dx1c" + Z,), writes=("Dxsq" + Z,))
            for g in range(8):
                b.op("tensor", lambda E, g=g: E.matmul(
                    ps_s[:], ones[:, 0:1], x1c[:, g * CH:(g + 1) * CH],
                    start=(g == 0), stop=(g == 7)),
                    reads=("ones", "Dx1c" + Z),
                    writes=("Dps_sq" + Z,) if g in (0, 7) else (),
                    inc=1 if g in (0, 7) else 0)
            for g in range(8):
                b.op("tensor", lambda E, g=g: E.matmul(
                    ps_q[:], ones[:, 0:1], xsq[:, g * CH:(g + 1) * CH],
                    start=(g == 0), stop=(g == 7)),
                    reads=("ones", "Dxsq" + Z),
                    writes=("Dps_sq" + Z,) if g in (0, 7) else (),
                    inc=1 if g in (0, 7) else 0)

        def _d_s3(ci, c0, xt, xT, s_c, g_c, x1c, xsq, strow, rrep, mrep2,
                  ps_t, ps_s, ps_q, ps_b1, ps_b2, Z):
            b.op("scalar", lambda E: E.activation(
                strow[:, 0 * CH:1 * CH], ps_s[:], AF.Copy, scale=1.0 / D),
                reads=("Dps_sq" + Z,), writes=("Dstrow" + Z,))
            b.op("scalar", lambda E: E.activation(
                strow[:, 1 * CH:2 * CH], ps_q[:], AF.Copy, scale=1.0 / D),
                reads=("Dps_sq" + Z,), writes=("Dstrow" + Z,))
            b.op("vector", lambda E: E.tensor_mul(
                strow[:, 2 * CH:3 * CH], strow[:, 0:CH], strow[:, 0:CH]),
                reads=("Dstrow" + Z,), writes=("Dstrow" + Z,))
            b.op("vector", lambda E: E.tensor_sub(
                strow[:, 2 * CH:3 * CH], strow[:, CH:2 * CH], strow[:, 2 * CH:3 * CH]),
                reads=("Dstrow" + Z,), writes=("Dstrow" + Z,))
            b.op("scalar", lambda E: E.activation(
                strow[:, 3 * CH:4 * CH], strow[:, 2 * CH:3 * CH], AF.Ln,
                bias=cvals[0:1, 0:1]),
                reads=("Dstrow" + Z, "cvals"), writes=("Dstrow" + Z,))
            b.op("scalar", lambda E: E.activation(
                strow[:, 3 * CH:4 * CH], strow[:, 3 * CH:4 * CH], AF.Exp, scale=-0.5),
                reads=("Dstrow" + Z,), writes=("Dstrow" + Z,))
            b.op("vector", lambda E: E.tensor_mul(
                strow[:, 4 * CH:5 * CH], strow[:, 0:CH], strow[:, 3 * CH:4 * CH]),
                reads=("Dstrow" + Z,), writes=("Dstrow" + Z,))
            b.op("vector", lambda E: E.tensor_scalar(
                strow[:, 4 * CH:5 * CH], strow[:, 4 * CH:5 * CH], -1.0, None,
                ALU.mult),
                reads=("Dstrow" + Z,), writes=("Dstrow" + Z,))
            b.op("tensor", lambda E: E.matmul(
                ps_b1[:], ones[0:1, :], strow[:, 3 * CH:4 * CH],
                start=True, stop=True),
                reads=("ones", "Dstrow" + Z), writes=("Dps_b" + Z,))
            b.op("scalar", lambda E: E.activation(rrep[:], ps_b1[:], AF.Copy),
                 reads=("Dps_b" + Z,), writes=("Drrep" + Z,))
            b.op("tensor", lambda E: E.matmul(
                ps_b2[:], ones[0:1, :], strow[:, 4 * CH:5 * CH],
                start=True, stop=True),
                reads=("ones", "Dstrow" + Z), writes=("Dps_b" + Z,))
            b.op("scalar", lambda E: E.activation(mrep2[:], ps_b2[:], AF.Copy),
                 reads=("Dps_b" + Z,), writes=("Dmrep2" + Z,))

        def _d_s4(ci, c0, xt, xT, s_c, g_c, x1c, xsq, strow, rrep, mrep2,
                  ps_t, ps_s, ps_q, ps_b1, ps_b2, Z):
            for g in range(8):
                b.op("vector", lambda E, g=g: E.tensor_mul(
                    xsq[:, g * CH:(g + 1) * CH], x1c[:, g * CH:(g + 1) * CH],
                    rrep[:]),
                    reads=("Dx1c" + Z, "Drrep" + Z), writes=("Dxsq" + Z,))
                b.op("vector", lambda E, g=g: E.tensor_add(
                    g_c[:, g * CH:(g + 1) * CH], xsq[:, g * CH:(g + 1) * CH],
                    mrep2[:]),
                    reads=("Dxsq" + Z, "Dmrep2" + Z), writes=("Dg_c" + Z,))
            b.dma("sync", P["x1n_hbm"][:, :, c0:c0 + CH],
                  g_c[:].rearrange("p (g t) -> p g t", g=8), reads=("Dg_c" + Z,))

        def dargs(ci):
            p = ci % 2
            return (ci, ci * CH, xt_[p], xT_[p], s_c_[p], g_c_[p], x1c_[p],
                    xsq_[p], strow_[p], rrep_[p], mrep2_[p], ps_t_[p], ps_s_[p],
                    ps_q_[p], ps_b1_[p], ps_b2_[p], f"{p}")

        for base in (0, 2):
            pair = [dargs(base), dargs(base + 1)]
            for st_fn in (_d_s0, _d_s1, _d_s2, _d_s3, _d_s4):
                for a in pair:
                    st_fn(*a)

def emit_E(b, nc, P):
    NT = T // 128
    SC = 1.0 / np.sqrt(2048.0)
    NT = T // 128
    SC = 1.0 / np.sqrt(2048.0)
    with contextlib.ExitStack() as st:
        e = st.enter_context
        x1n = e(nc.sbuf_tensor("E_x1n", [128, 8 * T], BF16))
        s1 = e(nc.sbuf_tensor("E_s1", [128, 8 * T], BF16))
        wpool = e(nc.sbuf_tensor("E_wpool", [128, 24 * 1024], BF16))
        eye = e(nc.sbuf_tensor("E_sb_eye", [128, 128], F32))
        eyeb = e(nc.sbuf_tensor("E_sb_eyeb", [128, 128], BF16))
        ones = e(nc.sbuf_tensor("E_sb_ones", [128, 128], F32))
        onesb = e(nc.sbuf_tensor("E_sb_onesb", [128, 128], BF16))
        sel8 = e(nc.sbuf_tensor("E_sb_sel8", [E0, E0 * 128], F32))
        lepeb = e(nc.sbuf_tensor("E_lepeb", [128, 8], F32))
        qkbr = e(nc.sbuf_tensor("E_qkbr", [1, 2048], F32))
        qkbrep = e(nc.sbuf_tensor("E_qkbrep", [128, 2048], F32))
        threp = e(nc.sbuf_tensor("E_threp", [128, 512], F32))
        tcol_ = [e(nc.sbuf_tensor(f"E_tcol{i}", [128, 1], F32)) for i in range(2)]
        ang_ = [e(nc.sbuf_tensor(f"E_ang{i}", [128, 512], F32)) for i in range(2)]
        ang2_ = [e(nc.sbuf_tensor(f"E_ang2{i}", [128, 512], F32)) for i in range(2)]
        angi_ = [e(nc.sbuf_tensor(f"E_angi{i}", [128, 512], I32)) for i in range(2)]
        sint_ = [e(nc.sbuf_tensor(f"E_sint{i}", [128, 512], BF16)) for i in range(2)]
        cost_ = [e(nc.sbuf_tensor(f"E_cost{i}", [128, 512], BF16)) for i in range(2)]
        kqf_ = [e(nc.sbuf_tensor(f"E_kqf{i}", [128, 1024], F32)) for i in range(2)]
        sca_ = [e(nc.sbuf_tensor(f"E_sca{i}", [128, 1024], F32)) for i in range(2)]
        kq_ = [e(nc.sbuf_tensor(f"E_kq{i}", [128, 1024], BF16)) for i in range(2)]
        kqr_ = [e(nc.sbuf_tensor(f"E_kqr{i}", [128, 1024], BF16)) for i in range(2)]
        v_t_ = [e(nc.sbuf_tensor(f"E_v_t{i}", [128, 1024], BF16)) for i in range(2)]
        qrT_ = [e(nc.sbuf_tensor(f"E_qrT{i}", [128, 1024], BF16)) for i in range(2)]
        kv_sb = e(nc.sbuf_tensor("E_kv_sb", [128, 1024], BF16))
        kmean = e(nc.sbuf_tensor("E_kmean", [1, 1024], F32))
        kmrep = e(nc.sbuf_tensor("E_kmrep", [128, 1024], BF16))
        zden_ = [e(nc.sbuf_tensor(f"E_zden{i}", [128, 8], F32)) for i in range(2)]
        zT = e(nc.sbuf_tensor("E_zT", [8, T], F32))
        zrep_ = [e(nc.sbuf_tensor(f"E_zrep{i}", [128, 512], BF16)) for i in range(2)]
        otmp_ = [e(nc.sbuf_tensor(f"E_otmp{i}", [128, 1024], BF16)) for i in range(2)]
        ps_a = e(nc.psum_tensor("E_ps_a", [128, 512], F32))
        ps_trb = e(nc.psum_tensor("E_ps_trb", [128, 512], BF16))
        ps_kv = e(nc.psum_tensor("E_ps_kv", [128, 1024], F32))
        ps_km = e(nc.psum_tensor("E_ps_km", [1, 1024], F32))
        ps_o = e(nc.psum_tensor("E_ps_o", [128, 1024], F32))

        b.dma("sync", eye[:], P["eye_in"], writes=("eye",))
        b.dma("sync", ones[:], P["ones_in"], writes=("ones",))
        b.dma("gpsimd", eyeb[:], P["eye_in"], writes=("eyeb",))
        b.dma("gpsimd", onesb[:], P["ones_in"], writes=("onesb",))
        b.dma("sync", sel8[:], P["sel8_in"], writes=("sel8",))
        b.dma("sync", lepeb[:], P["lepeb_in"], writes=("lepeb",))
        b.dma("sync", qkbr[:], P["qkb_in"].rearrange("(o t) -> o t", o=1), writes=("qkbr",))
        b.dma("sync", x1n[:].rearrange("p (g t) -> p g t", g=8),
              P["x1n_hbm"][:, :, :], writes=("x1n",))
        for nt in range(4):
            b.op("tensor", lambda E, nt=nt: E.matmul(
                ps_a[:], ones[0:1, :], qkbr[:, nt * 512:(nt + 1) * 512],
                start=True, stop=True),
                reads=("ones", "qkbr"), writes=("ps_a",))
            b.op("scalar", lambda E, nt=nt: E.activation(
                qkbrep[:, nt * 512:(nt + 1) * 512], ps_a[:], AF.Copy),
                reads=("ps_a",), writes=("qkbrep",))
        b.op("gpsimd", lambda E: E.iota(ang_[0][0:1, :], [[1, 512]],
                                        channel_multiplier=0,
                                        allow_small_or_imprecise_dtypes=True),
             writes=("Eang0",))
        b.op("scalar", lambda E: E.activation(
            ang_[0][0:1, :], ang_[0][0:1, :], AF.Exp, scale=float(-np.log(10000.0) / 512)),
            reads=("Eang0",), writes=("Eang0",))
        b.op("tensor", lambda E: E.matmul(ps_a[:], ones[0:1, :], ang_[0][0:1, :],
                                          start=True, stop=True),
             reads=("ones", "Eang0"), writes=("ps_a",))
        b.op("scalar", lambda E: E.activation(threp[:], ps_a[:], AF.Copy),
             reads=("ps_a",), writes=("threp",))

        for w in range(3):
            for k in range(8):
                b.dma("gpsimd", wpool[:, (w * 8 + k) * 1024:(w * 8 + k + 1) * 1024],
                      P["lepe_in"][w, k * 128:(k + 1) * 128, :], writes=("wpool",))
        for og in range(8):
            for nt in range(4):
                first = True
                for w in (1, 0, 2):
                    lo = nt * 512 + (w - 1)
                    hi = lo + 512
                    plo, phi = 0, 512
                    if lo < 0:
                        plo = 1; lo = 0
                    if hi > T:
                        phi = 511; hi = T
                    for k in range(8):
                        last = (w == 2 and k == 7)
                        b.op("tensor", lambda E, og=og, w=w, k=k, lo=lo, hi=hi,
                             plo=plo, phi=phi, first=first: E.matmul(
                            ps_a[:, plo:phi],
                            wpool[:, (w * 8 + k) * 1024 + og * 128:
                                  (w * 8 + k) * 1024 + (og + 1) * 128],
                            x1n[:, k * T + lo:k * T + hi],
                            start=first, stop=(w == 2 and k == 7)),
                            reads=("wpool", "x1n"),
                            writes=("ps_a",) if (first or last) else (),
                            inc=1 if (first or last) else 0)
                        first = False
                b.op("scalar", lambda E, og=og, nt=nt: E.activation(
                    s1[:, og * T + nt * 512:og * T + (nt + 1) * 512],
                    ps_a[:], AF.Copy),
                    reads=("ps_a",), writes=("s1",))
                b.op("vector", lambda E, og=og, nt=nt: E.tensor_scalar(
                    s1[:, og * T + nt * 512:og * T + (nt + 1) * 512],
                    s1[:, og * T + nt * 512:og * T + (nt + 1) * 512],
                    lepeb[:, og:og + 1], None, ALU.add),
                    reads=("s1", "lepeb"), writes=("s1",))

        for k in range(8):
            for half in range(2):
                b.dma("gpsimd", wpool[:, (k * 2 + half) * 1024:(k * 2 + half + 1) * 1024],
                      P["qkw_in"][k * 128:(k + 1) * 128, half * 1024:(half + 1) * 1024],
                      writes=("wpool",))

        def rope_tables(i, tcol, ang, ang2, angi, sint, cost, sca, Z):
            b.op("gpsimd", lambda E, i=i: E.iota(tcol[:], [[1, 1]], base=i * 128,
                                                 channel_multiplier=1,
                                                 allow_small_or_imprecise_dtypes=True),
                 writes=("Etcol" + Z,))
            b.op("vector", lambda E: E.tensor_scalar(
                ang[:], threp[:], tcol[:, 0:1], None, ALU.mult),
                reads=("threp", "Etcol" + Z), writes=("Eang" + Z,))
            for src_, dst, dstk in ((ang, sint, "Esint" + Z), (ang, cost, "Ecost" + Z)):
                if dst is cost:
                    b.op("vector", lambda E: E.tensor_scalar(
                        ang[:], ang[:], float(np.pi / 2), None, ALU.add),
                        reads=("Eang" + Z,), writes=("Eang" + Z,))
                b.op("vector", lambda E: E.tensor_scalar(
                    ang2[:], ang[:], float(1.0 / TWOPI), None, ALU.mult),
                    reads=("Eang" + Z,), writes=("Eang2" + Z,))
                b.op("vector", lambda E: E.tensor_copy(angi[:], ang2[:]),
                     reads=("Eang2" + Z,), writes=("Eangi" + Z,))
                b.op("vector", lambda E: E.tensor_copy(ang2[:], angi[:]),
                     reads=("Eangi" + Z,), writes=("Eang2" + Z,))
                b.op("vector", lambda E: E.scalar_tensor_tensor(
                    ang2[:], ang2[:], float(-TWOPI), ang[:], ALU.mult, ALU.add),
                    reads=("Eang2" + Z, "Eang" + Z), writes=("Eang2" + Z,))
                b.op("vector", lambda E: E.tensor_scalar(
                    sca[:, 0:512], ang2[:], float(np.pi), None, ALU.is_gt),
                    reads=("Eang2" + Z,), writes=("Esca" + Z,))
                b.op("vector", lambda E: E.scalar_tensor_tensor(
                    ang2[:], sca[:, 0:512], float(-TWOPI), ang2[:], ALU.mult, ALU.add),
                    reads=("Esca" + Z, "Eang2" + Z), writes=("Eang2" + Z,))
                b.op("vector", lambda E: E.tensor_scalar(
                    sca[:, 0:512], ang2[:], float(-np.pi), None, ALU.is_lt),
                    reads=("Eang2" + Z,), writes=("Esca" + Z,))
                b.op("vector", lambda E: E.scalar_tensor_tensor(
                    ang2[:], sca[:, 0:512], float(TWOPI), ang2[:], ALU.mult, ALU.add),
                    reads=("Esca" + Z, "Eang2" + Z), writes=("Eang2" + Z,))
                b.op("scalar", lambda E, dst=dst: E.activation(
                    dst[:], ang2[:], AF.Sin),
                    reads=("Eang2" + Z,), writes=(dstk,))

        def qk_stage(i, half, kqf, sca, kq, kqr, sint, cost, Z):
            for nsub in range(2):
                for k in range(8):
                    b.op("tensor", lambda E, i=i, half=half, nsub=nsub, k=k: E.matmul(
                        ps_a[:], x1n[:, k * T + i * 128:k * T + (i + 1) * 128],
                        wpool[:, (k * 2 + half) * 1024 + nsub * 512:
                              (k * 2 + half) * 1024 + (nsub + 1) * 512],
                        start=(k == 0), stop=(k == 7)),
                        reads=("x1n", "wpool"),
                        writes=("ps_a",) if k in (0, 7) else (),
                        inc=1 if k in (0, 7) else 0)
                b.op("scalar", lambda E, nsub=nsub: E.activation(
                    kqf[:, nsub * 512:(nsub + 1) * 512], ps_a[:], AF.Copy),
                    reads=("ps_a",), writes=("Ekqf" + Z,))
            b.op("vector", lambda E, half=half: E.tensor_add(
                kqf[:], kqf[:], qkbrep[:, half * 1024:(half + 1) * 1024]),
                reads=("Ekqf" + Z, "qkbrep"), writes=("Ekqf" + Z,))
            b.op("vector", lambda E: E.tensor_scalar(
                sca[:], kqf[:], 0.0, None, ALU.min),
                reads=("Ekqf" + Z,), writes=("Esca" + Z,))
            b.op("scalar", lambda E: E.activation(sca[:], sca[:], AF.Exp),
                 reads=("Esca" + Z,), writes=("Esca" + Z,))
            b.op("vector", lambda E: E.tensor_scalar(
                kqf[:], kqf[:], 0.0, None, ALU.max),
                reads=("Ekqf" + Z,), writes=("Ekqf" + Z,))
            b.op("vector", lambda E: E.tensor_add(kq[:], kqf[:], sca[:]),
                 reads=("Ekqf" + Z, "Esca" + Z), writes=("Ekq" + Z,))
            b.op("vector", lambda E: E.tensor_mul(
                kqr[:, 0:512], cost[:], kq[:, 0:1024:2]),
                reads=("Ecost" + Z, "Ekq" + Z), writes=("Ekqr" + Z,))
            b.op("vector", lambda E: E.tensor_mul(
                sca[:, 0:512], sint[:], kq[:, 1:1024:2]),
                reads=("Esint" + Z, "Ekq" + Z), writes=("Esca" + Z,))
            b.op("vector", lambda E: E.tensor_sub(
                kqr[:, 0:512], kqr[:, 0:512], sca[:, 0:512]),
                reads=("Ekqr" + Z, "Esca" + Z), writes=("Ekqr" + Z,))
            b.op("vector", lambda E: E.tensor_mul(
                kqr[:, 512:1024], cost[:], kq[:, 1:1024:2]),
                reads=("Ecost" + Z, "Ekq" + Z), writes=("Ekqr" + Z,))
            b.op("vector", lambda E: E.tensor_mul(
                sca[:, 0:512], sint[:], kq[:, 0:1024:2]),
                reads=("Esint" + Z, "Ekq" + Z), writes=("Esca" + Z,))
            b.op("vector", lambda E: E.tensor_add(
                kqr[:, 512:1024], kqr[:, 512:1024], sca[:, 0:512]),
                reads=("Ekqr" + Z, "Esca" + Z), writes=("Ekqr" + Z,))

        def p1_tail(i, kq, kqr, v_t, Z):
            for jj in (0, 4):
                for j in range(jj, jj + 4):
                    b.op("tensor", lambda E, i=i, j=j, jj=jj: E.transpose(
                        ps_trb[:, (j - jj) * 128:(j - jj) * 128 + 128],
                        x1n[:, j * T + i * 128:j * T + (i + 1) * 128], eyeb[:]),
                        reads=("x1n", "eyeb"), writes=("ps_trb",))
                for j in range(jj, jj + 4):
                    b.op("scalar", lambda E, j=j, jj=jj, v_t=v_t: E.activation(
                        v_t[:, j * 128:(j + 1) * 128],
                        ps_trb[:, (j - jj) * 128:(j - jj) * 128 + 128], AF.Copy),
                        reads=("ps_trb",), writes=("Ev_t" + Z,))
            for h in range(NH):
                b.op("tensor", lambda E, i=i, h=h, kqr=kqr, v_t=v_t: E.matmul(
                    ps_kv[:, h * 128:(h + 1) * 128], kqr[:, h * 128:(h + 1) * 128],
                    v_t[:, h * 128:(h + 1) * 128],
                    start=(i == 0 and h % 4 == 0),
                    stop=(i == NT - 1 and h % 4 == 3)),
                    reads=("Ekqr" + Z, "Ev_t" + Z),
                    writes=("ps_kv",) if i in (0, NT - 1) else (),
                    inc=1 if i in (0, NT - 1) else 0)
            for half in range(2):
                b.op("tensor", lambda E, i=i, half=half, kq=kq: E.matmul(
                    ps_km[:, half * 512:(half + 1) * 512], onesb[:, 0:1],
                    kq[:, half * 512:(half + 1) * 512],
                    start=(i == 0), stop=(i == NT - 1)),
                    reads=("onesb", "Ekq" + Z),
                    writes=("ps_km",) if i in (0, NT - 1) else (),
                    inc=1 if i in (0, NT - 1) else 0)

        def p1_args(i):
            p = i % 2
            return (p, tcol_[p], ang_[p], ang2_[p], angi_[p], sint_[p], cost_[p],
                    sca_[p], kqf_[p], kq_[p], kqr_[p], v_t_[p])

        for base in range(0, NT, 2):
            pr = [(base + u, p1_args(base + u)) for u in range(2)]
            for (i2, (p, tcol, ang, ang2, angi, sint, cost, sca, kqf, kq, kqr, v_t)) in pr:
                rope_tables(i2, tcol, ang, ang2, angi, sint, cost, sca, f"{p}")
            for (i2, (p, tcol, ang, ang2, angi, sint, cost, sca, kqf, kq, kqr, v_t)) in pr:
                qk_stage(i2, 1, kqf, sca, kq, kqr, sint, cost, f"{p}")
            for (i2, (p, tcol, ang, ang2, angi, sint, cost, sca, kqf, kq, kqr, v_t)) in pr:
                p1_tail(i2, kq, kqr, v_t, f"{p}")
        b.op("scalar", lambda E: E.activation(kv_sb[:], ps_kv[:], AF.Copy,
                                              scale=float(SC)),
             reads=("ps_kv",), writes=("kv_sb",))
        b.op("scalar", lambda E: E.activation(kmean[:], ps_km[:], AF.Copy,
                                              scale=float(1.0 / T)),
             reads=("ps_km",), writes=("kmean",))
        for half in range(2):
            b.op("tensor", lambda E, half=half: E.matmul(
                ps_o[:, half * 512:(half + 1) * 512], ones[0:1, :],
                kmean[:, half * 512:(half + 1) * 512], start=True, stop=True),
                reads=("ones", "kmean"), writes=("ps_o",))
        b.op("scalar", lambda E: E.activation(kmrep[:], ps_o[:], AF.Copy),
             reads=("ps_o",), writes=("kmrep",))

        def p2_tail(i, kq, kqr, sca, zden, qrT, otmp, zrep, Z):
            b.op("vector", lambda E, kq=kq, sca=sca: E.tensor_mul(sca[:], kq[:], kmrep[:]),
                 reads=("Ekq" + Z, "kmrep"), writes=("Esca" + Z,))
            b.op("vector", lambda E, sca=sca, zden=zden: E.tensor_reduce(
                zden[:], sca[:].rearrange("p (h d) -> p h d", h=8),
                axis=AX.X, op=ALU.add),
                reads=("Esca" + Z,), writes=("Ezden" + Z,))
            b.op("vector", lambda E, zden=zden: E.tensor_scalar(
                zden[:], zden[:], 1e-6, None, ALU.add),
                reads=("Ezden" + Z,), writes=("Ezden" + Z,))
            b.op("vector", lambda E, zden=zden: E.reciprocal(zden[:], zden[:]),
                 reads=("Ezden" + Z,), writes=("Ezden" + Z,))
            b.op("tensor", lambda E, zden=zden: E.matmul(ps_a[0:8, 0:128], zden[:], eye[:],
                                              start=True, stop=True),
                 reads=("Ezden" + Z, "eye"), writes=("ps_a",))
            b.op("scalar", lambda E, i=i: E.activation(
                zT[:, i * 128:(i + 1) * 128], ps_a[0:8, 0:128], AF.Copy),
                reads=("ps_a",), writes=("zT",))
            for jj in (0, 4):
                for j in range(jj, jj + 4):
                    b.op("tensor", lambda E, j=j, jj=jj, kqr=kqr: E.transpose(
                        ps_trb[:, (j - jj) * 128:(j - jj) * 128 + 128],
                        kqr[:, j * 128:(j + 1) * 128], eyeb[:]),
                        reads=("Ekqr" + Z, "eyeb"), writes=("ps_trb",))
                for j in range(jj, jj + 4):
                    b.op("scalar", lambda E, j=j, jj=jj, qrT=qrT: E.activation(
                        qrT[:, j * 128:(j + 1) * 128],
                        ps_trb[:, (j - jj) * 128:(j - jj) * 128 + 128], AF.Copy),
                        reads=("ps_trb",), writes=("EqrT" + Z,))
            for h in range(NH):
                b.op("tensor", lambda E, h=h, qrT=qrT: E.matmul(
                    ps_o[:, h * 128:(h + 1) * 128], kv_sb[:, h * 128:(h + 1) * 128],
                    qrT[:, h * 128:(h + 1) * 128],
                    start=(h % 4 == 0), stop=(h % 4 == 3)),
                    reads=("kv_sb", "EqrT" + Z), writes=("ps_o",))
            for hq in range(2):
                for h4 in range(4):
                    h = hq * 4 + h4
                    b.op("tensor", lambda E, h=h, h4=h4, i=i: E.matmul(
                        ps_a[:, h4 * 128:(h4 + 1) * 128],
                        sel8[:, h * 128:(h + 1) * 128],
                        zT[:, i * 128:(i + 1) * 128],
                        start=(h4 == 0), stop=(h4 == 3)),
                        reads=("sel8", "zT"), writes=("ps_a",))
                b.op("scalar", lambda E, hq=hq, zrep=zrep: E.activation(
                    zrep[:], ps_a[:], AF.Copy),
                    reads=("ps_a",), writes=("Ezrep" + Z,))
                for h4 in range(4):
                    h = hq * 4 + h4
                    b.op("vector", lambda E, h=h, h4=h4, otmp=otmp, zrep=zrep: E.tensor_mul(
                        otmp[:, h * 128:(h + 1) * 128],
                        ps_o[:, h * 128:(h + 1) * 128],
                        zrep[:, h4 * 128:(h4 + 1) * 128]),
                        reads=("ps_o", "Ezrep" + Z), writes=("Eotmp" + Z,))
            for h in range(NH):
                b.op("vector", lambda E, h=h, i=i, otmp=otmp: E.tensor_add(
                    s1[:, h * T + i * 128:h * T + (i + 1) * 128],
                    s1[:, h * T + i * 128:h * T + (i + 1) * 128],
                    otmp[:, h * 128:(h + 1) * 128]),
                    reads=("s1", "Eotmp" + Z), writes=("s1",))

        for base in range(0, NT, 2):
            pr = [(base + u, p1_args(base + u)) for u in range(2)]
            for (i2, (p, tcol, ang, ang2, angi, sint, cost, sca, kqf, kq, kqr, v_t)) in pr:
                rope_tables(i2, tcol, ang, ang2, angi, sint, cost, sca, f"{p}")
            for (i2, (p, tcol, ang, ang2, angi, sint, cost, sca, kqf, kq, kqr, v_t)) in pr:
                qk_stage(i2, 0, kqf, sca, kq, kqr, sint, cost, f"{p}")
            for (i2, (p, tcol, ang, ang2, angi, sint, cost, sca, kqf, kq, kqr, v_t)) in pr:
                p2_tail(i2, kq, kqr, sca, zden_[p], qrT_[p], otmp_[p], zrep_[p], f"{p}")
        for nt in range(4):
            b.dma("sync", P["s1_hbm"][:, :, nt * 512:(nt + 1) * 512],
                  s1[:, :].rearrange("p (g t) -> p g t", g=8)[:, :, nt * 512:(nt + 1) * 512],
                  reads=("s1",))


def emit_F(b, nc, P):
    NT = T // 128
    NT = T // 128
    with contextlib.ExitStack() as st:
        e = st.enter_context
        x1n = e(nc.sbuf_tensor("F_x1n", [128, 8 * T], BF16))
        wp = [e(nc.sbuf_tensor(f"F_wp{i}", [128, 2048], BF16)) for i in range(8)]
        eye = e(nc.sbuf_tensor("F_sb_eye", [128, 128], F32))
        ones = e(nc.sbuf_tensor("F_sb_ones", [128, 128], F32))
        sel8 = e(nc.sbuf_tensor("F_sb_sel8", [E0, E0 * 128], F32))
        projP = e(nc.sbuf_tensor("F_sb_projP", [128, 48], BF16))
        anchT = e(nc.sbuf_tensor("F_sb_anchT", [6, 6], F32))
        cscrep = e(nc.sbuf_tensor("F_cscrep", [128, 1], F32))
        qtok = e(nc.sbuf_tensor("F_qtok", [128, 8], F32))
        dott = e(nc.sbuf_tensor("F_dott", [128, 8], F32))
        o1 = e(nc.sbuf_tensor("F_o1", [128, 8], F32))
        o2 = e(nc.sbuf_tensor("F_o2", [128, 8], F32))
        scr = e(nc.sbuf_tensor("F_scr", [128, 8], F32))
        stat = e(nc.sbuf_tensor("F_stat", [128, 8], F32))
        onecol = e(nc.sbuf_tensor("F_onecol", [128, 8], F32))
        ewm = e(nc.sbuf_tensor("F_ewm", [128, 8], F32))
        ewT = e(nc.sbuf_tensor("F_ewT", [8, T], F32))
        mrep = e(nc.sbuf_tensor("F_mrep", [128, T], BF16))
        h_e = e(nc.sbuf_tensor("F_h_e", [128, 4 * T], BF16))
        yacc = e(nc.sbuf_tensor("F_yacc", [128, 8 * T], F32))
        u_f = e(nc.sbuf_tensor("F_u_f", [128, 512], F32))
        sq_f = e(nc.sbuf_tensor("F_sq_f", [128, 512], F32))
        th_f = e(nc.sbuf_tensor("F_th_f", [128, 512], F32))
        gtile = e(nc.sbuf_tensor("F_gtile", [128, 8 * 512], BF16))
        s1c = e(nc.sbuf_tensor("F_s1c", [128, 8 * 512], BF16))
        x1c = e(nc.sbuf_tensor("F_x1c", [128, 8 * 512], F32))
        outtok = e(nc.sbuf_tensor("F_outtok", [128, D], F32))
        ps8 = [e(nc.psum_tensor(f"F_ps8_{i}", [128, 512], F32)) for i in range(8)]

        b.dma("sync", eye[:], P["eye_in"], writes=("eye",))
        b.dma("sync", ones[:], P["ones_in"], writes=("ones",))
        b.dma("sync", sel8[:], P["sel8_in"], writes=("sel8",))
        b.dma("gpsimd", projP[:], P["projP_in"], writes=("projP",))
        b.dma("sync", anchT[:], P["anchT_in"], writes=("anchT",))
        b.dma("sync", cscrep[0:1, 0:1], P["temp_in"], writes=("csc0",))
        b.op("vector", lambda E: E.memset(onecol[:], 1.0), writes=("onecol",))
        b.op("vector", lambda E: E.tensor_scalar(
            cscrep[0:1, 0:1], cscrep[0:1, 0:1], 0.1, None, ALU.max),
            reads=("csc0",), writes=("csc0",))
        b.op("vector", lambda E: E.reciprocal(cscrep[0:1, 0:1], cscrep[0:1, 0:1]),
             reads=("csc0",), writes=("csc0",))
        b.op("vector", lambda E: E.tensor_scalar(
            cscrep[0:1, 0:1], cscrep[0:1, 0:1], 0.5, None, ALU.mult),
            reads=("csc0",), writes=("csc0",))
        b.op("tensor", lambda E: E.matmul(ps8[0][:, 0:1], ones[0:1, :],
                                          cscrep[0:1, 0:1], start=True, stop=True),
             reads=("ones", "csc0"), writes=("ps0",))
        b.op("scalar", lambda E: E.activation(cscrep[:], ps8[0][:, 0:1], AF.Copy),
             reads=("ps0",), writes=("cscrep",))
        b.dma("sync", x1n[:].rearrange("p (g t) -> p g t", g=8),
              P["x1n_hbm"][:, :, :], writes=("x1n",))

        for i in range(NT):
            for k in range(8):
                b.op("tensor", lambda E, i=i, k=k: E.matmul(
                    ps8[0][:, 0:6], x1n[:, k * T + i * 128:k * T + (i + 1) * 128],
                    projP[:, k * 6:(k + 1) * 6],
                    start=(k == 0), stop=(k == 7)),
                    reads=("x1n", "projP"),
                    writes=("ps0",) if k in (0, 7) else (),
                    inc=1 if k in (0, 7) else 0)
            b.op("scalar", lambda E: E.activation(
                qtok[:, 0:6], ps8[0][:, 0:6], AF.Tanh, scale=float(1.0 / 0.3)),
                reads=("ps0",), writes=("qtok",))
            b.op("tensor", lambda E: E.matmul(ps8[1][0:6, 0:128], qtok[:, 0:6],
                                              eye[:], start=True, stop=True),
                 reads=("qtok", "eye"), writes=("ps1",))
            b.op("scalar", lambda E: E.activation(
                u_f[0:6, 0:128], ps8[1][0:6, 0:128], AF.Copy),
                reads=("ps1",), writes=("u_f",))
            b.op("tensor", lambda E: E.matmul(ps8[2][:, 0:6], u_f[0:6, 0:128],
                                              anchT[:], start=True, stop=True),
                 reads=("u_f", "anchT"), writes=("ps2",))
            b.op("vector", lambda E: E.tensor_copy(dott[:, 0:6], ps8[2][:, 0:6]),
                 reads=("ps2",), writes=("dott",))

            def top1(dst, dstk, srck, src, gcol):
                b.op("vector", lambda E: E.reduce_max(
                    stat[:, gcol:gcol + 1], src[:, 0:6], axis=AX.X),
                    reads=(srck,), writes=("stat",))
                b.op("vector", lambda E: E.tensor_scalar(
                    dst[:, 0:6], src[:, 0:6], stat[:, gcol:gcol + 1], None,
                    ALU.is_equal),
                    reads=(srck, "stat"), writes=(dstk,))
                b.op("vector", lambda E: E.tensor_tensor_scan(
                    scr[:, 0:6], onecol[:, 0:6], dst[:, 0:6], 0.0,
                    ALU.mult, ALU.add),
                    reads=(dstk, "onecol"), writes=("scr",))
                b.op("vector", lambda E: E.tensor_scalar(
                    scr[:, 0:6], scr[:, 0:6], 1.0, None, ALU.is_equal),
                    reads=("scr",), writes=("scr",))
                b.op("vector", lambda E: E.tensor_mul(
                    dst[:, 0:6], dst[:, 0:6], scr[:, 0:6]),
                    reads=(dstk, "scr"), writes=(dstk,))

            top1(o1, "o1", "dott", dott, 0)
            b.op("vector", lambda E: E.scalar_tensor_tensor(
                qtok[:, 0:6], o1[:, 0:6], -1e30, dott[:, 0:6], ALU.mult, ALU.add),
                reads=("o1", "dott"), writes=("qtok",))
            top1(o2, "o2", "qtok", qtok, 1)
            b.op("vector", lambda E: E.tensor_sub(
                stat[:, 2:3], stat[:, 0:1], stat[:, 1:2]),
                reads=("stat",), writes=("stat",))
            b.op("vector", lambda E: E.tensor_mul(
                stat[:, 2:3], stat[:, 2:3], cscrep[:, 0:1]),
                reads=("stat", "cscrep"), writes=("stat",))
            b.op("scalar", lambda E: E.activation(
                stat[:, 3:4], stat[:, 2:3], AF.Sigmoid),
                reads=("stat",), writes=("stat",))
            b.op("vector", lambda E: E.tensor_scalar(
                stat[:, 4:5], stat[:, 3:4], -1.0, 1.0, ALU.mult, ALU.add),
                reads=("stat",), writes=("stat",))
            b.op("vector", lambda E: E.tensor_scalar(
                ewm[:, 0:6], o1[:, 0:6], stat[:, 3:4], None, ALU.mult),
                reads=("o1", "stat"), writes=("ewm",))
            b.op("vector", lambda E: E.tensor_scalar(
                scr[:, 0:6], o2[:, 0:6], stat[:, 4:5], None, ALU.mult),
                reads=("o2", "stat"), writes=("scr",))
            b.op("vector", lambda E: E.tensor_add(
                ewm[:, 0:6], ewm[:, 0:6], scr[:, 0:6]),
                reads=("ewm", "scr"), writes=("ewm",))
            b.op("vector", lambda E: E.memset(ewm[:, 6:8], 0.0), writes=("ewm",))
            b.op("tensor", lambda E: E.matmul(ps8[3][0:8, 0:128], ewm[:],
                                              eye[:], start=True, stop=True),
                 reads=("ewm", "eye"), writes=("ps3",))
            b.op("scalar", lambda E, i=i: E.activation(
                ewT[:, i * 128:(i + 1) * 128], ps8[3][0:8, 0:128], AF.Copy),
                reads=("ps3",), writes=("ewT",))

        b.op("vector", lambda E: E.memset(yacc[:], 0.0), writes=("yacc",))
        for ex in range(E1):
            for nt in range(4):
                b.op("tensor", lambda E, ex=ex, nt=nt: E.matmul(
                    ps8[4][:], sel8[0:6, ex * 128:(ex + 1) * 128],
                    ewT[0:6, nt * 512:(nt + 1) * 512], start=True, stop=True),
                    reads=("sel8", "ewT"), writes=("ps4",))
                b.op("scalar", lambda E, nt=nt: E.activation(
                    mrep[:, nt * 512:(nt + 1) * 512], ps8[4][:], AF.Copy),
                    reads=("ps4",), writes=("mrep",))
            for k in range(8):
                b.dma("gpsimd", wp[k // 4][:, (k % 4) * 512:(k % 4 + 1) * 512],
                      P["w1f_in"][ex, k * 128:(k + 1) * 128, :],
                      writes=(f"wp{k // 4}",))
            for ogl in range(4):
                for nt in range(4):
                    for k in range(8):
                        b.op("tensor", lambda E, ogl=ogl, nt=nt, k=k: E.matmul(
                            ps8[0][:],
                            wp[k // 4][:, (k % 4) * 512 + ogl * 128:
                                       (k % 4) * 512 + (ogl + 1) * 128],
                            x1n[:, k * T + nt * 512:k * T + (nt + 1) * 512],
                            start=(k == 0), stop=(k == 7)),
                            reads=(f"wp{k // 4}", "x1n"),
                            writes=("ps0",) if k in (0, 7) else (),
                            inc=1 if k in (0, 7) else 0)
                    b.op("scalar", lambda E: E.activation(u_f[:], ps8[0][:], AF.Copy),
                         reads=("ps0",), writes=("u_f",))
                    b.op("scalar", lambda E: E.activation(sq_f[:], u_f[:], AF.Square),
                         reads=("u_f",), writes=("sq_f",))
                    b.op("vector", lambda E: E.tensor_mul(sq_f[:], sq_f[:], u_f[:]),
                         reads=("sq_f", "u_f"), writes=("sq_f",))
                    b.op("vector", lambda E: E.scalar_tensor_tensor(
                        sq_f[:], sq_f[:], 0.044715, u_f[:], ALU.mult, ALU.add),
                        reads=("sq_f", "u_f"), writes=("sq_f",))
                    b.op("scalar", lambda E: E.activation(
                        th_f[:], sq_f[:], AF.Tanh, scale=0.7978845608028654),
                        reads=("sq_f",), writes=("th_f",))
                    b.op("vector", lambda E: E.tensor_scalar(
                        th_f[:], th_f[:], 1.0, 0.5, ALU.add, ALU.mult),
                        reads=("th_f",), writes=("th_f",))
                    b.op("vector", lambda E: E.tensor_mul(th_f[:], th_f[:], u_f[:]),
                         reads=("th_f", "u_f"), writes=("th_f",))
                    b.op("vector", lambda E, ogl=ogl, nt=nt: E.tensor_mul(
                        h_e[:, ogl * T + nt * 512:ogl * T + (nt + 1) * 512],
                        th_f[:], mrep[:, nt * 512:(nt + 1) * 512]),
                        reads=("th_f", "mrep"), writes=("h_e",))
            for k in range(4):
                b.dma("gpsimd", wp[4 + k // 2][:, (k % 2) * 1024:(k % 2 + 1) * 1024],
                      P["w2f_in"][ex, k * 128:(k + 1) * 128, :],
                      writes=(f"wp{4 + k // 2}",))
            for nt in range(4):
                for k in range(4):
                    for og in range(8):
                        b.op("tensor", lambda E, k=k, og=og, nt=nt: E.matmul(
                            ps8[og][:],
                            wp[4 + k // 2][:, (k % 2) * 1024 + og * 128:
                                           (k % 2) * 1024 + (og + 1) * 128],
                            h_e[:, k * T + nt * 512:k * T + (nt + 1) * 512],
                            start=(k == 0), stop=(k == 3)),
                            reads=(f"wp{4 + k // 2}", "h_e"),
                            writes=(f"ps{og}",) if k in (0, 3) else (),
                            inc=1 if k in (0, 3) else 0)
                for og in range(8):
                    b.op("vector", lambda E, og=og, nt=nt: E.scalar_tensor_tensor(
                        yacc[:, og * T + nt * 512:og * T + (nt + 1) * 512],
                        ps8[og][:], 1.0,
                        yacc[:, og * T + nt * 512:og * T + (nt + 1) * 512],
                        ALU.mult, ALU.add),
                        reads=(f"ps{og}", "yacc"), writes=("yacc",))
        for nt in range(4):
            c0 = nt * 512
            for og in range(8):
                b.op("scalar", lambda E, og=og, nt=nt: E.activation(
                    gtile[:, og * 512:(og + 1) * 512],
                    yacc[:, og * T + nt * 512:og * T + (nt + 1) * 512], AF.Silu),
                    reads=("yacc",), writes=("gtile",))
            b.dma("sync", s1c[:].rearrange("p (g t) -> p g t", g=8),
                  P["s1_hbm"][:, :, c0:c0 + 512], writes=("s1c",))
            b.dma("sync", x1c[:].rearrange("p (g t) -> p g t", g=8),
                  P["x1_hbm"][:, :, c0:c0 + 512], writes=("x1c",))
            for og in range(8):
                b.op("vector", lambda E, og=og: E.tensor_mul(
                    gtile[:, og * 512:(og + 1) * 512],
                    gtile[:, og * 512:(og + 1) * 512],
                    s1c[:, og * 512:(og + 1) * 512]),
                    reads=("gtile", "s1c"), writes=("gtile",))
                b.op("vector", lambda E, og=og: E.tensor_add(
                    x1c[:, og * 512:(og + 1) * 512],
                    x1c[:, og * 512:(og + 1) * 512],
                    gtile[:, og * 512:(og + 1) * 512]),
                    reads=("gtile", "x1c"), writes=("x1c",))
            for tt in range(4):
                for jj in (0, 4):
                    for j in range(jj, jj + 4):
                        b.op("tensor", lambda E, j=j, jj=jj, tt=tt: E.transpose(
                            ps8[1][:, (j - jj) * 128:(j - jj) * 128 + 128],
                            x1c[:, j * 512 + tt * 128:j * 512 + (tt + 1) * 128],
                            eye[:]),
                            reads=("x1c", "eye"), writes=("ps1",))
                    for j in range(jj, jj + 4):
                        b.op("scalar", lambda E, j=j, jj=jj: E.activation(
                            outtok[:, j * 128:(j + 1) * 128],
                            ps8[1][:, (j - jj) * 128:(j - jj) * 128 + 128], AF.Copy),
                            reads=("ps1",), writes=("outtok",))
                b.dma("sync", P["out_hbm"][c0 + tt * 128:c0 + (tt + 1) * 128, :],
                      outtok[:], reads=("outtok",))




def emit_counts_mk(b, nc, P, st):
    NTT = 4
    if True:
        e = st.enter_context
        xTc_ = [e(nc.sbuf_tensor(f"N_xTc{i}", [128, 8 * 512], F32)) for i in range(4)]
        wr2 = e(nc.sbuf_tensor("N_wr2", [128, E0 * 8], F32))
        onesN = e(nc.sbuf_tensor("N_ones", [128, 128], F32))
        srow_ = [e(nc.sbuf_tensor(f"N_srow{i}", [1, 4 * 512], F32)) for i in range(4)]
        Srep = e(nc.sbuf_tensor("N_Srep", [128, E0], F32))
        brrep = e(nc.sbuf_tensor("N_brrep", [128, E0], F32))
        Srow = e(nc.sbuf_tensor("N_Srow", [1, E0], F32))
        mucol_ = [e(nc.sbuf_tensor(f"N_mucol{i}", [128, 2], F32)) for i in range(4)]
        ltok_ = [e(nc.sbuf_tensor(f"N_ltok{i}", [128, E0], F32)) for i in range(4)]
        o1_ = [e(nc.sbuf_tensor(f"N_o1{i}", [128, E0], F32)) for i in range(4)]
        o2_ = [e(nc.sbuf_tensor(f"N_o2{i}", [128, E0], F32)) for i in range(4)]
        scr_ = [e(nc.sbuf_tensor(f"N_scr{i}", [128, E0], F32)) for i in range(4)]
        stat_ = [e(nc.sbuf_tensor(f"N_stat{i}", [128, 8], F32)) for i in range(4)]
        onecol = e(nc.sbuf_tensor("N_onecol", [128, E0], F32))
        pm64 = e(nc.sbuf_tensor("N_pm64", [1, 64], F32))
        acnt = e(nc.sbuf_tensor("N_acnt", [1, 64], F32))
        offsr = e(nc.sbuf_tensor("N_offsr", [1, E0], F32))
        cval = e(nc.sbuf_tensor("N_cval", [128, 1], F32))
        ps_sqx_ = [e(nc.psum_tensor(f"N_ps_sq{i}", [64, 512], F32)) for i in range(2)]
        ps_l2x_ = [e(nc.psum_tensor(f"N_ps_l2{i}", [128, 32], F32)) for i in range(2)]
        ps_mx_ = [e(nc.psum_tensor(f"N_ps_misc{i}", [128, 32], F32)) for i in range(2)]
        ps_b = ps_mx_[0][:, 16:24]

        b.dma("sync", wr2[:], P["wr_in"], writes=("Nwr",))
        b.dma("sync", onesN[:], P["ones_in"], writes=("Nones",))
        b.dma("sync", pm64[:], P["pm64_in"], writes=("Npm",))
        b.dma("sync", brrep[0:1, :], P["br_in"].rearrange("(o e) -> o e", o=1),
              writes=("Nbr",))
        b.op("vector", lambda E: E.memset(acnt[:], 0.0), writes=("Nacnt",))
        b.op("vector", lambda E: E.memset(onecol[:], 1.0), writes=("Nonecol",))
        b.op("vector", lambda E: E.memset(cval[:], 1.0), writes=("Ncval",))
        b.op("tensor", lambda E: E.matmul(ps_sqx_[0][0:1, 0:64], onesN[:, 0:1], wr2[:],
                                          start=True, stop=True),
             reads=("Nones", "Nwr"), writes=("Nps_sq0",))
        b.op("vector", lambda E: E.tensor_copy(srow_[0][:, 0:64], ps_sqx_[0][0:1, 0:64]),
             reads=("Nps_sq0",), writes=("Nsrow0",))
        b.op("vector", lambda E: E.tensor_reduce(
            Srow[:], srow_[0][0:1, 0:64].rearrange("p (k e) -> p e k", k=8),
            axis=AX.X, op=ALU.add),
            reads=("Nsrow0",), writes=("NSrow",))
        b.op("tensor", lambda E: E.matmul(ps_b[:], onesN[0:1, :], Srow[:],
                                          start=True, stop=True),
             reads=("Nones", "NSrow"), writes=("Nps_misc0",))
        b.op("scalar", lambda E: E.activation(Srep[:], ps_b[:], AF.Copy),
             reads=("Nps_misc0",), writes=("NSrep",))
        b.op("tensor", lambda E: E.matmul(ps_b[:], onesN[0:1, :], brrep[0:1, :],
                                          start=True, stop=True),
             reads=("Nones", "Nbr"), writes=("Nps_misc0",))
        b.op("scalar", lambda E: E.activation(brrep[:], ps_b[:], AF.Copy),
             reads=("Nps_misc0",), writes=("Nbrrep",))

        def top1(dst, dstk, src_t, srck, gcol, stat, scr, Z):
            b.op("vector", lambda E: E.reduce_max(
                stat[:, gcol:gcol + 1], src_t[:], axis=AX.X),
                reads=(srck,), writes=("Nstat" + Z,))
            b.op("vector", lambda E: E.tensor_scalar(
                dst[:], src_t[:], stat[:, gcol:gcol + 1], None, ALU.is_equal),
                reads=(srck, "Nstat" + Z), writes=(dstk,))
            b.op("vector", lambda E: E.tensor_tensor_scan(
                scr[:], onecol[:], dst[:], 0.0, ALU.mult, ALU.add),
                reads=(dstk, "Nonecol"), writes=("Nscr" + Z,))
            b.op("vector", lambda E: E.tensor_scalar(
                scr[:], scr[:], 1.0, None, ALU.is_equal),
                reads=("Nscr" + Z,), writes=("Nscr" + Z,))
            b.op("vector", lambda E: E.tensor_mul(dst[:], dst[:], scr[:]),
                 reads=(dstk, "Nscr" + Z), writes=(dstk,))

        def _c_s0(bo, ci, c0, xTc, srow, mucol, ltok, o1, o2, scr, stat,
                  ps_s, ps_q, ps_l2, ps_m, ps_c, Z, Z2):
            if True:
                b.dma("sync", xTc[:].rearrange("p (g t) -> p g t", g=8),
                      P["xTf_in"][bo].rearrange("(g p) t -> p g t", p=128)[:, :, c0:c0 + 512],
                      writes=("NxTc" + Z,))
                for g in range(8):
                    b.op("tensor", lambda E, g=g: E.matmul(
                        ps_s[:], onesN[:, 0:1], xTc[:, g * 512:(g + 1) * 512],
                        start=(g == 0), stop=(g == 7)),
                        reads=("Nones", "NxTc" + Z),
                        writes=("Nps_sq" + Z2,) if g in (0, 7) else (),
                        inc=1 if g in (0, 7) else 0)

        def _c_s1(bo, ci, c0, xTc, srow, mucol, ltok, o1, o2, scr, stat,
                  ps_s, ps_q, ps_l2, ps_m, ps_c, Z, Z2):
            if True:
                for tt in range(NTT):
                    for k in range(8):
                        b.op("tensor", lambda E, tt=tt, k=k: E.matmul(
                            ps_l2[tt][:], xTc[:, k * 512 + tt * 128:k * 512 + (tt + 1) * 128],
                            wr2[:, k * 8:(k + 1) * 8],
                            start=(k == 0), stop=(k == 7)),
                            reads=("NxTc" + Z, "Nwr"),
                            writes=("Nps_l2" + Z2,) if k in (0, 7) else (),
                            inc=1 if k in (0, 7) else 0)

        def _c_s2(bo, ci, c0, xTc, srow, mucol, ltok, o1, o2, scr, stat,
                  ps_s, ps_q, ps_l2, ps_m, ps_c, Z, Z2):
            if True:
                b.op("scalar", lambda E: E.activation(xTc[:], xTc[:], AF.Square),
                     reads=("NxTc" + Z,), writes=("NxTc" + Z,))
                for g in range(8):
                    b.op("tensor", lambda E, g=g: E.matmul(
                        ps_q[:], onesN[:, 0:1], xTc[:, g * 512:(g + 1) * 512],
                        start=(g == 0), stop=(g == 7)),
                        reads=("Nones", "NxTc" + Z),
                        writes=("Nps_sq" + Z2,) if g in (0, 7) else (),
                        inc=1 if g in (0, 7) else 0)

        def _c_s3(bo, ci, c0, xTc, srow, mucol, ltok, o1, o2, scr, stat,
                  ps_s, ps_q, ps_l2, ps_m, ps_c, Z, Z2):
            if True:
                b.op("scalar", lambda E: E.activation(
                    srow[:, 0:512], ps_s[:], AF.Copy, scale=1.0 / D),
                    reads=("Nps_sq" + Z2,), writes=("Nsrow" + Z,))
                b.op("scalar", lambda E: E.activation(
                    srow[:, 512:1024], ps_q[:], AF.Copy, scale=1.0 / D),
                    reads=("Nps_sq" + Z2,), writes=("Nsrow" + Z,))
                b.op("vector", lambda E: E.tensor_mul(
                    srow[:, 1024:1536], srow[:, 0:512], srow[:, 0:512]),
                    reads=("Nsrow" + Z,), writes=("Nsrow" + Z,))
                b.op("vector", lambda E: E.tensor_sub(
                    srow[:, 1024:1536], srow[:, 512:1024], srow[:, 1024:1536]),
                    reads=("Nsrow" + Z,), writes=("Nsrow" + Z,))
                b.op("scalar", lambda E: E.activation(
                    srow[:, 1536:2048], srow[:, 1024:1536], AF.Ln,
                    bias=P["cval_eps"]),
                    reads=("Nsrow" + Z,), writes=("Nsrow" + Z,))
                b.op("scalar", lambda E: E.activation(
                    srow[:, 1536:2048], srow[:, 1536:2048], AF.Exp, scale=-0.5),
                    reads=("Nsrow" + Z,), writes=("Nsrow" + Z,))

        def _c_s4(bo, ci, c0, xTc, srow, mucol, ltok, o1, o2, scr, stat,
                  ps_s, ps_q, ps_l2, ps_m, ps_c, Z, Z2):
            if True:
                for tt in range(NTT):
                    b.op("tensor", lambda E, tt=tt: E.matmul(
                        ps_m[:, 0:1], srow[0:1, tt * 128:(tt + 1) * 128],
                        cval[0:1, 0:1], start=True, stop=False),
                        reads=("Nsrow" + Z, "Ncval"), writes=("Nps_misc" + Z2,))
                    b.op("tensor", lambda E, tt=tt: E.matmul(
                        ps_m[:, 1:2], srow[0:1, 1536 + tt * 128:1536 + (tt + 1) * 128],
                        cval[0:1, 0:1], start=False, stop=True),
                        reads=("Nsrow" + Z, "Ncval"), writes=("Nps_misc" + Z2,))
                    b.op("scalar", lambda E: E.activation(mucol[:], ps_m[:], AF.Copy),
                         reads=("Nps_misc" + Z2,), writes=("Nmucol" + Z,))
                    b.op("vector", lambda E, tt=tt: E.scalar_tensor_tensor(
                        ltok[:], Srep[:], mucol[:, 0:1], ps_l2[tt][:],
                        ALU.mult, ALU.subtract),
                        reads=("NSrep", "Nmucol" + Z, "Nps_l2" + Z2), writes=("Nltok" + Z,))
                    b.op("vector", lambda E: E.tensor_scalar(
                        ltok[:], ltok[:], mucol[:, 1:2], -1.0, ALU.mult, ALU.mult),
                        reads=("Nltok" + Z, "Nmucol" + Z), writes=("Nltok" + Z,))
                    b.op("vector", lambda E: E.tensor_add(ltok[:], ltok[:], brrep[:]),
                         reads=("Nltok" + Z, "Nbrrep"), writes=("Nltok" + Z,))
                    top1(o1, "No1" + Z, ltok, "Nltok" + Z, 4, stat, scr, Z)
                    b.op("vector", lambda E: E.scalar_tensor_tensor(
                        ltok[:], o1[:], -1e30, ltok[:], ALU.mult, ALU.add),
                        reads=("No1" + Z, "Nltok" + Z), writes=("Nltok" + Z,))
                    top1(o2, "No2" + Z, ltok, "Nltok" + Z, 5, stat, scr, Z)
                    b.op("vector", lambda E: E.tensor_add(o1[:], o1[:], o2[:]),
                         reads=("No1" + Z, "No2" + Z), writes=("No1" + Z,))
                    b.op("tensor", lambda E: E.matmul(
                        ps_c[:], onesN[:, 0:1], o1[:], start=True, stop=True),
                        reads=("Nones", "No1" + Z), writes=("Nps_misc" + Z2,))
                    b.op("vector", lambda E, bo=bo: E.tensor_add(
                        acnt[0:1, bo * 8:(bo + 1) * 8],
                        acnt[0:1, bo * 8:(bo + 1) * 8], ps_c[:]),
                        reads=("Nps_misc" + Z2, "Nacnt"), writes=("Nacnt",))

        def cargs(bo, ci):
            pz = (bo * 4 + ci) % 4
            pz2 = (bo * 4 + ci) % 2
            return (bo, ci, ci * 512, xTc_[pz], srow_[pz], mucol_[pz],
                    ltok_[pz], o1_[pz], o2_[pz], scr_[pz], stat_[pz],
                    ps_sqx_[pz2][0:1, 0:512], ps_sqx_[pz2][32:33, 0:512],
                    [ps_l2x_[pz2][:, i * 8:(i + 1) * 8] for i in range(4)],
                    ps_mx_[pz2][:, 0:2], ps_mx_[pz2][0:1, 8:16],
                    f"{pz}", f"{pz2}")

        def do_bo(bo):
            for base in (0, 2):
                arglist = [cargs(bo, base), cargs(bo, base + 1)]
                for st_fn in (_c_s0, _c_s1, _c_s2, _c_s3, _c_s4):
                    for a in arglist:
                        st_fn(*a)

        def finale():
            b.op("vector", lambda E: E.tensor_mul(acnt[:], acnt[:], pm64[:]),
                 reads=("Nacnt", "Npm"), writes=("Nacnt",))
            b.op("vector", lambda E: E.tensor_reduce(
                offsr[:], acnt[0:1, 0:64].rearrange("p (k e) -> p e k", k=8),
                axis=AX.X, op=ALU.add),
                reads=("Nacnt",), writes=("Noffsr",))
            b.op("tensor", lambda E: E.matmul(ps_b[:], onesN[0:1, :], offsr[:],
                                              start=True, stop=True),
                 reads=("Nones", "Noffsr"), writes=("Nps_misc0",))
            b.op("scalar", lambda E: E.activation(P["offs_rep"][:], ps_b[:], AF.Copy),
                 reads=("Nps_misc0",), writes=("offs_rep",))


        return do_bo, finale

def build_fused():
    nc = bass.Bass("TRN2", target_bir_lowering=False, debug=False)
    ap = lambda name, shape, dt_, kind: nc.dram_tensor(name, shape, dt_, kind=kind).ap()
    P = {}
    P["x_in"] = ap("x", [T, D], F32, "ExternalInput")
    P["wr_in"] = ap("w_router", [128, E0 * 8], F32, "ExternalInput")
    P["br_in"] = ap("b_router", [E0], F32, "ExternalInput")
    P["eye_in"] = ap("eye", [128, 128], F32, "ExternalInput")
    P["triu_in"] = ap("triu", [128, 128], F32, "ExternalInput")
    P["ones_in"] = ap("ones", [128, 128], F32, "ExternalInput")
    P["wip_in"] = ap("wipT", [D, 4096], F32, "ExternalInput")
    P["wxp_in"] = ap("wxpP", [128, 16 * 96], F32, "ExternalInput")
    P["wdt_in"] = ap("wdtT", [64, DI], F32, "ExternalInput")
    P["wop_in"] = ap("wopT", [DI, D], F32, "ExternalInput")
    P["wglu_in"] = ap("wgluT", [D, DI], F32, "ExternalInput")
    P["conv_in"] = ap("convP", [128, 64], F32, "ExternalInput")
    P["convb_in"] = ap("convbP", [128, 16], F32, "ExternalInput")
    P["dtb_in"] = ap("dtbP", [128, 16], F32, "ExternalInput")
    P["alog_in"] = ap("alogP", [128, 256], F32, "ExternalInput")
    P["dv_in"] = ap("DP", [128, 16], F32, "ExternalInput")
    P["glub_in"] = ap("glubP", [128, 16], F32, "ExternalInput")
    P["lng_in"] = ap("lngP", [128, 8], F32, "ExternalInput")
    P["lnb_in"] = ap("lnbP", [128, 8], F32, "ExternalInput")
    P["sel_in"] = ap("sel16", [16, 16 * 128], F32, "ExternalInput")
    P["pm64_in"] = ap("pm64", [1, 64], F32, "ExternalInput")
    P["xTf_in"] = ap("xTf", [NCORE, D, T], F32, "ExternalInput")
    P["w1_in"] = ap("w1", [E0, D, 4096], F32, "ExternalInput")
    P["b1_in"] = ap("b1P", [E0, 128, 32], F32, "ExternalInput")
    P["w2_in"] = ap("w2", [E0, 4096, D], F32, "ExternalInput")
    P["b2_in"] = ap("b2", [E0, D], F32, "ExternalInput")
    P["sel8_in"] = ap("sel8", [E0, E0 * 128], F32, "ExternalInput")
    P["qkw_in"] = ap("qk_w", [D, 2 * D], F32, "ExternalInput")
    P["qkb_in"] = ap("qk_b", [2 * D], F32, "ExternalInput")
    P["lepe_in"] = ap("lepe_w", [3, D, D], F32, "ExternalInput")
    P["lepeb_in"] = ap("lepebP", [128, 8], F32, "ExternalInput")
    P["projP_in"] = ap("projP", [128, 48], F32, "ExternalInput")
    P["anchT_in"] = ap("anchT", [6, 6], F32, "ExternalInput")
    P["temp_in"] = ap("rtemp", [1, 1], F32, "ExternalInput")
    P["w1f_in"] = ap("w1f", [E1, D, DE1], F32, "ExternalInput")
    P["w2f_in"] = ap("w2f", [E1, DE1, D], F32, "ExternalInput")
    xnb_t = nc.dram_tensor("i_xnT_b", [D, T], BF16, kind="Internal")
    P["xnTb_hbm"] = xnb_t.ap().rearrange("(g p) t -> p g t", p=128)
    sT_t = nc.dram_tensor("i_sT", [D, T], BF16, kind="Internal")
    P["sT_hbm"] = sT_t.ap().rearrange("(g p) t -> p g t", p=128)
    gT_t = nc.dram_tensor("i_gT", [D, T], BF16, kind="Internal")
    P["gT_ap"] = gT_t.ap()
    P["gT_hbm"] = gT_t.ap().rearrange("(g p) t -> p g t", p=128)
    x1_t = nc.dram_tensor("i_x1T", [D, T], F32, kind="Internal")
    P["x1_hbm"] = x1_t.ap().rearrange("(g p) t -> p g t", p=128)
    x1n_t = nc.dram_tensor("i_x1nT_b", [D, T], BF16, kind="Internal")
    P["x1n_hbm"] = x1n_t.ap().rearrange("(g p) t -> p g t", p=128)
    s1_t = nc.dram_tensor("i_s1T", [D, T], BF16, kind="Internal")
    P["s1_hbm"] = s1_t.ap().rearrange("(g p) t -> p g t", p=128)
    P["out_hbm"] = ap("out", [T, D], F32, "ExternalOutput")

    b = B(nc)
    with contextlib.ExitStack() as top:
        route_sb = top.enter_context(nc.sbuf_tensor("route_sb", [128, 16 * 20], F32))
        P["route_sb"] = route_sb
        P["offs_rep"] = top.enter_context(nc.sbuf_tensor("offs_rep", [128, E0], F32))
        cve = top.enter_context(nc.sbuf_tensor("cval_eps", [1, 1], F32))
        P["cval_eps"] = cve[0:1, 0:1]
        b.op("vector", lambda E: E.memset(cve[:], 1e-5), writes=("cval_eps",))
        emit_A(b, nc, P)
        b.barrier()
        with contextlib.ExitStack() as cstk:
            do_bo, cnt_finale = emit_counts_mk(b, nc, P, cstk)
            for bo in range(NCORE):
                do_bo(bo)
            cnt_finale()
        b.barrier()
        with contextlib.ExitStack() as bstk:
            do_chunk = emit_B_mk(b, nc, P, bstk)
            for ci in range(4):
                do_chunk(ci)
        b.barrier()
        emit_C(b, nc, P)
        b.barrier()
        emit_D(b, nc, P)
        b.barrier()
        emit_E(b, nc, P)
        b.barrier()
        emit_F(b, nc, P)
        b.finish_into(top)
    return nc


def kernel(**inputs):
    cst = host_prep(inputs)
    x_full = np.ascontiguousarray(inputs["L0_attn_x"])
    ncX = build_fused()
    in_maps = []
    for core in range(NCORE):
        pm64 = np.zeros((1, 64), np.float32)
        pm64[0, :core * 8] = 1.0
        m = {"x": np.ascontiguousarray(x_full[core]),
             "w_router": cst["w_router_p"],
             "b_router": np.ascontiguousarray(inputs["L0_ffn_b_router"].astype(np.float32)),
             "eye": cst["eye"], "triu": cst["triu"], "ones": cst["ones"],
             "pm64": pm64, "xTf": cst["xTf"]}
        for k in ["wipT", "wxpP", "wdtT", "wopT", "wgluT", "convP", "convbP",
                  "dtbP", "alogP", "DP", "glubP", "lngP", "lnbP", "sel16",
                  "w1c", "b1P", "w2c", "b2c", "sel8", "qk_w", "qk_b", "lepe_w",
                  "lepebP", "projP", "anchT", "rtemp", "w1f", "w2f"]:
            name = {"w1c": "w1", "w2c": "w2", "b2c": "b2"}.get(k, k)
            m[name] = cst[k]
        in_maps.append(m)
    res = run_bass_kernel_spmd(ncX, in_maps, core_ids=list(range(NCORE)))
    out = np.stack([np.asarray(res.results[c]["out"]) for c in range(NCORE)], 0)
    return out.astype(np.float32)
